# version 05
# baseline (speedup 1.0000x reference)
import concourse.bass as bass
import concourse.mybir as mybir

ENGS = ["T", "V", "S", "P", "IO"]
SECTION = {"T": "tensor", "V": "vector", "S": "scalar", "P": "gpsimd", "IO": "sync"}
DMA_POOL = 48


class Ctx:
    def __init__(self, nc: bass.Bass, dma_pool=DMA_POOL):
        self.nc = nc
        self.sems = {e: nc.alloc_semaphore(f"sem_{e}") for e in ["T", "V", "S", "P"]}
        self.count = {e: 0 for e in ["T", "V", "S", "P"]}
        self.dma_pool_n = dma_pool
        self.dma_sw_n = 24
        tot = dma_pool + self.dma_sw_n
        self.dma_sems = [nc.alloc_semaphore(f"dma{i}") for i in range(tot)]
        self.dma_target = [0] * tot
        self.dma_waited = [0] * tot
        self.dma_rr = 0
        self.dma_rr_sw = dma_pool
        self.clock = {e: {} for e in ENGS}
        self.streams = {e: [] for e in ENGS}
        self.bufs = {}
        self.nops = 0

    def _buf(self, key):
        if key not in self.bufs:
            self.bufs[key] = {"w": None, "r": [], "ww": []}
        return self.bufs[key]

    def _need_edges(self, eng, reads, writes, is_dma=False):
        need = {}
        def add(src, val):
            if src == "T" == eng:
                return
            if val <= self.clock[eng].get(src, 0):
                return
            need[src] = max(need.get(src, 0), val)
        for key in reads:
            b = self._buf(key)
            if b["w"] is not None:
                add(*b["w"])
            for (src, val) in b["ww"]:
                add(src, val)
        for key in writes:
            b = self._buf(key)
            if b["w"] is not None:
                wsrc = b["w"][0]
                if not (is_dma and isinstance(wsrc, tuple)):
                    add(*b["w"])
                    for (src, val) in b["ww"]:
                        add(src, val)
            for (src, val) in b["r"]:
                add(src, val)
        return need

    def _emit_waits(self, eng, need):
        for src, val in sorted(need.items(), key=lambda kv: str(kv[0])):
            self.streams[eng].append(("wait", (src, val)))
            self.clock[eng][src] = val
            if isinstance(src, tuple):
                idx = src[1]
                self.dma_waited[idx] = max(self.dma_waited[idx], val)

    def op(self, eng, fn, reads=(), writes=(), tag=""):
        assert eng in ("T", "V", "S", "P")
        need = self._need_edges(eng, reads, writes)
        self._emit_waits(eng, need)
        self.count[eng] += 1
        myval = self.count[eng]
        self.streams[eng].append(("op", (fn, eng, myval, tag)))
        self.nops += 1
        for key in writes:
            b = self._buf(key)
            b["w"] = (eng, myval)
            b["r"] = []
            b["ww"] = []
        for key in reads:
            self._buf(key)["r"].append((eng, myval))
        return myval

    def dma(self, dst_ap, src_ap, reads=(), writes=(), eng=None, tag="", allow_slow=False):
        if eng is None:
            try:
                is_load = (str(src_ap.space).endswith("DRAM")
                           and str(dst_ap.space).endswith("SBUF"))
            except Exception:
                is_load = True
            eng = "IO" if is_load else "P"
        need = self._need_edges(eng, reads, writes, is_dma=True)
        self._emit_waits(eng, need)
        if eng == "P":
            idx = self.dma_rr_sw
            self.dma_rr_sw = self.dma_pool_n + ((idx - self.dma_pool_n + 1) % self.dma_sw_n)
        else:
            idx = self.dma_rr
            self.dma_rr = (self.dma_rr + 1) % self.dma_pool_n
        if self.dma_waited[idx] < self.dma_target[idx]:
            self._emit_waits(eng, {("D", idx): self.dma_target[idx]})
        self.dma_target[idx] += 16
        target = self.dma_target[idx]
        sem = self.dma_sems[idx]

        def fn(e, dst_ap=dst_ap, src_ap=src_ap, sem=sem):
            kw = dict(allow_slow_non_contiguous=True) if allow_slow else {}
            return e.dma_start(out=dst_ap, in_=src_ap, **kw)

        self.streams[eng].append(("dma", (fn, sem, tag)))
        self.nops += 1
        src = ("D", idx)
        for key in writes:
            b = self._buf(key)
            prev = b["w"]
            b["w"] = (src, target)
            if prev is not None and isinstance(prev[0], tuple):
                b["ww"] = list(b["ww"]) + [prev]
            else:
                b["ww"] = []
            b["r"] = []
        for key in reads:
            self._buf(key)["r"].append((src, target))
        return (src, target)

    def barrier_end(self):
        need = {}
        for e in ["T", "V", "S", "P"]:
            if self.count[e] > self.clock["IO"].get(e, 0):
                need[e] = self.count[e]
        for i in range(len(self.dma_sems)):
            if self.dma_target[i] > self.clock["IO"].get(("D", i), 0):
                need[("D", i)] = self.dma_target[i]
        self._emit_waits("IO", need)

    def finish(self):
        with self.nc.Block() as block:
            for eng in ENGS:
                stream = self.streams[eng]
                if not stream:
                    continue
                section = getattr(block, SECTION[eng])

                def make_body(stream=stream, eng=eng):
                    def body(engine_obj):
                        for kind, payload in stream:
                            if kind == "wait":
                                src, val = payload
                                sem = (self.dma_sems[src[1]] if isinstance(src, tuple)
                                       else self.sems[src])
                                engine_obj.wait_ge(sem, val)
                            elif kind == "dma":
                                fn, sem, tag = payload
                                fn(engine_obj).then_inc(sem, 16)
                            else:
                                fn, e, myval, tag = payload
                                fn(engine_obj).then_inc(self.sems[e], 1)
                    return body

                section(make_body())


"""Megakernel stage library for nn_mk_prb_d2 (per-core program builder).

Layout conventions:
- activations channel-major in HBM scratch: [BL, C, T] (BL=2 local batches)
- host passes x pre-transposed to [BL, 1024, T]; output written [BL, 1024, T]
- per-channel vectors host-reshaped to [128, n_tiles] (column j = tile j)
- weights passed in natural [K, M] (= lhsT-ready) layout except fp4 ffn
  weights which are quantized+transposed on device.
"""
import numpy as np
import concourse.bass as bass
import concourse.mybir as mybir

F32 = mybir.dt.float32
BF16 = mybir.dt.bfloat16
AF = mybir.ActivationFunctionType
ALU = mybir.AluOpType
AX = mybir.AxisListType

BL = 2
T = 4096
D = 1024
NT = 8
FFI = 2816
NFF = 22
CH = 512
NCH = T // CH


def setcfg(t=4096, bl=2):
    global T, BL, NCH
    T, BL, NCH = t, bl, t // CH
MAGIC = float(np.float32(1.5 * 2 ** 23))
EPS_LN = 1e-5


def ceil_div(a, b):
    return (a + b - 1) // b


class Arena:
    def __init__(self, ctx, tile, name, base, cols):
        self.ctx, self.t, self.name = ctx, tile, name
        self.base, self.cols = base, cols
        self.cur = base
        self.entries = []
        self.kid = 0

    def alloc(self, elems, dtype=F32, tag=""):
        dsz = 2 if dtype == BF16 else 4
        fcols = ceil_div(elems * dsz, 4)
        start = (self.cur + 3) & ~3
        end = start + fcols
        assert end <= self.base + self.cols, \
            f"arena {self.name} overflow: {end - self.base} > {self.cols} ({tag})"
        key = f"{self.name}{self.kid}_{tag}"
        self.kid += 1
        olap = [e for e in self.entries if e[0] < end and e[1] > start]
        if olap:
            self.ctx.merge_keys(key, [k for (_, _, k) in olap])
        keep = []
        for (s, e, k) in self.entries:
            if s < end and e > start:
                if s < start:
                    keep.append((s, start, k))
                if e > end:
                    keep.append((end, e, k))
            else:
                keep.append((s, e, k))
        keep.append((start, end, key))
        self.entries = keep
        self.cur = end
        ap = self.t[:, start:end]
        if dtype == BF16:
            ap = ap.bitcast(BF16)
        ap = ap[:, :elems]
        return ap, key

    def reset(self):
        self.cur = self.base


class MK:
    def __init__(self, nc, externalize=(), feed=()):
        self.nc = nc
        self.feed = set(feed)
        self.ctx = Ctx(nc)
        def merge_keys(new, olds):
            b = self.ctx._buf(new)
            rs = list(b["r"])
            for k in olds:
                ob = self.ctx._buf(k)
                if ob["w"] is not None:
                    rs.append(ob["w"])
                rs.extend(ob["r"])
                rs.extend(ob["ww"])
            b["w"] = None
            b["r"] = rs
        self.ctx.merge_keys = merge_keys
        self.externalize = set(externalize)
        self.d = {}
        POOL = 50000
        self.pool = nc.alloc_sbuf_tensor("pool", [128, POOL], F32)
        self.P = Arena(self.ctx, self.pool, "P", 0, 2000)
        self.A = Arena(self.ctx, self.pool, "A", 2000, POOL - 2000)
        self.ps = [nc.alloc_psum_tensor(f"ps{i}", [128, 512], F32) for i in range(8)]
        self.ps_keys = [f"ps{i}" for i in range(8)]
        self.ps_rr = 0
        self.strip = {}
        self.cvals = {}
        self.cpool = None

    def dram_in(self, name, shape, dtype=F32):
        ap = self.nc.dram_tensor(name, list(shape), dtype, kind="ExternalInput").ap()
        self.d[name] = ap
        return ap

    def dram_scratch(self, name, shape, dtype=F32):
        kind = ("ExternalInput" if name in self.feed else
                "ExternalOutput" if name in self.externalize else "Internal")
        ap = self.nc.dram_tensor(name, list(shape), dtype, kind=kind).ap()
        self.d[name] = ap
        return ap

    def dram_out(self, name, shape, dtype=F32):
        ap = self.nc.dram_tensor(name, list(shape), dtype, kind="ExternalOutput").ap()
        self.d[name] = ap
        return ap

    def psum(self):
        i = self.ps_rr
        self.ps_rr = (self.ps_rr + 1) % 8
        return self.ps[i], self.ps_keys[i]

    def setup_consts(self, values):
        vals = sorted(set(float(v) for v in values))
        self.cpool, ck = self.P.alloc(len(vals), F32, "cpool")
        for j, v in enumerate(vals):
            self.cvals[v] = j
            self.nc.const_aps.register(F32, v, self.cpool[:, j:j + 1])
            v32 = float(np.float32(v))
            if v32 != v:
                self.nc.const_aps.register(F32, v32, self.cpool[:, j:j + 1])
            self.ctx.op("V", lambda e, j=j, v=v: e.memset(self.cpool[:, j:j + 1], v),
                        writes=["cpool"])
        self.ck = "cpool"

    def o(self, eng, fn, reads=(), writes=(), tag=""):
        return self.ctx.op(eng, fn, reads=reads, writes=writes, tag=tag)

    def load_strip(self, name, dname, cols, cast_bf16=False):
        ap, key = self.P.alloc(cols, F32, name)
        rows = self.d[dname].shape[0]
        self.ctx.dma(ap[:rows, :], self.d[dname], writes=[key])
        self.strip[name] = (ap, key)
        return ap, key

    def mm_group(self, ps_ap, ps_key, pairs, reads, tag=""):
        def fn(e):
            last = None
            n = len(pairs)
            for i, (lt, rh) in enumerate(pairs):
                last = e.matmul(ps_ap, lt, rh, start=(i == 0), stop=(i == n - 1))
            return last
        self.o("T", fn, reads=reads, writes=[ps_key], tag=tag)

    def load_w_lhsT_bf16(self, dram_name, K, M, tag, stg_n=2):
        ntk = ceil_div(K, 128)
        stg = [self.A.alloc(M, F32, f"{tag}wstg{i}") for i in range(stg_n)]
        out = []
        for k in range(ntk):
            kk = min(128, K - k * 128)
            f, fk = stg[k % stg_n]
            b, bk = self.A.alloc(M, BF16, f"{tag}wb{k}")
            self.ctx.dma(f[:kk, :], self.d[dram_name][k * 128:k * 128 + kk, :], writes=[fk])
            self.o("V", lambda e, b=b, f=f, kk=kk: e.tensor_copy(b[:kk, :], f[:kk, :]),
                   reads=[fk], writes=[bk])
            out.append((b, bk, kk))
        return out

    def st_prep(self):
        P = self.P
        self.load_strip("b_in", "L0_attn_b_in", NT)
        self.load_strip("B_vec", "L0_attn_B_vec", NT)
        self.load_strip("log_A", "L0_attn_log_A", NT)
        self.load_strip("Dv", "L0_attn_D", NT)
        a_ap, a_key = P.alloc(NT, F32, "Astrip")
        la, lak = self.strip["log_A"]
        self.o("S", lambda e: e.activation(out=a_ap[:, :], in_=la[:, :], func=AF.Exp),
               reads=[lak], writes=[a_key])
        self.o("V", lambda e: e.tensor_scalar_mul(a_ap[:, :], a_ap[:, :], -1.0),
               reads=[a_key], writes=[a_key])
        self.strip["A"] = (a_ap, a_key)

    def st_ln(self, src, dstf, dstb, tag):
        A = self.A
        A.reset()
        onesw, onesk = A.alloc(2, F32, "ones1")
        self.o("V", lambda e: e.memset(onesw[:, 0:1], 1.0), writes=[onesk])
        xch = [A.alloc(CH, F32, f"xch{i}") for i in range(2)]
        sqch = [A.alloc(CH, F32, f"sqch{i}") for i in range(2)]
        R1, R1k = A.alloc(T, F32, "lnR1")
        R2, R2k = A.alloc(T, F32, "lnR2")
        R3, R3k = A.alloc(T, F32, "lnR3")
        sums, c1 = R1[0:1, :], R3[0:1, :]
        sumsq, c2, msq = R2[0:1, :], R3[0:1, 0:T], R2[0:1, 0:T]
        R4, R4k = A.alloc(T, F32, "lnR4")
        R5, R5k = A.alloc(T, F32, "lnR5")
        c1, c2, msq = R3[0:1, :], R4[0:1, :], R5[0:1, :]
        R1k_c1, R2k_c2, R2k_m = R3k, R4k, R5k
        c1b, c1bk = A.alloc(T, F32, "c1b")
        c2b, c2bk = A.alloc(T, F32, "c2b")
        xfull = [A.alloc(T, F32, f"xf{i}") for i in range(2)]
        xn, xnk = A.alloc(T, F32, "xn")
        xnb, xnbk = A.alloc(T, BF16, "xnb")
        scr_b = self.dram_scratch(f"lnscr_{tag}", [2, T], F32)
        for b in range(BL):
            for c in range(NCH):
                psA, pkA = self.psum()
                psB, pkB = self.psum()
                sl = slice(c * CH, (c + 1) * CH)
                for j in range(NT):
                    xa, xk = xch[j % 2]
                    sa, sk = sqch[j % 2]
                    self.ctx.dma(xa[:, :], self.d[src][b, j * 128:(j + 1) * 128, sl],
                                 writes=[xk], reads=[f"D:{src}"])
                    self.o("S", lambda e, sa=sa, xa=xa: e.activation(
                        out=sa[:, :], in_=xa[:, :], func=AF.Square),
                        reads=[xk], writes=[sk])
                    self.o("T", lambda e, psA=psA, xa=xa, j=j: e.matmul(
                        psA[0:1, :], onesw[:, 0:1], xa[:, :],
                        start=(j == 0), stop=(j == NT - 1)),
                        reads=[onesk, xk], writes=[pkA])
                    self.o("T", lambda e, psB=psB, sa=sa, j=j: e.matmul(
                        psB[0:1, :], onesw[:, 0:1], sa[:, :],
                        start=(j == 0), stop=(j == NT - 1)),
                        reads=[onesk, sk], writes=[pkB])
                self.o("V", lambda e, psA=psA, sl=sl: e.tensor_copy(sums[:, sl], psA[0:1, :]),
                       reads=[pkA], writes=[R1k])
                self.o("V", lambda e, psB=psB, sl=sl: e.tensor_copy(sumsq[:, sl], psB[0:1, :]),
                       reads=[pkB], writes=[R2k])
            self.o("V", lambda e: e.tensor_scalar_mul(sums, sums, 1.0 / D),
                   reads=[R1k], writes=[R1k])
            self.o("S", lambda e: e.activation(out=msq, in_=sums, func=AF.Square),
                   reads=[R1k], writes=[R2k_m])
            self.o("V", lambda e: e.tensor_scalar_mul(sumsq, sumsq, 1.0 / D),
                   reads=[R2k], writes=[R2k])
            self.o("V", lambda e: e.tensor_sub(sumsq, sumsq, msq),
                   reads=[R2k, R2k_m], writes=[R2k])
            self.o("S", lambda e: e.activation(out=sumsq, in_=sumsq, func=AF.Ln, bias=EPS_LN),
                   reads=[R2k, self.ck], writes=[R2k])
            self.o("S", lambda e: e.activation(out=c1, in_=sumsq, func=AF.Exp, scale=-0.5),
                   reads=[R2k], writes=[R1k_c1])
            self.o("V", lambda e: e.scalar_tensor_tensor(out=c2, in0=sums, scalar=-1.0,
                                                         in1=c1, op0=ALU.mult, op1=ALU.mult),
                   reads=[R1k, R1k_c1], writes=[R2k_c2])
            self.ctx.dma(scr_b[0:1, :], c1, reads=[R1k_c1], writes=["lnscr" + tag])
            self.ctx.dma(scr_b[1:2, :], c2, reads=[R2k_c2], writes=["lnscr" + tag])
            self.ctx.dma(c1b[:, :], scr_b[0:1, :].broadcast_to([128, T]),
                         reads=["lnscr" + tag], writes=[c1bk], eng="P")
            self.ctx.dma(c2b[:, :], scr_b[1:2, :].broadcast_to([128, T]),
                         reads=["lnscr" + tag], writes=[c2bk], eng="P")
            for j in range(NT):
                xa, xk = xfull[j % 2]
                self.ctx.dma(xa[:, :], self.d[src][b, j * 128:(j + 1) * 128, :],
                             writes=[xk], reads=[f"D:{src}"])
                self.o("V", lambda e, xa=xa: e.tensor_mul(xn[:, :], xa[:, :], c1b[:, :]),
                       reads=[xk, c1bk], writes=[xnk])
                self.o("V", lambda e: e.tensor_add(xn[:, :], xn[:, :], c2b[:, :]),
                       reads=[xnk, c2bk], writes=[xnk])
                if dstb is not None:
                    self.o("V", lambda e: e.tensor_copy(xnb[:, :], xn[:, :]),
                           reads=[xnk], writes=[xnbk])
                    self.ctx.dma(self.d[dstb][b, j * 128:(j + 1) * 128, :], xnb[:, :],
                                 reads=[xnbk], writes=[f"D:{dstb}"])
                if dstf is not None:
                    self.ctx.dma(self.d[dstf][b, j * 128:(j + 1) * 128, :], xn[:, :],
                                 reads=[xnk], writes=[f"D:{dstf}"])

    def st_l0scan(self):
        A = self.A
        A.reset()
        w = self.load_w_lhsT_bf16("L0_attn_W_in", D, D, "win")
        xnb = []
        for j in range(NT):
            t_ap, t_key = A.alloc(T, BF16, f"xnb{j}")
            xnb.append((t_ap, t_key))
        ub, ubk = A.alloc(T, F32, "ub")
        hb, hbk = A.alloc(T, BF16, "hb")
        b_in, b_in_k = self.strip["b_in"]
        Bv, Bv_k = self.strip["B_vec"]
        Ast, Ast_k = self.strip["A"]
        for b in range(BL):
            for j in range(NT):
                t_ap, t_key = xnb[j]
                self.ctx.dma(t_ap[:, :], self.d["xn0b"][b, j * 128:(j + 1) * 128, :],
                             writes=[t_key], reads=["D:xn0b"])
            for j in range(NT):
                for c in range(NCH):
                    ps, pk = self.psum()
                    sl = slice(c * CH, (c + 1) * CH)
                    self.mm_group(ps[:, :], pk,
                                  [(w[k][0][:, j * 128:(j + 1) * 128], xnb[k][0][:, sl])
                                   for k in range(NT)],
                                  reads=[w[k][1] for k in range(NT)] + [xnb[k][1] for k in range(NT)],
                                  tag=f"u{b}{j}{c}")
                    self.o("V", lambda e, ps=ps, sl=sl, j=j: e.tensor_scalar(
                        out=ub[:, sl], in0=ps[:, :], scalar1=b_in[:, j:j + 1],
                        scalar2=Bv[:, j:j + 1], op0=ALU.add, op1=ALU.mult),
                        reads=[pk, b_in_k, Bv_k], writes=[ubk])
                self.o("V", lambda e, j=j: e.tensor_tensor_scan(
                    out=hb[:, :], data0=Ast[:, j:j + 1].broadcast_to([128, T]),
                    data1=ub[:, :], initial=0.0, op0=ALU.mult, op1=ALU.add),
                    reads=[ubk, Ast_k], writes=[hbk])
                self.ctx.dma(self.d["h0b"][b, j * 128:(j + 1) * 128, :], hb[:, :],
                             reads=[hbk], writes=["D:h0b"])

    def st_l0y(self):
        A = self.A
        A.reset()
        w = self.load_w_lhsT_bf16("L0_attn_C", D, D, "C")
        hbt = []
        for j in range(NT):
            t_ap, t_key = A.alloc(T, BF16, f"hb{j}")
            hbt.append((t_ap, t_key))
        xnbt, xnbk = A.alloc(T, BF16, "xnbt")
        xf, xfk = A.alloc(T, F32, "xf")
        acc, acck = A.alloc(T, F32, "acc")
        Dv, Dv_k = self.strip["Dv"]
        for b in range(BL):
            for j in range(NT):
                t_ap, t_key = hbt[j]
                self.ctx.dma(t_ap[:, :], self.d["h0b"][b, j * 128:(j + 1) * 128, :],
                             writes=[t_key], reads=["D:h0b"])
            for j in range(NT):
                self.ctx.dma(xnbt[:, :], self.d["xn0b"][b, j * 128:(j + 1) * 128, :],
                             writes=[xnbk], reads=["D:xn0b"])
                self.ctx.dma(xf[:, :], self.d["x"][b, j * 128:(j + 1) * 128, :],
                             writes=[xfk], reads=["D:x"])
                for c in range(NCH):
                    ps, pk = self.psum()
                    sl = slice(c * CH, (c + 1) * CH)
                    self.mm_group(ps[:, :], pk,
                                  [(w[k][0][:, j * 128:(j + 1) * 128], hbt[k][0][:, sl])
                                   for k in range(NT)],
                                  reads=[w[k][1] for k in range(NT)] + [hbt[k][1] for k in range(NT)],
                                  tag=f"y{b}{j}{c}")
                    self.o("V", lambda e, ps=ps, sl=sl, j=j: e.scalar_tensor_tensor(
                        out=acc[:, sl], in0=xnbt[:, sl], scalar=Dv[:, j:j + 1],
                        in1=ps[:, :], op0=ALU.mult, op1=ALU.add),
                        reads=[pk, xnbk, Dv_k], writes=[acck])
                self.o("V", lambda e: e.tensor_add(acc[:, :], acc[:, :], xf[:, :]),
                       reads=[acck, xfk], writes=[acck])
                self.ctx.dma(self.d["a0"][b, j * 128:(j + 1) * 128, :], acc[:, :],
                             reads=[acck], writes=["D:a0"])

    def emit_qdq(self, xT, xT_key, out_bf, out_key, Cfree, scr, pool_ops=False):
        nb = Cfree // 16
        x3 = xT[:, :Cfree].rearrange("p (b k) -> p b k", k=16)
        (bm, bmk), (sc, sck), (inv, invk), (t, tk), (a, ak), (sg, sgk), \
            (r1, r1k), (r2, r2k) = scr
        o = self.o
        o("V", lambda e: e.tensor_reduce(out=bm[:, :nb], in_=x3, axis=AX.X,
                                         op=ALU.max, apply_absolute_value=True),
          reads=[xT_key], writes=[bmk])
        o("V", lambda e: e.tensor_scalar(out=sc[:, :nb], in0=bm[:, :nb], scalar1=1.0 / 6.0,
                                         scalar2=1e-12, op0=ALU.mult, op1=ALU.max),
          reads=[bmk], writes=[sck])
        o("V", lambda e: e.reciprocal(inv[:, :nb], sc[:, :nb]), reads=[sck], writes=[invk])
        invr = inv[:, :nb].unsqueeze(2).broadcast_to([128, nb, 16])
        scr_r = sc[:, :nb].unsqueeze(2).broadcast_to([128, nb, 16])
        t3 = t[:, :Cfree].rearrange("p (b k) -> p b k", k=16)
        E1 = "P" if pool_ops else "V"
        o(E1, lambda e: e.tensor_tensor(out=t3, in0=x3, in1=invr, op=ALU.mult),
          reads=[xT_key, invk], writes=[tk])
        o("S", lambda e: e.activation(out=a[:, :Cfree], in_=t[:, :Cfree], func=AF.Abs),
          reads=[tk], writes=[ak])
        o("S", lambda e: e.activation(out=sg[:, :Cfree], in_=t[:, :Cfree], func=AF.Sign),
          reads=[tk], writes=[sgk])
        o("V", lambda e: e.tensor_scalar(out=r1[:, :Cfree], in0=a[:, :Cfree], scalar1=0.5,
                                         scalar2=4.0, op0=ALU.mult, op1=ALU.add),
          reads=[ak], writes=[r1k])
        o("V", lambda e: e.scalar_tensor_tensor(out=r1[:, :Cfree], in0=a[:, :Cfree], scalar=2.0,
                                                in1=r1[:, :Cfree], op0=ALU.mult, op1=ALU.min),
          reads=[ak, r1k], writes=[r1k])
        o("V", lambda e: e.scalar_tensor_tensor(out=r1[:, :Cfree], in0=a[:, :Cfree], scalar=2.0,
                                                in1=r1[:, :Cfree], op0=ALU.add, op1=ALU.min),
          reads=[ak, r1k], writes=[r1k])
        o("V", lambda e: e.tensor_scalar(out=r1[:, :Cfree], in0=r1[:, :Cfree], scalar1=MAGIC,
                                         scalar2=MAGIC, op0=ALU.add, op1=ALU.subtract),
          reads=[r1k], writes=[r1k])
        o("V", lambda e: e.tensor_scalar(out=r2[:, :Cfree], in0=r1[:, :Cfree], scalar1=0.5,
                                         scalar2=None, op0=ALU.mult),
          reads=[r1k], writes=[r2k])
        o("V", lambda e: e.scalar_tensor_tensor(out=r2[:, :Cfree], in0=r1[:, :Cfree], scalar=-2.0,
                                                in1=r2[:, :Cfree], op0=ALU.add, op1=ALU.max),
          reads=[r1k, r2k], writes=[r2k])
        o("V", lambda e: e.tensor_scalar(out=r1[:, :Cfree], in0=r1[:, :Cfree], scalar1=2.0,
                                         scalar2=8.0, op0=ALU.mult, op1=ALU.subtract),
          reads=[r1k], writes=[r1k])
        o("V", lambda e: e.tensor_max(r2[:, :Cfree], r2[:, :Cfree], r1[:, :Cfree]),
          reads=[r1k, r2k], writes=[r2k])
        o(E1, lambda e: e.tensor_mul(r2[:, :Cfree], r2[:, :Cfree], sg[:, :Cfree]),
          reads=[r2k, sgk], writes=[r2k])
        r23 = r2[:, :Cfree].rearrange("p (b k) -> p b k", k=16)
        ob3 = out_bf[:, :Cfree].rearrange("p (b k) -> p b k", k=16)
        o(E1, lambda e: e.tensor_tensor(out=ob3, in0=r23, in1=scr_r, op=ALU.mult),
          reads=[r2k, sck], writes=[out_key])

    def qdq_scratch(self, Cfree):
        A = self.A
        nb = Cfree // 16
        return [A.alloc(nb, F32, "bm"), A.alloc(nb, F32, "sc"), A.alloc(nb, F32, "inv"),
                A.alloc(Cfree, F32, "t"), A.alloc(Cfree, F32, "a"), A.alloc(Cfree, F32, "sg"),
                A.alloc(Cfree, F32, "r1"), A.alloc(Cfree, F32, "r2")]

    def st_quant_act(self, src, dst, C):
        A = self.A
        A.reset()
        ntc = C // 128
        idb, idbk = self.ident_bf16()
        sets = []
        for i in range(2):
            xin = A.alloc(128 * ntc, F32, f"qin{i}")
            xT = A.alloc(C, F32, f"xT{i}")
            qT = A.alloc(C, BF16, f"qT{i}")
            qcm = A.alloc(128 * ntc, BF16, f"qcm{i}")
            scr = self.qdq_scratch(C)
            sets.append((xin, xT, qT, qcm, scr))
        it = 0
        for b in range(BL):
            for tc in range(T // 128):
                (xin, xink), (xT, xTk), (qT, qTk), (qcm, qcmk), scr = sets[it % 2]
                it += 1
                tsl = slice(tc * 128, (tc + 1) * 128)
                for cj in range(ntc):
                    self.ctx.dma(xin[:, cj * 128:(cj + 1) * 128],
                                 self.d[src][b, cj * 128:(cj + 1) * 128, tsl],
                                 writes=[xink], reads=[f"D:{src}"])
                for cj in range(ntc):
                    ps, pk = self.psum()
                    self.o("T", lambda e, ps=ps, cj=cj, xin=xin: e.transpose(
                        ps[:, 0:128], xin[:, cj * 128:(cj + 1) * 128], self.ident_f32()[0]),
                        reads=[xink, self.ident_f32()[1]], writes=[pk])
                    self.o("V", lambda e, ps=ps, cj=cj, xT=xT: e.tensor_copy(
                        xT[:, cj * 128:(cj + 1) * 128], ps[:, 0:128]),
                        reads=[pk], writes=[xTk])
                self.emit_qdq(xT, xTk, qT, qTk, C, scr, pool_ops=True)
                for cj in range(ntc):
                    ps, pk = self.psum()
                    psb = ps[:, :].bitcast(BF16)
                    self.o("T", lambda e, psb=psb, cj=cj, qT=qT: e.transpose(
                        psb[:, 0:128], qT[:, cj * 128:(cj + 1) * 128], idb[:, :]),
                        reads=[qTk, idbk], writes=[pk])
                    self.o("V", lambda e, psb=psb, cj=cj, qcm=qcm: e.tensor_copy(
                        qcm[:, cj * 128:(cj + 1) * 128], psb[:, 0:128]),
                        reads=[pk], writes=[qcmk])
                for cj in range(ntc):
                    self.ctx.dma(self.d[dst][b, cj * 128:(cj + 1) * 128, tsl],
                                 qcm[:, cj * 128:(cj + 1) * 128],
                                 reads=[qcmk], writes=[f"D:{dst}"])

    def ident_f32(self):
        if "identf" not in self.strip:
            ap, key = self.P.alloc(128, F32, "identf")
            self.ctx.dma(ap[:, :], self.d["c_ident"], writes=[key])
            self.strip["identf"] = (ap, key)
        return self.strip["identf"]

    def ident_bf16(self):
        if "identb" not in self.strip:
            f, fk = self.ident_f32()
            ap, key = self.P.alloc(128, BF16, "identb")
            self.o("V", lambda e: e.tensor_copy(ap[:, :], f[:, :]), reads=[fk], writes=[key])
            self.strip["identb"] = (ap, key)
        return self.strip["identb"]

    def st_quant_w(self, src, dst, R, C):
        A = self.A
        A.reset()
        idb, idbk = self.ident_bf16()
        ntr, ntc = R // 128, C // 128
        wrow, wrowk = A.alloc(C, F32, "wrow")
        qrow, qrowk = A.alloc(C, BF16, "qrow")
        tout, toutk = A.alloc(128, BF16, "tout")
        scr = self.qdq_scratch(C)
        for r in range(ntr):
            self.ctx.dma(wrow[:, :], self.d[src][r * 128:(r + 1) * 128, :],
                         writes=[wrowk], reads=[f"D:{src}"])
            self.emit_qdq(wrow, wrowk, qrow, qrowk, C, scr)
            for cj in range(ntc):
                ps, pk = self.psum()
                psb = ps[:, :].bitcast(BF16)
                self.o("T", lambda e, psb=psb, cj=cj: e.transpose(
                    psb[:, 0:128], qrow[:, cj * 128:(cj + 1) * 128], idb[:, :]),
                    reads=[qrowk, idbk], writes=[pk])
                self.o("V", lambda e, psb=psb: e.tensor_copy(tout[:, :], psb[:, 0:128]),
                       reads=[pk], writes=[toutk])
                self.ctx.dma(self.d[dst][cj * 128:(cj + 1) * 128, r * 128:(r + 1) * 128],
                             tout[:, :], reads=[toutk], writes=[f"D:{dst}"])

    def st_gateup(self):
        A = self.A
        A.reset()
        wg = []
        for k in range(NT):
            ap, key = A.alloc(FFI, BF16, f"wg{k}")
            self.ctx.dma(ap[:, :], self.d["wgq"][k * 128:(k + 1) * 128, :],
                         writes=[key], reads=["D:wgq"])
            wg.append((ap, key))
        wu = []
        for k in range(NT):
            ap, key = A.alloc(FFI, BF16, f"wu{k}")
            self.ctx.dma(ap[:, :], self.d["wuq"][k * 128:(k + 1) * 128, :],
                         writes=[key], reads=["D:wuq"])
            wu.append((ap, key))
        xq = []
        for j in range(NT):
            ap, key = A.alloc(T, BF16, f"xq{j}")
            xq.append((ap, key))
        sg, sgk = A.alloc(CH, F32, "sg")
        hm, hmk = A.alloc(T, F32, "hm")
        for b in range(BL):
            for j in range(NT):
                ap, key = xq[j]
                self.ctx.dma(ap[:, :], self.d["xq0"][b, j * 128:(j + 1) * 128, :],
                             writes=[key], reads=["D:xq0"])
            for m in range(NFF):
                msl = slice(m * 128, (m + 1) * 128)
                for c in range(NCH):
                    sl = slice(c * CH, (c + 1) * CH)
                    psG, pkG = self.psum()
                    psU, pkU = self.psum()
                    self.mm_group(psG[:, :], pkG,
                                  [(wg[k][0][:, msl], xq[k][0][:, sl]) for k in range(NT)],
                                  reads=[wg[k][1] for k in range(NT)] + [xq[k][1] for k in range(NT)])
                    self.mm_group(psU[:, :], pkU,
                                  [(wu[k][0][:, msl], xq[k][0][:, sl]) for k in range(NT)],
                                  reads=[wu[k][1] for k in range(NT)] + [xq[k][1] for k in range(NT)])
                    self.o("S", lambda e, psG=psG: e.activation(out=sg[:, :], in_=psG[:, :],
                                                                func=AF.Silu),
                           reads=[pkG], writes=[sgk])
                    self.o("V", lambda e, psU=psU, sl=sl: e.tensor_mul(hm[:, sl], psU[:, :], sg[:, :]),
                           reads=[pkU, sgk], writes=[hmk])
                self.ctx.dma(self.d["h0f"][b, m * 128:(m + 1) * 128, :], hm[:, :],
                             reads=[hmk], writes=["D:h0f"])

    def st_down(self):
        A = self.A
        A.reset()
        wd = []
        for k in range(NFF):
            ap, key = A.alloc(D, BF16, f"wd{k}")
            self.ctx.dma(ap[:, :], self.d["wdq"][k * 128:(k + 1) * 128, :],
                         writes=[key], reads=["D:wdq"])
            wd.append((ap, key))
        hq = []
        for k in range(NFF):
            ap, key = A.alloc(CH, BF16, f"hq{k}")
            hq.append((ap, key))
        a0t, a0k = A.alloc(CH, F32, "a0t")
        x1t, x1k = A.alloc(CH, F32, "x1t")
        for b in range(BL):
            for c in range(NCH):
                sl = slice(c * CH, (c + 1) * CH)
                for k in range(NFF):
                    ap, key = hq[k]
                    self.ctx.dma(ap[:, :], self.d["hq0"][b, k * 128:(k + 1) * 128, sl],
                                 writes=[key], reads=["D:hq0"])
                for m in range(NT):
                    msl = slice(m * 128, (m + 1) * 128)
                    ps, pk = self.psum()
                    self.mm_group(ps[:, :], pk,
                                  [(wd[k][0][:, msl], hq[k][0][:, :]) for k in range(NFF)],
                                  reads=[wd[k][1] for k in range(NFF)] + [hq[k][1] for k in range(NFF)])
                    self.ctx.dma(a0t[:, :], self.d["a0"][b, msl, sl],
                                 writes=[a0k], reads=["D:a0"])
                    self.o("V", lambda e, ps=ps: e.tensor_add(x1t[:, :], ps[:, :], a0t[:, :]),
                           reads=[pk, a0k], writes=[x1k])
                    self.ctx.dma(self.d["x1"][b, msl, sl], x1t[:, :],
                                 reads=[x1k], writes=["D:x1"])



B0_STAGES = ["prep", "wq", "ln0", "xq", "scan", "y", "gateup", "hq", "down"]


def declare_block0(mk):
    mk.dram_in("x", [BL, D, T])
    mk.dram_in("c_ident", [128, 128])
    mk.dram_in("L0_attn_W_in", [D, D])
    mk.dram_in("L0_attn_C", [D, D])
    for nm in ["L0_attn_b_in", "L0_attn_B_vec", "L0_attn_log_A", "L0_attn_D"]:
        mk.dram_in(nm, [128, NT])
    mk.dram_in("L0_ffn_w_gate", [FFI, D])
    mk.dram_in("L0_ffn_w_up", [FFI, D])
    mk.dram_in("L0_ffn_w_down", [D, FFI])
    mk.dram_scratch("xn0f", [BL, D, T])
    mk.dram_scratch("xn0b", [BL, D, T], BF16)
    mk.dram_scratch("h0b", [BL, D, T], BF16)
    mk.dram_scratch("a0", [BL, D, T])
    mk.dram_scratch("xq0", [BL, D, T], BF16)
    mk.dram_scratch("wgq", [D, FFI], BF16)
    mk.dram_scratch("wuq", [D, FFI], BF16)
    mk.dram_scratch("wdq", [FFI, D], BF16)
    mk.dram_scratch("h0f", [BL, FFI, T])
    mk.dram_scratch("hq0", [BL, FFI, T], BF16)
    mk.dram_scratch("x1", [BL, D, T])


def emit_block0(mk, upto=None, consts=True):
    stages = B0_STAGES if upto is None else B0_STAGES[:B0_STAGES.index(upto) + 1]
    if consts:
        mk.setup_consts([EPS_LN])
    for s in stages:
        if s == "prep":
            mk.st_prep()
        elif s == "wq":
            mk.st_quant_w("L0_ffn_w_gate", "wgq", FFI, D)
            mk.st_quant_w("L0_ffn_w_up", "wuq", FFI, D)
            mk.st_quant_w("L0_ffn_w_down", "wdq", D, FFI)
        elif s == "ln0":
            mk.st_ln("x", "xn0f", "xn0b", "ln0")
        elif s == "xq":
            mk.st_quant_act("xn0f", "xq0", D)
        elif s == "scan":
            mk.st_l0scan()
        elif s == "y":
            mk.st_l0y()
        elif s == "gateup":
            mk.st_gateup()
        elif s == "hq":
            mk.st_quant_act("h0f", "hq0", FFI)
        elif s == "down":
            mk.st_down()


def vec_strip(v, n_tiles):
    v = np.asarray(v, np.float32).ravel()
    out = np.zeros((128, n_tiles), np.float32)
    full = v.reshape(-1)
    for j in range(n_tiles):
        seg = full[j * 128:(j + 1) * 128]
        out[:len(seg), j] = seg
    return np.ascontiguousarray(out)


def host_prep_block0(inp, b_sl, Tcur):
    m = {
        "c_ident": np.eye(128, dtype=np.float32),
        "L0_attn_W_in": np.asarray(inp["L0_attn_W_in"]),
        "L0_attn_C": np.asarray(inp["L0_attn_C"]),
        "L0_attn_b_in": vec_strip(inp["L0_attn_b_in"], NT),
        "L0_attn_B_vec": vec_strip(inp["L0_attn_B_vec"], NT),
        "L0_attn_log_A": vec_strip(inp["L0_attn_log_A"], NT),
        "L0_attn_D": vec_strip(inp["L0_attn_D"], NT),
        "L0_ffn_w_gate": np.asarray(inp["L0_ffn_w_gate"]),
        "L0_ffn_w_up": np.asarray(inp["L0_ffn_w_up"]),
        "L0_ffn_w_down": np.asarray(inp["L0_ffn_w_down"]),
    }
    return {k: np.ascontiguousarray(np.asarray(v, np.float32)) for k, v in m.items()}


DI = 2048
NDI = 16
CPROJ = 4128
NPROJ = 33
NHEAD = 32
PI = float(np.pi)
INV2PI = float(np.float32(1.0 / (2 * np.pi)))
TWOPI = float(np.float32(2 * np.pi))


class MK1:
    pass


def _mixin(cls):
    for k, v in list(cls.__dict__.items()):
        if not k.startswith("__"):
            setattr(MK, k, v)
    return cls


@_mixin
class _Block1Attn:
    def st_l1proj(self):
        A = self.A
        A.reset()
        w = self.load_w_lhsT_bf16("L1_attn_in_proj_w", D, CPROJ, "iproj", stg_n=1)
        xnb = []
        for j in range(NT):
            ap, key = A.alloc(T, BF16, f"x1nb{j}")
            xnb.append((ap, key))
        zbuf, zbk = A.alloc(T + 3, F32, "zbuf")
        acc, acck = A.alloc(T, F32, "cacc")
        ob, obk = A.alloc(T, BF16, "convb")
        ck = [self.strip[f"conv_k{i}"] for i in range(4)]
        cb, cbk = self.strip["conv_b"]
        for b in range(BL):
            for j in range(NT):
                ap, key = xnb[j]
                self.ctx.dma(ap[:, :], self.d["xn1b"][b, j * 128:(j + 1) * 128, :],
                             writes=[key], reads=["D:xn1b"])
            for m in range(NPROJ):
                mm = min(128, CPROJ - m * 128)
                msl = slice(m * 128, m * 128 + mm)
                self.o("V", lambda e: e.memset(zbuf[:, 0:3], 0.0), writes=[zbk])
                for c in range(NCH):
                    ps, pk = self.psum()
                    self.mm_group(ps[:mm, :], pk,
                                  [(w[k][0][:, msl], xnb[k][0][:, c * CH:(c + 1) * CH])
                                   for k in range(NT)],
                                  reads=[w[k][1] for k in range(NT)] + [x[1] for x in xnb])
                    self.o("V", lambda e, ps=ps, c=c, mm=mm: e.tensor_copy(
                        zbuf[:mm, 3 + c * CH:3 + (c + 1) * CH], ps[:mm, :]),
                        reads=[pk], writes=[zbk])
                self.o("V", lambda e, m=m, mm=mm: e.tensor_scalar(
                    out=acc[:mm, :], in0=zbuf[:mm, 0:T], scalar1=ck[0][0][:mm, m:m + 1],
                    scalar2=cb[:mm, m:m + 1], op0=ALU.mult, op1=ALU.add),
                    reads=[zbk, ck[0][1], cbk], writes=[acck])
                for i in (1, 2, 3):
                    self.o("V", lambda e, i=i, m=m, mm=mm: e.scalar_tensor_tensor(
                        out=acc[:mm, :], in0=zbuf[:mm, i:i + T], scalar=ck[i][0][:mm, m:m + 1],
                        in1=acc[:mm, :], op0=ALU.mult, op1=ALU.add),
                        reads=[zbk, ck[i][1], acck], writes=[acck])
                if m < NDI:
                    self.o("V", lambda e: e.tensor_copy(ob[:, :], acc[:, :]),
                           reads=[acck], writes=[obk])
                    self.ctx.dma(self.d["xval"][b, m * 128:(m + 1) * 128, :], ob[:, :],
                                 reads=[obk], writes=["D:xval"])
                elif m < 2 * NDI:
                    self.o("S", lambda e: e.activation(out=ob[:, :], in_=acc[:, :],
                                                       func=AF.Silu),
                           reads=[acck], writes=[obk])
                    self.ctx.dma(self.d["xgate"][b, (m - NDI) * 128:(m - NDI + 1) * 128, :],
                                 ob[:, :], reads=[obk], writes=["D:xgate"])
                else:
                    self.ctx.dma(self.d["sraw"][b, :, :], acc[0:32, :],
                                 reads=[acck], writes=["D:sraw"])

    def _softplus_strip(self, dst, dst_k, src, src_k, n):
        t1, t1k = self.A.alloc(4, F32, "spt1")
        t2, t2k = self.A.alloc(4, F32, "spt2")
        self.o("S", lambda e: e.activation(out=t1[:n, 0:1], in_=src[:n, 0:1], func=AF.Abs),
               reads=[src_k], writes=[t1k])
        self.o("S", lambda e: e.activation(out=t1[:n, 0:1], in_=t1[:n, 0:1], func=AF.Exp,
                                           scale=-1.0), reads=[t1k], writes=[t1k])
        self.o("S", lambda e: e.activation(out=t1[:n, 0:1], in_=t1[:n, 0:1], func=AF.Ln,
                                           bias=1.0), reads=[t1k, self.ck], writes=[t1k])
        self.o("V", lambda e: e.tensor_scalar_max(t2[:n, 0:1], src[:n, 0:1], 0.0),
               reads=[src_k], writes=[t2k])
        self.o("V", lambda e: e.tensor_add(dst[:n, 0:1], t1[:n, 0:1], t2[:n, 0:1]),
               reads=[t1k, t2k], writes=[dst_k])

    def st_l1mixA(self):
        A = self.A
        A.reset()
        dec, deck = self.strip["decay_s"]
        anc, anck = self.strip["anchor_s"]
        ssc, ssck = self.strip["score_ss"]
        nsp, nspk = A.alloc(4, F32, "negsp")
        spd, spdk = A.alloc(4, F32, "spd")
        spa, spak = A.alloc(4, F32, "spa")
        self._softplus_strip(spd, spdk, dec, deck, 28)
        self._softplus_strip(spa, spak, anc, anck, 4)
        self.o("V", lambda e: e.tensor_scalar_mul(nsp[0:28, 0:1], spd[0:28, 0:1], -1.0),
               reads=[spdk], writes=[nspk])
        self.o("V", lambda e: e.tensor_scalar_mul(spa[0:4, 0:1], spa[0:4, 0:1], -1.0),
               reads=[spak], writes=[spak])
        nspscr = self.dram_scratch("nspscr", [4, 1], F32)
        self.ctx.dma(nspscr[0:4, :], spa[0:4, 0:1], reads=[spak], writes=["D:nspscr"])
        self.ctx.dma(nsp[28:32, 0:1], nspscr[0:4, :], reads=["D:nspscr"], writes=[nspk], eng="P")
        dp, dpk = A.alloc(T, F32, "distpos")
        self.ctx.dma(dp[0:28, :], self.d["c_dist"][0:1, :].broadcast_to([28, T]),
                     reads=["D:c_dist"], writes=[dpk])
        self.ctx.dma(dp[28:32, :], self.d["c_pos"][0:1, :].broadcast_to([4, T]),
                     reads=["D:c_pos"], writes=[dpk])
        targ, targk = A.alloc(T, F32, "targ")
        tw, twk = A.alloc(T, F32, "tw")
        sr, srk = A.alloc(T, F32, "srA")
        pw, pwk = A.alloc(T, F32, "pwA")
        z32, z32k = A.alloc(T, F32, "z32")
        den, denk = A.alloc(T, F32, "denA")
        self.o("V", lambda e: e.memset(z32[0:32, :], 0.0), writes=[z32k])
        self.o("V", lambda e: e.tensor_scalar_mul(targ[0:32, :], dp[0:32, :], nsp[0:32, 0:1]),
               reads=[dpk, nspk], writes=[targk])
        self.o("S", lambda e: e.activation(out=tw[0:32, :], in_=targ[0:32, :], func=AF.Exp),
               reads=[targk], writes=[twk])
        for b in range(BL):
            self.ctx.dma(sr[0:32, :], self.d["sraw"][b, :, :], writes=[srk], reads=["D:sraw"])
            self.o("V", lambda e: e.tensor_scalar(out=pw[0:32, :], in0=sr[0:32, :],
                                                  scalar1=ssc[0:32, 0:1], scalar2=20.0,
                                                  op0=ALU.mult, op1=ALU.min),
                   reads=[srk, ssck], writes=[pwk])
            self.o("V", lambda e: e.tensor_scalar_max(pw[0:32, :], pw[0:32, :], -20.0),
                   reads=[pwk], writes=[pwk])
            self.o("S", lambda e: e.activation(out=pw[0:32, :], in_=pw[0:32, :], func=AF.Exp),
                   reads=[pwk], writes=[pwk])
            self.o("V", lambda e: e.tensor_mul(pw[0:32, :], pw[0:32, :], tw[0:32, :]),
                   reads=[pwk, twk], writes=[pwk])
            self.ctx.dma(self.d["pwdram"][b, :, :], pw[0:32, :], reads=[pwk], writes=["D:pwdram"])
            self.o("V", lambda e: e.tensor_tensor_scan(out=den[0:32, :], data0=pw[0:32, :],
                                                       data1=z32[0:32, :], initial=0.0,
                                                       op0=ALU.add, op1=ALU.add),
                   reads=[pwk, z32k], writes=[denk])
            self.o("V", lambda e: e.tensor_scalar_max(den[0:32, :], den[0:32, :], 1e-4),
                   reads=[denk], writes=[denk])
            self.o("V", lambda e: e.reciprocal(den[0:32, :], den[0:32, :]),
                   reads=[denk], writes=[denk])
            self.ctx.dma(self.d["invdram"][b, :, :], den[0:32, :], reads=[denk],
                         writes=["D:invdram"])

    def st_l1mixB(self):
        A = self.A
        A.reset()
        th, thk = self.strip["theta_s"]
        dl, dlk = A.alloc(4, F32, "dlog")
        self.ctx.dma(dl[0:1, 0:3], self.d["deriv_l"], writes=[dlk], reads=["D:deriv_l"])
        dmx, dmxk = A.alloc(4, F32, "dmax")
        self.o("V", lambda e: e.tensor_reduce(out=dmx[0:1, 0:1], in_=dl[0:1, 0:3],
                                              axis=AX.X, op=ALU.max),
               reads=[dlk], writes=[dmxk])
        self.o("V", lambda e: e.scalar_tensor_tensor(out=dl[0:1, 0:3], in0=dmx[0:1, 0:1].broadcast_to([1, 3]),
                                                     scalar=-1.0, in1=dl[0:1, 0:3],
                                                     op0=ALU.mult, op1=ALU.add),
               reads=[dlk, dmxk], writes=[dlk])
        self.o("S", lambda e: e.activation(out=dl[0:1, 0:3], in_=dl[0:1, 0:3], func=AF.Exp),
               reads=[dlk], writes=[dlk])
        self.o("V", lambda e: e.tensor_reduce(out=dmx[0:1, 0:1], in_=dl[0:1, 0:3],
                                              axis=AX.X, op=ALU.add),
               reads=[dlk], writes=[dmxk])
        self.o("V", lambda e: e.reciprocal(dmx[0:1, 0:1], dmx[0:1, 0:1]),
               reads=[dmxk], writes=[dmxk])
        self.o("V", lambda e: e.tensor_scalar_mul(dl[0:1, 0:3], dl[0:1, 0:3], dmx[0:1, 0:1]),
               reads=[dlk, dmxk], writes=[dlk])
        self.o("V", lambda e: e.tensor_scalar_mul(dl[0:1, 3:4], dl[0:1, 2:3], -1.0),
               reads=[dlk], writes=[dlk])
        wscr = self.dram_scratch("wscr", [1, 4], F32)
        self.ctx.dma(wscr[0:1, :], dl[0:1, 0:4], reads=[dlk], writes=["D:wscr"])
        wb_, wbk_ = A.alloc(4, F32, "wbb")
        for j in range(3):
            self.ctx.dma(wb_[:, j:j + 1], wscr[0:1, j if j < 2 else 3].unsqueeze(1).broadcast_to([128, 1]),
                         reads=["D:wscr"], writes=[wbk_]) if False else None
        for j, col in enumerate([0, 1, 3]):
            self.ctx.dma(wb_[:, j:j + 1], wscr[0:1, col:col + 1].broadcast_to([128, 1]),
                         reads=["D:wscr"], writes=[wbk_], eng="P")
        wre, wrek = A.alloc(128, F32, "wreF")
        wim, wimk = A.alloc(128, F32, "wimF")
        wreb, wrebk = A.alloc(128, BF16, "wreB")
        wimb, wimbk = A.alloc(128, BF16, "wimB")
        nsre, nsrek = self.strip["ns_re"]
        nsim, nsimk = self.strip["ns_im"]
        for (wf, wfk, dn) in ((wre, wrek, "L1_attn_W_re"), (wim, wimk, "L1_attn_W_im")):
            self.o("V", lambda e, wf=wf: e.memset(wf[:, :], 0.0), writes=[wfk])
            self.ctx.dma(wf[0:64, 0:64], self.d[dn], reads=[f"D:{dn}"], writes=[wfk])
            self.ctx.dma(wf[64:128, 64:128], self.d[dn], reads=[f"D:{dn}"], writes=[wfk])
        self.o("V", lambda e: e.tensor_scalar_mul(wre[:, :], wre[:, :], nsre[:, 0:1]),
               reads=[wrek, nsrek], writes=[wrek])
        self.o("V", lambda e: e.tensor_scalar_mul(wim[:, :], wim[:, :], nsim[:, 0:1]),
               reads=[wimk, nsimk], writes=[wimk])
        self.o("V", lambda e: e.tensor_copy(wreb[:, :], wre[:, :]), reads=[wrek], writes=[wrebk])
        self.o("V", lambda e: e.tensor_copy(wimb[:, :], wim[:, :]), reads=[wimk], writes=[wimbk])
        xvt, xvtk = A.alloc(T, BF16, "xvt")
        B = [A.alloc(T, F32, f"B{i}") for i in range(6)]
        z128, z128k = A.alloc(T, F32, "z128")
        nre, nrek = A.alloc(T, BF16, "nreb")
        nim, nimk = A.alloc(T, BF16, "nimb")
        mst, mstk = A.alloc(T, F32, "mst")
        yrw, yrwk = A.alloc(T, BF16, "yrw")
        g64, g64k = self.strip["g64"]
        self.o("V", lambda e: e.memset(z128[:, :], 0.0), writes=[z128k])
        o = self.o
        for b in range(BL):
            for jt in range(NDI):
                k0 = 2 * jt
                (b1, b1k), (b2, b2k), (b3, b3k), (b4, b4k), (b5, b5k), (b6, b6k) = B
                self.ctx.dma(xvt[:, :], self.d["xval"][b, jt * 128:(jt + 1) * 128, :],
                             writes=[xvtk], reads=["D:xval"])
                o("V", lambda e, jt=jt: e.tensor_scalar_mul(b1[:, :], xvt[:, :], th[:, jt:jt + 1]),
                  reads=[xvtk, thk], writes=[b1k])
                o("V", lambda e: e.tensor_scalar(out=b2[:, :], in0=b1[:, :], scalar1=INV2PI,
                                                 scalar2=MAGIC, op0=ALU.mult, op1=ALU.add),
                  reads=[b1k], writes=[b2k])
                o("V", lambda e: e.tensor_scalar(out=b2[:, :], in0=b2[:, :], scalar1=MAGIC,
                                                 scalar2=None, op0=ALU.subtract),
                  reads=[b2k], writes=[b2k])
                o("V", lambda e: e.scalar_tensor_tensor(out=b3[:, :], in0=b2[:, :], scalar=-TWOPI,
                                                        in1=b1[:, :], op0=ALU.mult, op1=ALU.add),
                  reads=[b2k, b1k], writes=[b3k])
                o("V", lambda e: e.tensor_scalar(out=b2[:, :], in0=b1[:, :], scalar1=INV2PI,
                                                 scalar2=0.25, op0=ALU.mult, op1=ALU.add),
                  reads=[b1k], writes=[b2k])
                o("V", lambda e: e.tensor_scalar(out=b2[:, :], in0=b2[:, :], scalar1=MAGIC,
                                                 scalar2=MAGIC, op0=ALU.add, op1=ALU.subtract),
                  reads=[b2k], writes=[b2k])
                o("V", lambda e: e.scalar_tensor_tensor(out=b4[:, :], in0=b2[:, :], scalar=-TWOPI,
                                                        in1=b1[:, :], op0=ALU.mult, op1=ALU.add),
                  reads=[b2k, b1k], writes=[b4k])
                o("S", lambda e: e.activation(out=b2[:, :], in_=b4[:, :], func=AF.Sin,
                                              bias=PI / 2),
                  reads=[b4k, self.ck], writes=[b2k])
                o("S", lambda e: e.activation(out=b5[:, :], in_=b3[:, :], func=AF.Sin),
                  reads=[b3k], writes=[b5k])
                o("V", lambda e: e.tensor_scalar(out=b4[:, :], in0=xvt[:, :],
                                                 scalar1=wb_[:, 1:2], scalar2=wb_[:, 0:1],
                                                 op0=ALU.mult, op1=ALU.add),
                  reads=[xvtk, wbk_], writes=[b4k])
                o("S", lambda e: e.activation(out=b6[:, :], in_=xvt[:, :], func=AF.Square),
                  reads=[xvtk], writes=[b6k])
                o("V", lambda e: e.scalar_tensor_tensor(out=b4[:, :], in0=b6[:, :],
                                                        scalar=wb_[:, 2:3], in1=b4[:, :],
                                                        op0=ALU.mult, op1=ALU.add),
                  reads=[b6k, wbk_, b4k], writes=[b4k])
                o("V", lambda e: e.tensor_mul(b6[:, :], b4[:, :], b2[:, :]),
                  reads=[b4k, b2k], writes=[b6k])
                o("V", lambda e: e.tensor_mul(b2[:, :], b4[:, :], b5[:, :]),
                  reads=[b4k, b5k], writes=[b2k])
                self.ctx.dma(b4[0:64, :], self.d["pwdram"][b, k0:k0 + 1, :].broadcast_to([64, T]),
                             reads=["D:pwdram"], writes=[b4k])
                self.ctx.dma(b4[64:128, :], self.d["pwdram"][b, k0 + 1:k0 + 2, :].broadcast_to([64, T]),
                             reads=["D:pwdram"], writes=[b4k])
                o("V", lambda e: e.tensor_mul(b3[:, :], b6[:, :], b4[:, :]),
                  reads=[b6k, b4k], writes=[b3k])
                o("V", lambda e: e.tensor_mul(b5[:, :], b2[:, :], b4[:, :]),
                  reads=[b2k, b4k], writes=[b5k])
                o("V", lambda e: e.tensor_tensor_scan(out=b6[:, :], data0=b3[:, :],
                                                      data1=z128[:, :], initial=0.0,
                                                      op0=ALU.add, op1=ALU.add),
                  reads=[b3k, z128k], writes=[b6k])
                o("V", lambda e: e.tensor_tensor_scan(out=b2[:, :], data0=b5[:, :],
                                                      data1=z128[:, :], initial=0.0,
                                                      op0=ALU.add, op1=ALU.add),
                  reads=[b5k, z128k], writes=[b2k])
                self.ctx.dma(b4[0:64, :], self.d["invdram"][b, k0:k0 + 1, :].broadcast_to([64, T]),
                             reads=["D:invdram"], writes=[b4k])
                self.ctx.dma(b4[64:128, :], self.d["invdram"][b, k0 + 1:k0 + 2, :].broadcast_to([64, T]),
                             reads=["D:invdram"], writes=[b4k])
                o("V", lambda e: e.tensor_mul(b6[:, :], b6[:, :], b4[:, :]),
                  reads=[b6k, b4k], writes=[b6k])
                o("V", lambda e: e.tensor_mul(b2[:, :], b2[:, :], b4[:, :]),
                  reads=[b2k, b4k], writes=[b2k])
                o("V", lambda e: e.tensor_copy(nre[:, :], b6[:, :]), reads=[b6k], writes=[nrek])
                o("V", lambda e: e.tensor_copy(nim[:, :], b2[:, :]), reads=[b2k], writes=[nimk])
                o("S", lambda e: e.activation(out=b3[:, :], in_=b6[:, :], func=AF.Square),
                  reads=[b6k], writes=[b3k])
                o("S", lambda e: e.activation(out=b5[:, :], in_=b2[:, :], func=AF.Square),
                  reads=[b2k], writes=[b5k])
                for c in range(NCH):
                    sl = slice(c * CH, (c + 1) * CH)
                    psM, pkM = self.psum()
                    self.o("T", lambda e, psM=psM, sl=sl: (
                        e.matmul(psM[0:2, :], g64[:, 0:2], b3[:, sl], start=True, stop=False),
                        e.matmul(psM[0:2, :], g64[:, 0:2], b5[:, sl], start=False, stop=True))[-1],
                        reads=[g64k, b3k, b5k], writes=[pkM])
                    self.o("V", lambda e, psM=psM, sl=sl: e.tensor_copy(mst[0:2, sl], psM[0:2, :]),
                           reads=[pkM], writes=[mstk])
                    psY, pkY = self.psum()
                    self.o("T", lambda e, psY=psY, sl=sl: (
                        e.matmul(psY[:, :], wreb[:, :], nre[:, sl], start=True, stop=False),
                        e.matmul(psY[:, :], wimb[:, :], nim[:, sl], start=False, stop=True))[-1],
                        reads=[wrebk, wimbk, nrek, nimk], writes=[pkY])
                    self.o("V", lambda e, psY=psY, sl=sl: e.tensor_copy(yrw[:, sl], psY[:, :]),
                           reads=[pkY], writes=[yrwk])
                self.ctx.dma(self.d["msd"][b, k0:k0 + 2, :], mst[0:2, :],
                             reads=[mstk], writes=["D:msd"])
                self.ctx.dma(self.d["yraw"][b, jt * 128:(jt + 1) * 128, :], yrw[:, :],
                             reads=[yrwk], writes=["D:yraw"])

    def st_l1mixC(self):
        A = self.A
        A.reset()
        ms, msk = A.alloc(T, F32, "msC")
        inv, invk = A.alloc(T, F32, "invC")
        for b in range(BL):
            self.ctx.dma(ms[0:32, :], self.d["msd"][b, :, :], writes=[msk], reads=["D:msd"])
            self.o("S", lambda e: e.activation(out=ms[0:32, :], in_=ms[0:32, :], func=AF.Ln,
                                               scale=1.0 / 128.0, bias=EPS_LN),
                   reads=[msk, self.ck], writes=[msk])
            self.o("S", lambda e: e.activation(out=ms[0:32, :], in_=ms[0:32, :], func=AF.Exp,
                                               scale=-0.5),
                   reads=[msk], writes=[msk])
            self.ctx.dma(self.d["combd"][b, :, :], ms[0:32, :], reads=[msk], writes=["D:combd"])

    def st_l1out(self):
        A = self.A
        A.reset()
        combb, combbk = A.alloc(T, F32, "combb")
        xgt, xgtk = A.alloc(T, BF16, "xgt")
        yrt, yrtk = A.alloc(T, BF16, "yrt")
        g2, g2k = A.alloc(T, F32, "g2")
        ygt, ygtk = A.alloc(T, BF16, "ygt")
        for b in range(BL):
            for jt in range(NDI):
                k0 = 2 * jt
                self.ctx.dma(combb[0:64, :], self.d["combd"][b, k0:k0 + 1, :].broadcast_to([64, T]),
                             reads=["D:combd"], writes=[combbk])
                self.ctx.dma(combb[64:128, :], self.d["combd"][b, k0 + 1:k0 + 2, :].broadcast_to([64, T]),
                             reads=["D:combd"], writes=[combbk])
                self.ctx.dma(xgt[:, :], self.d["xgate"][b, jt * 128:(jt + 1) * 128, :],
                             writes=[xgtk], reads=["D:xgate"])
                self.ctx.dma(yrt[:, :], self.d["yraw"][b, jt * 128:(jt + 1) * 128, :],
                             writes=[yrtk], reads=["D:yraw"])
                self.o("V", lambda e: e.tensor_mul(g2[:, :], xgt[:, :], combb[:, :]),
                       reads=[xgtk, combbk], writes=[g2k])
                self.o("V", lambda e: e.tensor_mul(ygt[:, :], yrt[:, :], g2[:, :]),
                       reads=[yrtk, g2k], writes=[ygtk])
                self.ctx.dma(self.d["ygb"][b, jt * 128:(jt + 1) * 128, :], ygt[:, :],
                             reads=[ygtk], writes=["D:ygb"])
        A.reset()
        w = self.load_w_lhsT_bf16("L1_attn_out_proj_w", DI, D, "oproj")
        ygc = [A.alloc(CH, BF16, f"ygc{k}") for k in range(NDI)]
        at, atk = A.alloc(CH, BF16, "attn1t")
        for b in range(BL):
            for c in range(NCH):
                sl = slice(c * CH, (c + 1) * CH)
                for k in range(NDI):
                    ap, key = ygc[k]
                    self.ctx.dma(ap[:, :], self.d["ygb"][b, k * 128:(k + 1) * 128, sl],
                                 writes=[key], reads=["D:ygb"])
                for m in range(NT):
                    msl = slice(m * 128, (m + 1) * 128)
                    ps, pk = self.psum()
                    self.mm_group(ps[:, :], pk,
                                  [(w[k][0][:, msl], ygc[k][0][:, :]) for k in range(NDI)],
                                  reads=[w[k][1] for k in range(NDI)] + [y[1] for y in ygc])
                    self.o("V", lambda e, ps=ps: e.tensor_copy(at[:, :], ps[:, :]),
                           reads=[pk], writes=[atk])
                    self.ctx.dma(self.d["attn1"][b, msl, sl], at[:, :],
                                 reads=[atk], writes=["D:attn1"])


@_mixin
class _Block1Decl:
    def st_prep1(self):
        self.load_strip("conv_b", "conv_b_s", NPROJ)
        for i in range(4):
            self.load_strip(f"conv_k{i}", f"conv_k{i}_s", NPROJ)
        self.load_strip("theta_s", "theta_s", NDI)
        self.load_strip("score_ss", "score_ss", 1)
        self.load_strip("decay_s", "decay_s", 1)
        self.load_strip("anchor_s", "anchor_s", 1)
        self.load_strip("ns_re", "ns_re", 1)
        self.load_strip("ns_im", "ns_im", 1)
        self.load_strip("g64", "c_g64", 2)


B1A_STAGES = ["prep1", "ln1", "proj", "mixA", "mixB", "mixC", "out1"]


def declare_block1a(mk, feed_xn=False):
    for nm, shape in [("L1_attn_in_proj_w", [D, CPROJ]), ("conv_b_s", [128, NPROJ]),
                      ("conv_k0_s", [128, NPROJ]), ("conv_k1_s", [128, NPROJ]),
                      ("conv_k2_s", [128, NPROJ]), ("conv_k3_s", [128, NPROJ]),
                      ("theta_s", [128, NDI]), ("score_ss", [32, 1]),
                      ("decay_s", [28, 1]), ("anchor_s", [4, 1]),
                      ("deriv_l", [1, 3]), ("L1_attn_W_re", [64, 64]),
                      ("L1_attn_W_im", [64, 64]), ("ns_re", [128, 1]), ("ns_im", [128, 1]),
                      ("L1_attn_out_proj_w", [DI, D]), ("c_pos", [1, T]),
                      ("c_dist", [1, T]), ("c_g64", [128, 2])]:
        mk.dram_in(nm, shape)
    mk.dram_scratch("xn1f", [BL, D, T])
    mk.dram_scratch("xn1b", [BL, D, T], BF16)
    mk.dram_scratch("xval", [BL, DI, T], BF16)
    mk.dram_scratch("xgate", [BL, DI, T], BF16)
    mk.dram_scratch("sraw", [BL, 32, T])
    mk.dram_scratch("pwdram", [BL, 32, T])
    mk.dram_scratch("invdram", [BL, 32, T])
    mk.dram_scratch("msd", [BL, 32, T])
    mk.dram_scratch("combd", [BL, 32, T])
    mk.dram_scratch("yraw", [BL, DI, T], BF16)
    mk.dram_scratch("ygb", [BL, DI, T], BF16)
    mk.dram_scratch("attn1", [BL, D, T], BF16)


def emit_block1a(mk, upto=None, skip=()):
    stages = B1A_STAGES if upto is None else B1A_STAGES[:B1A_STAGES.index(upto) + 1]
    for s in stages:
        if s in skip:
            continue
        if s == "prep1":
            mk.st_prep1()
        elif s == "ln1":
            mk.st_ln("x1", "xn1f", "xn1b", "ln1")
        elif s == "proj":
            mk.st_l1proj()
        elif s == "mixA":
            mk.st_l1mixA()
        elif s == "mixB":
            mk.st_l1mixB()
        elif s == "mixC":
            mk.st_l1mixC()
        elif s == "out1":
            mk.st_l1out()


def host_prep_block1a(inp, Tcur):
    ck = np.asarray(inp["L1_attn_conv_k"], np.float32).reshape(4, CPROJ)
    theta = np.asarray(inp["L1_attn_theta"], np.float32).reshape(NHEAD * 64)
    ns = np.asarray(inp["L1_attn_norm_scale"], np.float32)
    pos = np.arange(Tcur, dtype=np.float32)
    g64 = np.zeros((128, 2), np.float32)
    g64[0:64, 0] = 1.0
    g64[64:128, 1] = 1.0
    m = {
        "L1_attn_in_proj_w": inp["L1_attn_in_proj_w"],
        "conv_b_s": vec_strip(inp["L1_attn_conv_b"], NPROJ),
        "theta_s": vec_strip(theta, NDI),
        "score_ss": np.asarray(inp["L1_attn_score_scale"], np.float32).reshape(32, 1),
        "decay_s": np.asarray(inp["L1_attn_decay_slopes"], np.float32).reshape(28, 1),
        "anchor_s": np.asarray(inp["L1_attn_anchor_slopes"], np.float32).reshape(4, 1),
        "deriv_l": np.asarray(inp["L1_attn_deriv_logits"], np.float32).reshape(1, 3),
        "L1_attn_W_re": inp["L1_attn_W_re"],
        "L1_attn_W_im": inp["L1_attn_W_im"],
        "ns_re": np.concatenate([ns[0:64], ns[0:64]]).reshape(128, 1),
        "ns_im": np.concatenate([ns[64:128], ns[64:128]]).reshape(128, 1),
        "L1_attn_out_proj_w": inp["L1_attn_out_proj_w"],
        "c_pos": pos.reshape(1, Tcur),
        "c_dist": np.maximum(np.float32(Tcur - 1) - pos, 0.0).reshape(1, Tcur),
        "c_g64": g64,
    }
    for i in range(4):
        m[f"conv_k{i}_s"] = vec_strip(ck[i], NPROJ)
    return {k: np.ascontiguousarray(np.asarray(v, np.float32)) for k, v in m.items()}


INNER = 512
NIN = 4
HID = 2730
NW1 = 22
QKS = 8.0


@_mixin
class _Block1Ffn:
    def load_w_ap_lhsT_bf16(self, dram_ap, dkey, K, M, tag, stg_n=2):
        ntk = ceil_div(K, 128)
        stg = [self.A.alloc(M, F32, f"{tag}wstg{i}") for i in range(stg_n)]
        out = []
        for k in range(ntk):
            kk = min(128, K - k * 128)
            f, fk = stg[k % stg_n]
            bb, bk = self.A.alloc(M, BF16, f"{tag}wb{k}")
            self.ctx.dma(f[:kk, :], dram_ap[k * 128:k * 128 + kk, :], writes=[fk], reads=[dkey])
            self.o("V", lambda e, bb=bb, f=f, kk=kk: e.tensor_copy(bb[:kk, :], f[:kk, :]),
                   reads=[fk], writes=[bk])
            out.append((bb, bk, kk))
        return out

    def st_ffn_qkv(self, li, fin_b):
        A = self.A
        A.reset()
        w = self.load_w_ap_lhsT_bf16(self.d["L1_ffn_wqkv"][li], "D:L1_ffn_wqkv",
                                     D, 3 * INNER, f"wqkv{li}", stg_n=1)
        fb = []
        for k in range(NT):
            ap, key = A.alloc(T + 1, BF16, f"fqkv{k}")
            fb.append((ap, key))
        qf2 = [A.alloc(T, F32, f"qkvqf{i}") for i in range(2)]
        vb2 = [A.alloc(T, BF16, f"qkvvb{i}") for i in range(2)]
        vtb2 = [A.alloc(128, BF16, f"qkvvtb{i}") for i in range(2)]
        idb, idbk = self.ident_bf16()
        for b in range(BL):
            for k in range(NT):
                ap, key = fb[k]
                self.o("V", lambda e, ap=ap: e.memset(ap[:, 0:1], 0.0), writes=[key])
                self.ctx.dma(ap[:, 1:T + 1], self.d[fin_b][b, k * 128:(k + 1) * 128, :],
                             writes=[key], reads=[f"D:{fin_b}"])
            for m in range(12):
                msl = slice(m * 128, (m + 1) * 128)
                is_v = m >= 8
                qf, qfk = qf2[m % 2]
                vb, vbk = vb2[m % 2]
                for c in range(NCH):
                    ps, pk = self.psum()
                    pairs = []
                    for k in range(NT):
                        off = 1 if k < 4 else 0
                        pairs.append((w[k][0][:, msl],
                                      fb[k][0][:, off + c * CH: off + c * CH + CH]))
                    self.mm_group(ps[:, :], pk, pairs,
                                  reads=[w[k][1] for k in range(NT)] + [f[1] for f in fb])
                    if is_v:
                        self.o("V", lambda e, ps=ps, c=c, vb=vb: e.tensor_copy(
                            vb[:, c * CH:(c + 1) * CH], ps[:, :]), reads=[pk], writes=[vbk])
                    else:
                        self.o("V", lambda e, ps=ps, c=c, qf=qf: e.tensor_copy(
                            qf[:, c * CH:(c + 1) * CH], ps[:, :]), reads=[pk], writes=[qfk])
                if is_v:
                    for tc in range(T // 128):
                        ps, pk = self.psum()
                        psb = ps[:, :].bitcast(BF16)
                        vtb, vtbk = vtb2[tc % 2]
                        self.o("T", lambda e, psb=psb, tc=tc, vb=vb: e.transpose(
                            psb[:, 0:128], vb[:, tc * 128:(tc + 1) * 128], idb[:, :]),
                            reads=[vbk, idbk], writes=[pk])
                        self.o("V", lambda e, psb=psb, vtb=vtb: e.tensor_copy(
                            vtb[:, :], psb[:, 0:128]),
                            reads=[pk], writes=[vtbk])
                        self.ctx.dma(self.d[f"vT{li}"][b, m - 8, tc * 128:(tc + 1) * 128, :],
                                     vtb[:, :], reads=[vtbk], writes=[f"D:vT{li}"])
                else:
                    dst = f"qraw{li}" if m < 4 else f"kraw{li}"
                    mm = m if m < 4 else m - 4
                    self.ctx.dma(self.d[dst][b, mm * 128:(mm + 1) * 128, :], qf[:, :],
                                 reads=[qfk], writes=[f"D:{dst}"])
        A.reset()
        csb, csbk = A.alloc(T, BF16, "csbufB")
        ssb, ssbk = A.alloc(T, BF16, "ssbufB")
        cstage, cstagek = A.alloc(T, F32, "cstage")
        qfB = [A.alloc(T, F32, f"qfB{i}") for i in range(2)]
        sqc2 = [A.alloc(CH, F32, f"sqcB{i}") for i in range(2)]
        rsbB = [A.alloc(T, F32, f"rsbB{i}") for i in range(2)]
        ss2B = rsbB
        qrotB = [A.alloc(T, F32, f"qrotB{i}") for i in range(2)]
        m1B = [A.alloc(T, F32, f"m1B{i}") for i in range(2)]
        qobB = [A.alloc(T, BF16, f"qobB{i}") for i in range(2)]
        g64, g64k = self.strip["g64"]
        rs_scr = self.dram_scratch(f"rs_scr{li}", [4, 2, T], F32)

        def load_consts(c1n, c2n):
            self.ctx.dma(cstage[:, :], self.d[c1n], writes=[cstagek], reads=[f"D:{c1n}"])
            self.o("V", lambda e: e.tensor_copy(csb[:, :], cstage[:, :]),
                   reads=[cstagek], writes=[csbk])
            self.ctx.dma(cstage[:, :], self.d[c2n], writes=[cstagek], reads=[f"D:{c2n}"])
            self.o("V", lambda e: e.tensor_copy(ssb[:, :], cstage[:, :]),
                   reads=[cstagek], writes=[ssbk])
        it = 0
        for qk in range(2):
            srcn = f"qraw{li}" if qk == 0 else f"kraw{li}"
            dstn = f"qb{li}" if qk == 0 else f"kb{li}"
            load_consts("c_csq2" if qk == 0 else "c_csk2",
                        "c_ssq2" if qk == 0 else "c_ssk2")
            dupn = f"qs_dup_{li}" if qk == 0 else f"ks_dup_{li}"
            rotn = f"qsrot_dup_{li}" if qk == 0 else f"ksrot_dup_{li}"
            dup, dupk = self.strip[dupn]
            rot, rotk = self.strip[rotn]
            for b in range(BL):
                for m in range(4):
                    qf, qfk = qfB[it % 2]
                    sqc, sqck = sqc2[it % 2]
                    ss2, ss2k = ss2B[it % 2]
                    rsb, rsbk = rsbB[it % 2]
                    qrot, qrotk = qrotB[it % 2]
                    m1, m1k = m1B[it % 2]
                    m2, m2k = qrot, qrotk
                    qob, qobk = qobB[it % 2]
                    scr_i = it % 4
                    it += 1
                    self.ctx.dma(qf[:, :], self.d[srcn][b, m * 128:(m + 1) * 128, :],
                                 writes=[qfk], reads=[f"D:{srcn}"])
                    for c in range(NCH):
                        ps2, pk2 = self.psum()
                        self.o("S", lambda e, c=c, sqc=sqc, qf=qf: e.activation(
                            out=sqc[:, :], in_=qf[:, c * CH:(c + 1) * CH], func=AF.Square),
                            reads=[qfk], writes=[sqck])
                        self.o("T", lambda e, ps2=ps2, sqc=sqc: e.matmul(
                            ps2[0:2, :], g64[:, 0:2], sqc[:, :], start=True, stop=True),
                            reads=[g64k, sqck], writes=[pk2])
                        self.o("V", lambda e, ps2=ps2, c=c, ss2=ss2: e.tensor_copy(
                            ss2[0:2, c * CH:(c + 1) * CH], ps2[0:2, :]),
                            reads=[pk2], writes=[ss2k])
                    self.o("S", lambda e, ss2=ss2: e.activation(out=ss2[0:2, :], in_=ss2[0:2, :],
                                                       func=AF.Ln, bias=1e-24),
                           reads=[ss2k, self.ck], writes=[ss2k])
                    self.o("S", lambda e, ss2=ss2: e.activation(out=ss2[0:2, :], in_=ss2[0:2, :],
                                                       func=AF.Exp, scale=-0.5),
                           reads=[ss2k], writes=[ss2k])
                    self.ctx.dma(rs_scr[scr_i, 0:2, :], ss2[0:2, :], reads=[ss2k],
                                 writes=[f"D:rs{li}{scr_i}"])
                    self.ctx.dma(rsb[0:64, :], rs_scr[scr_i, 0:1, :].broadcast_to([64, T]),
                                 reads=[f"D:rs{li}{scr_i}"], writes=[rsbk], eng="P")
                    self.ctx.dma(rsb[64:128, :], rs_scr[scr_i, 1:2, :].broadcast_to([64, T]),
                                 reads=[f"D:rs{li}{scr_i}"], writes=[rsbk], eng="P")
                    for (d0, s0) in [(0, 32), (32, 0), (64, 96), (96, 64)]:
                        self.ctx.dma(qrot[d0:d0 + 32, :], qf[s0:s0 + 32, :],
                                     reads=[qfk], writes=[qrotk])
                    self.o("V", lambda e, m1=m1, qf=qf: e.tensor_mul(m1[:, :], qf[:, :], csb[:, :]),
                           reads=[qfk, csbk], writes=[m1k])
                    self.o("V", lambda e, m1=m1, dup=dup: e.tensor_scalar_mul(
                        m1[:, :], m1[:, :], dup[:, 0:1]),
                           reads=[m1k, dupk], writes=[m1k])
                    self.o("P", lambda e, m2=m2, qrot=qrot: e.tensor_mul(
                        m2[:, :], qrot[:, :], ssb[:, :]),
                           reads=[qrotk, ssbk], writes=[m2k])
                    self.o("P", lambda e, m2=m2, rot=rot: e.tensor_scalar_mul(
                        m2[:, :], m2[:, :], rot[:, 0:1]),
                           reads=[m2k, rotk], writes=[m2k])
                    self.o("V", lambda e, m1=m1, m2=m2: e.tensor_add(m1[:, :], m1[:, :], m2[:, :]),
                           reads=[m1k, m2k], writes=[m1k])
                    self.o("V", lambda e, qob=qob, m1=m1, rsb=rsb: e.tensor_mul(
                        qob[:, :], m1[:, :], rsb[:, :]),
                           reads=[m1k, rsbk], writes=[qobk])
                    self.ctx.dma(self.d[dstn][b, m * 128:(m + 1) * 128, :], qob[:, :],
                                 reads=[qobk], writes=[f"D:{dstn}"])

    def strip_big(self, name):
        if name not in self.strip:
            ap, key = self.P.alloc(T, F32, name)
            self.ctx.dma(ap[:, :], self.d[name], writes=[key])
            self.strip[name] = (ap, key)
        return self.strip[name]

    def st_ffn_attn(self, li):
        A = self.A
        A.reset()
        qaug, qaugk = A.alloc(T, BF16, "qaug")
        kaug, kaugk = A.alloc(T, BF16, "kaug")
        NKC = T // 128
        vtl, _vtlk0 = A.alloc(65 * NKC, BF16, "vtl")
        vtlk = [f"vtl{kc}" for kc in range(NKC)]
        for _k in vtlk:
            self.ctx.merge_keys(_k, [_vtlk0])
        self.o("V", lambda e: e.memset(vtl[:, :], 1.0), writes=vtlk + [_vtlk0])
        scrE, scrEk = A.alloc(256, F32, "scrE")
        rmxb, rmxbk = A.alloc(T // 128 + 4, F32, "rmxb")
        rmx2, rmx2k = A.alloc(4, F32, "rmx2")
        rmte, rmtek = A.alloc(128, F32, "rmte")
        rmf, rmfk = A.alloc(T, F32, "rmf")
        nrm, nrmk = A.alloc(T, BF16, "nrm")
        expb2 = [A.alloc(CH, BF16, f"expb{i}") for i in range(2)]
        oacc, oacck = A.alloc(T, F32, "oacc")
        rss, rssk = A.alloc(T, F32, "rss")
        onesb, onesbk = A.alloc(2, BF16, "onesb")
        self.o("V", lambda e: e.memset(onesb[:, 0:1], 1.0), writes=[onesbk])
        mnd, mndk = self.strip_c128("c_mnd")
        mnf, mnfk = self.strip_c128("c_mnf")
        md2, md2k = self.strip_c128("c_md2")
        mf2, mf2k = self.strip_c128("c_mf2")
        NQC = T // 128
        NQG = T // CH
        for b in range(BL):
            for h in range(8):
                mv, half = h // 2, h % 2
                self.ctx.dma(qaug[0:64, :], self.d[f"qb{li}"][b, half * 64 + mv * 128:
                                                             half * 64 + mv * 128 + 64, :],
                             writes=[qaugk], reads=[f"D:qb{li}"])
                self.ctx.dma(kaug[0:64, :], self.d[f"kb{li}"][b, half * 64 + mv * 128:
                                                              half * 64 + mv * 128 + 64, :],
                             writes=[kaugk], reads=[f"D:kb{li}"])
                self.o("V", lambda e: e.memset(kaug[64:65, :], 1.0), writes=[kaugk])
                for kc in range(NQC):
                    self.ctx.dma(vtl[:, kc * 65:kc * 65 + 64],
                                 self.d[f"vT{li}"][b, mv, kc * 128:(kc + 1) * 128,
                                                   half * 64:half * 64 + 64],
                                 writes=[vtlk[kc]], reads=[f"D:vT{li}", _vtlk0])
                for qc in range(NQC):
                    j0 = max(0, qc - 4) * 128
                    nj = (qc + 1) * 128 - j0
                    njA = min(nj, 512)
                    njB = nj - njA
                    pb = 2 + 2 * (qc % 2)
                    psA, pkA = self.ps[pb], self.ps_keys[pb]
                    psB, pkB = self.ps[pb + 1], self.ps_keys[pb + 1]
                    qsl = slice(qc * 128, (qc + 1) * 128)
                    def p1(e, qsl=qsl, j0=j0, njA=njA, njB=njB, psA=psA, psB=psB):
                        last = e.matmul(psA[:, 0:njA], qaug[0:64, qsl],
                                        kaug[0:64, j0:j0 + njA], start=True, stop=True)
                        if njB:
                            last = e.matmul(psB[:, 0:njB], qaug[0:64, qsl],
                                            kaug[0:64, j0 + njA:j0 + njA + njB],
                                            start=True, stop=True)
                        return last
                    self.o("T", p1, reads=[qaugk, kaugk],
                           writes=[pkA] + ([pkB] if njB else []))
                    if njB:
                        self.o("V", lambda e, psB=psB: e.tensor_add(
                            scrE[:, 0:128], psB[:, 0:128], mnd[:, :]),
                            reads=[pkB, mndk], writes=[scrEk])
                        self.o("V", lambda e, psA=psA: e.tensor_add(
                            scrE[:, 128:256], psA[:, 0:128], mnf[:, :]),
                            reads=[pkA, mnfk], writes=[scrEk])
                        self.o("V", lambda e, qc=qc: e.tensor_reduce(
                            out=rmxb[:, qc:qc + 1], in_=scrE[:, 0:256], axis=AX.X, op=ALU.max),
                            reads=[scrEk], writes=[rmxbk])
                        self.o("V", lambda e, psA=psA: e.tensor_reduce(
                            out=rmx2[:, 0:1], in_=psA[:, 128:512], axis=AX.X, op=ALU.max),
                            reads=[pkA], writes=[rmx2k])
                        self.o("V", lambda e, qc=qc: e.tensor_max(rmxb[:, qc:qc + 1],
                                                           rmxb[:, qc:qc + 1], rmx2[:, 0:1]),
                               reads=[rmxbk, rmx2k], writes=[rmxbk])
                    else:
                        dcol = qc * 128
                        self.o("V", lambda e, psA=psA, dcol=dcol: e.tensor_add(
                            scrE[:, 0:128], psA[:, dcol:dcol + 128], mnd[:, :]),
                            reads=[pkA, mndk], writes=[scrEk])
                        self.o("V", lambda e, qc=qc: e.tensor_reduce(
                            out=rmxb[:, qc:qc + 1], in_=scrE[:, 0:128], axis=AX.X, op=ALU.max),
                            reads=[scrEk], writes=[rmxbk])
                        if dcol > 0:
                            self.o("V", lambda e, psA=psA, dcol=dcol: e.tensor_reduce(
                                out=rmx2[:, 0:1], in_=psA[:, 0:dcol], axis=AX.X, op=ALU.max),
                                reads=[pkA], writes=[rmx2k])
                            self.o("V", lambda e, qc=qc: e.tensor_max(rmxb[:, qc:qc + 1],
                                                               rmxb[:, qc:qc + 1], rmx2[:, 0:1]),
                                   reads=[rmxbk, rmx2k], writes=[rmxbk])
                NQCl = T // 128
                pst, pkt = self.ps[6], self.ps_keys[6]
                idf, idfk = self.ident_f32()
                self.o("T", lambda e, NQCl=NQCl: e.transpose(
                    pst[0:NQCl, 0:128], rmxb[:, 0:NQCl], idf[:, :]),
                    reads=[rmxbk, idfk], writes=[pkt])
                self.o("V", lambda e, NQCl=NQCl: e.tensor_scalar_mul(
                    rmte[0:NQCl, :], pst[0:NQCl, 0:128], -1.0),
                    reads=[pkt], writes=[rmtek])
                self.ctx.dma(self.d["rmx"][b, h, :].rearrange("(a c) -> a c", c=128),
                             rmte[0:NQCl, :], reads=[rmtek], writes=["D:rmx"])
                self.ctx.dma(rmf[0:1, :], self.d["rmx"][b, h, :].unsqueeze(0),
                             reads=["D:rmx"], writes=[rmfk], eng="P")
                self.o("V", lambda e: e.tensor_copy(nrm[0:1, :], rmf[0:1, :]),
                       reads=[rmfk], writes=[nrmk])
                self.ctx.dma(qaug[64:65, :], nrm[0:1, :], reads=[nrmk], writes=[qaugk])
                for qg in range(NQG):
                    psAV, pkAV = self.ps[0], self.ps_keys[0]
                    psRS, pkRS = self.ps[1], self.ps_keys[1]
                    kcs = [kc for kc in range(max(0, qg * 4 - 4), qg * 4 + 4)]
                    kcs.sort(key=lambda kc: -(min(3, kc - qg * 4 + 4)
                                              - max(0, kc - qg * 4)))
                    qsl = slice(qg * CH, (qg + 1) * CH)
                    pend = None
                    def emit_av(p, last):
                        ilo_, wdt_, expb_, expbk_, kc_, first_ = p
                        self.o("T", lambda e, kc=kc_, first=first_, last=last, psAV=psAV,
                               ilo=ilo_, wdt=wdt_, expb=expb_: e.matmul(
                            psAV[0:65, ilo:ilo + wdt], vtl[:, kc * 65:(kc + 1) * 65],
                            expb[:, ilo:ilo + wdt], start=first, stop=last),
                            reads=[vtlk[kc_], _vtlk0, expbk_], writes=[pkAV])
                    for ki, kc in enumerate(kcs):
                        si_lo = max(0, kc - qg * 4)
                        si_hi = min(3, kc - qg * 4 + 4)
                        ilo = si_lo * 128
                        wdt = (si_hi - si_lo + 1) * 128
                        psS, pkS = (self.ps[2], self.ps_keys[2]) if ki % 2 == 0 else \
                                   (self.ps[3], self.ps_keys[3])
                        expb, expbk = expb2[ki % 2]
                        self.o("T", lambda e, psS=psS, kc=kc, qg=qg, ilo=ilo, wdt=wdt: e.matmul(
                            psS[:, 0:wdt], kaug[0:65, kc * 128:(kc + 1) * 128],
                            qaug[0:65, qg * CH + ilo:qg * CH + ilo + wdt],
                            start=True, stop=True),
                            reads=[kaugk, qaugk], writes=[pkS])
                        self.o("S", lambda e, psS=psS, ilo=ilo, wdt=wdt, expb=expb: e.activation(
                            out=expb[:, ilo:ilo + wdt], in_=psS[:, 0:wdt],
                            func=AF.Exp, scale=QKS),
                            reads=[pkS], writes=[expbk])
                        for si in range(si_lo, si_hi + 1):
                            rel = (qg * 4 + si) - kc
                            if rel == 0:
                                self.o("P", lambda e, si=si, expb=expb: e.tensor_mul(
                                    expb[:, si * 128:(si + 1) * 128],
                                    expb[:, si * 128:(si + 1) * 128], md2[:, :]),
                                    reads=[expbk, md2k], writes=[expbk])
                            elif rel == 4:
                                self.o("P", lambda e, si=si, expb=expb: e.tensor_mul(
                                    expb[:, si * 128:(si + 1) * 128],
                                    expb[:, si * 128:(si + 1) * 128], mf2[:, :]),
                                    reads=[expbk, mf2k], writes=[expbk])
                        if pend is not None:
                            emit_av(pend, False)
                        pend = (ilo, wdt, expb, expbk, kc, ki == 0)
                    emit_av(pend, True)
                    self.o("V", lambda e, psAV=psAV, qsl=qsl: e.tensor_copy(
                        oacc[0:64, qsl], psAV[0:64, :]), reads=[pkAV], writes=[oacck])
                    self.o("V", lambda e, psAV=psAV, qsl=qsl: e.tensor_copy(
                        rss[0:1, qsl], psAV[64:65, :]), reads=[pkAV], writes=[rssk])
                self.ctx.dma(self.d[f"oraw{li}"][b, h * 64:(h + 1) * 64, :], oacc[0:64, :],
                             reads=[oacck], writes=[f"D:oraw{li}"])
                self.ctx.dma(self.d[f"rsum{li}"][b, h, :].unsqueeze(0), rss[0:1, :],
                             reads=[rssk], writes=[f"D:rsum{li}"])

    def strip_c128(self, name):
        if name not in self.strip:
            ap, key = self.P.alloc(128, F32, name)
            self.ctx.dma(ap[:, :], self.d[name], writes=[key])
            self.strip[name] = (ap, key)
        return self.strip[name]

    def st_ffn_wo(self, li, fin_f, gout):
        A = self.A
        A.reset()
        w = self.load_w_ap_lhsT_bf16(self.d["L1_ffn_wo"][li], "D:L1_ffn_wo",
                                     INNER, D, f"wo{li}")
        rcp, rcpk = A.alloc(T, F32, "rcp")
        rcb, rcbk = A.alloc(T, F32, "rcb")
        orm, ormk = A.alloc(T, F32, "orm")
        onb = [A.alloc(T, BF16, f"onb{k}") for k in range(NIN)]
        ft, ftk = A.alloc(T, F32, "ftwo")
        go, gok = A.alloc(T, F32, "gowo")
        gob, gobk = A.alloc(T, BF16, "gobwo")
        rc_scr = self.dram_scratch(f"rc_scr{li}", [8, T], F32)
        for b in range(BL):
            self.ctx.dma(rcp[0:8, :], self.d[f"rsum{li}"][b, :, :], writes=[rcpk],
                         reads=[f"D:rsum{li}"])
            self.o("V", lambda e: e.reciprocal(rcp[0:8, :], rcp[0:8, :]),
                   reads=[rcpk], writes=[rcpk])
            self.ctx.dma(rc_scr[0:8, :], rcp[0:8, :], reads=[rcpk], writes=[f"D:rc{li}"])
            for mv in range(NIN):
                h0 = mv * 2
                self.ctx.dma(rcb[0:64, :], rc_scr[h0:h0 + 1, :].broadcast_to([64, T]),
                             reads=[f"D:rc{li}"], writes=[rcbk], eng="P")
                self.ctx.dma(rcb[64:128, :], rc_scr[h0 + 1:h0 + 2, :].broadcast_to([64, T]),
                             reads=[f"D:rc{li}"], writes=[rcbk], eng="P")
                self.ctx.dma(orm[:, :], self.d[f"oraw{li}"][b, mv * 128:(mv + 1) * 128, :],
                             writes=[ormk], reads=[f"D:oraw{li}"])
                ap, key = onb[mv]
                self.o("V", lambda e, ap=ap: e.tensor_mul(ap[:, :], orm[:, :], rcb[:, :]),
                       reads=[ormk, rcbk], writes=[key])
            for m in range(NT):
                msl = slice(m * 128, (m + 1) * 128)
                self.ctx.dma(ft[:, :], self.d[fin_f][b, msl, :], writes=[ftk],
                             reads=[f"D:{fin_f}"])
                for c in range(NCH):
                    ps, pk = self.psum()
                    sl = slice(c * CH, (c + 1) * CH)
                    self.mm_group(ps[:, :], pk,
                                  [(w[k][0][:, msl], onb[k][0][:, sl]) for k in range(NIN)],
                                  reads=[w[k][1] for k in range(NIN)] + [x[1] for x in onb])
                    self.o("V", lambda e, ps=ps, sl=sl: e.tensor_add(go[:, sl], ps[:, :],
                                                                     ft[:, sl]),
                           reads=[pk, ftk], writes=[gok])
                self.o("V", lambda e: e.tensor_copy(gob[:, :], go[:, :]),
                       reads=[gok], writes=[gobk])
                self.ctx.dma(self.d[gout][b, msl, :], go[:, :], reads=[gok],
                             writes=[f"D:{gout}"])
                self.ctx.dma(self.d[gout + "b"][b, msl, :], gob[:, :], reads=[gobk],
                             writes=[f"D:{gout}b"])

    def st_ffn_lnmid(self, li, gsrc, xmdst):
        A = self.A
        A.reset()
        lng, lngk = self.strip[f"lng{li}"]
        lnb, lnbk = self.strip[f"lnb{li}"]
        onesw, onesk = A.alloc(2, F32, "onesm")
        self.o("V", lambda e: e.memset(onesw[:, 0:1], 1.0), writes=[onesk])
        xch = [A.alloc(CH + 1, F32, f"mch{i}") for i in range(2)]
        sqch, sqchk = A.alloc(CH, F32, "msq")
        R1, R1k = A.alloc(T, F32, "mR1")
        R2, R2k = A.alloc(T, F32, "mR2")
        R3, R3k = A.alloc(T, F32, "mR3")
        R4, R4k = A.alloc(T, F32, "mR4")
        R5, R5k = A.alloc(T, F32, "mR5")
        sums, sumsq = R1[0:1, :], R2[0:1, :]
        c1, c2, msq2 = R3[0:1, :], R4[0:1, :], R5[0:1, :]
        R1k_c1, R2k_c2, R2k_m = R3k, R4k, R5k
        c1b, c1bk = A.alloc(T, F32, "mc1b")
        c2b, c2bk = A.alloc(T, F32, "mc2b")
        xfull = [A.alloc(T + 1, F32, f"mxf{i}") for i in range(2)]
        t1, t1k = A.alloc(T, F32, "mt1")
        xmb, xmbk = A.alloc(T, BF16, "mxmb")
        scr_b = self.dram_scratch(f"mscr_{li}", [2, T], F32)
        for b in range(BL):
            for c in range(NCH):
                psA, pkA = self.psum()
                psB, pkB = self.psum()
                for j in range(NT):
                    xa, xk = xch[j % 2]
                    if c == 0:
                        self.o("V", lambda e, xa=xa: e.memset(xa[:, 0:1], 0.0), writes=[xk])
                        self.ctx.dma(xa[:, 1:CH + 1], self.d[gsrc][b, j * 128:(j + 1) * 128,
                                                                   0:CH],
                                     writes=[xk], reads=[f"D:{gsrc}"])
                    else:
                        self.ctx.dma(xa[:, 0:CH + 1], self.d[gsrc][b, j * 128:(j + 1) * 128,
                                                                   c * CH - 1:c * CH + CH],
                                     writes=[xk], reads=[f"D:{gsrc}"])
                    off = 1 if j < 4 else 0
                    self.o("S", lambda e, xa=xa, off=off: e.activation(
                        out=sqch[:, :], in_=xa[:, off:off + CH], func=AF.Square),
                        reads=[xk], writes=[sqchk])
                    self.o("T", lambda e, psA=psA, xa=xa, off=off, j=j: e.matmul(
                        psA[0:1, :], onesw[:, 0:1], xa[:, off:off + CH],
                        start=(j == 0), stop=(j == NT - 1)),
                        reads=[onesk, xk], writes=[pkA])
                    self.o("T", lambda e, psB=psB, j=j: e.matmul(
                        psB[0:1, :], onesw[:, 0:1], sqch[:, :],
                        start=(j == 0), stop=(j == NT - 1)),
                        reads=[onesk, sqchk], writes=[pkB])
                self.o("V", lambda e, psA=psA, c=c: e.tensor_copy(
                    sums[:, c * CH:(c + 1) * CH], psA[0:1, :]), reads=[pkA], writes=[R1k])
                self.o("V", lambda e, psB=psB, c=c: e.tensor_copy(
                    sumsq[:, c * CH:(c + 1) * CH], psB[0:1, :]), reads=[pkB], writes=[R2k])
            self.o("V", lambda e: e.tensor_scalar_mul(sums, sums, 1.0 / D),
                   reads=[R1k], writes=[R1k])
            self.o("S", lambda e: e.activation(out=msq2, in_=sums, func=AF.Square),
                   reads=[R1k], writes=[R2k_m])
            self.o("V", lambda e: e.tensor_scalar_mul(sumsq, sumsq, 1.0 / D),
                   reads=[R2k], writes=[R2k])
            self.o("V", lambda e: e.tensor_sub(sumsq, sumsq, msq2),
                   reads=[R2k, R2k_m], writes=[R2k])
            self.o("S", lambda e: e.activation(out=sumsq, in_=sumsq, func=AF.Ln, bias=EPS_LN),
                   reads=[R2k, self.ck], writes=[R2k])
            self.o("S", lambda e: e.activation(out=c1, in_=sumsq, func=AF.Exp, scale=-0.5),
                   reads=[R2k], writes=[R1k_c1])
            self.o("V", lambda e: e.scalar_tensor_tensor(out=c2, in0=sums, scalar=-1.0,
                                                         in1=c1, op0=ALU.mult, op1=ALU.mult),
                   reads=[R1k, R1k_c1], writes=[R2k_c2])
            self.ctx.dma(scr_b[0:1, :], c1, reads=[R1k_c1], writes=[f"D:mscr{li}"])
            self.ctx.dma(scr_b[1:2, :], c2, reads=[R2k_c2], writes=[f"D:mscr{li}"])
            self.ctx.dma(c1b[:, :], scr_b[0:1, :].broadcast_to([128, T]),
                         reads=[f"D:mscr{li}"], writes=[c1bk], eng="P")
            self.ctx.dma(c2b[:, :], scr_b[1:2, :].broadcast_to([128, T]),
                         reads=[f"D:mscr{li}"], writes=[c2bk], eng="P")
            for j in range(NT):
                xa, xk = xfull[j % 2]
                self.o("V", lambda e, xa=xa: e.memset(xa[:, 0:1], 0.0), writes=[xk])
                self.ctx.dma(xa[:, 1:T + 1], self.d[gsrc][b, j * 128:(j + 1) * 128, :],
                             writes=[xk], reads=[f"D:{gsrc}"])
                off = 1 if j < 4 else 0
                self.o("V", lambda e, xa=xa, off=off: e.tensor_mul(
                    t1[:, :], xa[:, off:off + T], c1b[:, :]),
                    reads=[xk, c1bk], writes=[t1k])
                self.o("V", lambda e: e.tensor_add(t1[:, :], t1[:, :], c2b[:, :]),
                       reads=[t1k, c2bk], writes=[t1k])
                self.o("V", lambda e, j=j: e.tensor_scalar(
                    out=xmb[:, :], in0=t1[:, :], scalar1=lng[:, j:j + 1],
                    scalar2=lnb[:, j:j + 1], op0=ALU.mult, op1=ALU.add),
                    reads=[t1k, lngk, lnbk], writes=[xmbk])
                self.ctx.dma(self.d[xmdst][b, j * 128:(j + 1) * 128, :], xmb[:, :],
                             reads=[xmbk], writes=[f"D:{xmdst}"])

    def st_ffn_w1(self, li, xmsrc):
        A = self.A
        A.reset()
        w = self.load_w_ap_lhsT_bf16(self.d["L1_ffn_w1"][li], "D:L1_ffn_w1",
                                     D, 2 * HID, f"w1{li}", stg_n=1)
        b1a, b1ak = self.strip[f"b1a{li}"]
        b1g, b1gk = self.strip[f"b1g{li}"]
        xm = []
        for k in range(NT):
            ap, key = A.alloc(T, BF16, f"xm{k}")
            xm.append((ap, key))
        gl, glk = A.alloc(CH, F32, "gelu")
        ta, tak = A.alloc(CH, F32, "ta")
        hgm, hgmk = A.alloc(T, BF16, "hgm")
        for b in range(BL):
            for k in range(NT):
                ap, key = xm[k]
                self.ctx.dma(ap[:, :], self.d[xmsrc][b, k * 128:(k + 1) * 128, :],
                             writes=[key], reads=[f"D:{xmsrc}"])
            for m in range(NW1):
                mm = min(128, HID - m * 128)
                asl = slice(m * 128, m * 128 + mm)
                gsl = slice(HID + m * 128, HID + m * 128 + mm)
                for c in range(NCH):
                    sl = slice(c * CH, (c + 1) * CH)
                    psA, pkA = self.psum()
                    psG, pkG = self.psum()
                    self.mm_group(psA[:mm, :], pkA,
                                  [(w[k][0][:, asl], xm[k][0][:, sl]) for k in range(NT)],
                                  reads=[w[k][1] for k in range(NT)] + [x[1] for x in xm])
                    self.mm_group(psG[:mm, :], pkG,
                                  [(w[k][0][:, gsl], xm[k][0][:, sl]) for k in range(NT)],
                                  reads=[w[k][1] for k in range(NT)] + [x[1] for x in xm])
                    self.o("S", lambda e, psG=psG, m=m, mm=mm: e.activation(
                        out=gl[:mm, :], in_=psG[:mm, :], func=AF.Gelu,
                        bias=b1g[:mm, m:m + 1]),
                        reads=[pkG, b1gk], writes=[glk])
                    self.o("V", lambda e, psA=psA, m=m, mm=mm: e.tensor_scalar(
                        out=ta[:mm, :], in0=psA[:mm, :], scalar1=b1a[:mm, m:m + 1],
                        scalar2=None, op0=ALU.add),
                        reads=[pkA, b1ak], writes=[tak])
                    self.o("V", lambda e, sl=sl, mm=mm: e.tensor_mul(
                        hgm[:mm, sl], ta[:mm, :], gl[:mm, :]),
                        reads=[tak, glk], writes=[hgmk])
                self.ctx.dma(self.d[f"hgb{li}"][b, m * 128:m * 128 + mm, :], hgm[:mm, :],
                             reads=[hgmk], writes=[f"D:hgb{li}"])

    def st_ffn_w2(self, li, gsrc, fdst, also_bf16):
        A = self.A
        A.reset()
        w = self.load_w_ap_lhsT_bf16(self.d["L1_ffn_w2"][li], "D:L1_ffn_w2",
                                     HID, D, f"w2{li}")
        b2, b2k = self.strip[f"b2{li}"]
        hq = [A.alloc(CH, BF16, f"hgc{k}") for k in range(NW1)]
        gt, gtk = A.alloc(CH, F32, "gt2")
        f2, f2k = A.alloc(CH, F32, "f2t")
        f2b, f2bk = A.alloc(CH, BF16, "f2bt")
        for b in range(BL):
            for c in range(NCH):
                sl = slice(c * CH, (c + 1) * CH)
                for k in range(NW1):
                    kk = min(128, HID - k * 128)
                    ap, key = hq[k]
                    self.ctx.dma(ap[:kk, :], self.d[f"hgb{li}"][b, k * 128:k * 128 + kk, sl],
                                 writes=[key], reads=[f"D:hgb{li}"])
                for m in range(NT):
                    msl = slice(m * 128, (m + 1) * 128)
                    ps, pk = self.psum()
                    pairs = []
                    for k in range(NW1):
                        kk = min(128, HID - k * 128)
                        pairs.append((w[k][0][:kk, msl], hq[k][0][:kk, :]))
                    self.mm_group(ps[:, :], pk, pairs,
                                  reads=[w[k][1] for k in range(NW1)] + [x[1] for x in hq])
                    self.ctx.dma(gt[:, :], self.d[gsrc][b, msl, sl], writes=[gtk],
                                 reads=[f"D:{gsrc}"])
                    self.o("V", lambda e, ps=ps, m=m: e.tensor_scalar(
                        out=f2[:, :], in0=ps[:, :], scalar1=b2[:, m:m + 1],
                        scalar2=None, op0=ALU.add),
                        reads=[pk, b2k], writes=[f2k])
                    self.o("V", lambda e: e.tensor_add(f2[:, :], f2[:, :], gt[:, :]),
                           reads=[f2k, gtk], writes=[f2k])
                    self.ctx.dma(self.d[fdst][b, msl, sl], f2[:, :], reads=[f2k],
                                 writes=[f"D:{fdst}"])
                    if also_bf16:
                        self.o("V", lambda e: e.tensor_copy(f2b[:, :], f2[:, :]),
                               reads=[f2k], writes=[f2bk])
                        self.ctx.dma(self.d[fdst + "b"][b, msl, sl], f2b[:, :],
                                     reads=[f2bk], writes=[f"D:{fdst}b"])

    def st_final(self):
        A = self.A
        A.reset()
        x1t, x1k = A.alloc(T, F32, "fx1")
        att, attk = A.alloc(T, BF16, "fatt")
        f1t, f1k = A.alloc(T, F32, "ff1")
        ot, otk = A.alloc(T, F32, "fout")
        for b in range(BL):
            for j in range(NT):
                msl = slice(j * 128, (j + 1) * 128)
                self.ctx.dma(x1t[:, :], self.d["x1"][b, msl, :], writes=[x1k], reads=["D:x1"])
                self.ctx.dma(att[:, :], self.d["attn1"][b, msl, :], writes=[attk],
                             reads=["D:attn1"])
                self.ctx.dma(f1t[:, :], self.d["f1"][b, msl, :], writes=[f1k], reads=["D:f1"])
                self.o("V", lambda e: e.tensor_add(ot[:, :], x1t[:, :], f1t[:, :]),
                       reads=[x1k, f1k], writes=[otk])
                self.o("V", lambda e: e.tensor_add(ot[:, :], ot[:, :], att[:, :]),
                       reads=[otk, attk], writes=[otk])
                self.ctx.dma(self.d["out"][b, msl, :], ot[:, :], reads=[otk], writes=["D:out"])


B1B_STAGES = ["prep2",
              "qkv0", "attn0", "wo0", "lnm0", "w10", "w20",
              "qkv1", "attn1s", "wo1", "lnm1", "w11", "w21", "final"]


def declare_block1b(mk):
    for nm, shape in [("L1_ffn_wqkv", [2, D, 3 * INNER]), ("L1_ffn_wo", [2, INNER, D]),
                      ("L1_ffn_w1", [2, D, 2 * HID]), ("L1_ffn_w2", [2, HID, D]),
                      ("c_csq2", [128, T]), ("c_ssq2", [128, T]),
                      ("c_csk2", [128, T]), ("c_ssk2", [128, T]),
                      ("c_mnd", [128, 128]), ("c_mnf", [128, 128]),
                      ("c_md2", [128, 128]), ("c_mf2", [128, 128]),
                      ("qs_dup_0", [128, 1]), ("qsrot_dup_0", [128, 1]),
                      ("ks_dup_0", [128, 1]), ("ksrot_dup_0", [128, 1]),
                      ("qs_dup_1", [128, 1]), ("qsrot_dup_1", [128, 1]),
                      ("ks_dup_1", [128, 1]), ("ksrot_dup_1", [128, 1]),
                      ("lng0_s", [128, NT]), ("lnb0_s", [128, NT]),
                      ("lng1_s", [128, NT]), ("lnb1_s", [128, NT]),
                      ("b1a0_s", [128, NW1]), ("b1g0_s", [128, NW1]),
                      ("b1a1_s", [128, NW1]), ("b1g1_s", [128, NW1]),
                      ("b20_s", [128, NT]), ("b21_s", [128, NT])]:
        mk.dram_in(nm, shape)
    for li in range(2):
        mk.dram_scratch(f"qraw{li}", [BL, INNER, T])
        mk.dram_scratch(f"kraw{li}", [BL, INNER, T])
        mk.dram_scratch(f"qb{li}", [BL, INNER, T], BF16)
        mk.dram_scratch(f"kb{li}", [BL, INNER, T], BF16)
        mk.dram_scratch(f"vT{li}", [BL, NIN, T, 128], BF16)
        mk.dram_scratch(f"oraw{li}", [BL, INNER, T])
        mk.dram_scratch(f"rsum{li}", [BL, 8, T])
        mk.dram_scratch(f"g{li}", [BL, D, T])
        mk.dram_scratch(f"g{li}b", [BL, D, T], BF16)
        mk.dram_scratch(f"xm{li}", [BL, D, T], BF16)
        mk.dram_scratch(f"hgb{li}", [BL, HID, T], BF16)
    mk.dram_scratch("rmx", [BL, 8, T])
    mk.dram_scratch("f0", [BL, D, T])
    mk.dram_scratch("f0b", [BL, D, T], BF16)
    mk.dram_scratch("f1", [BL, D, T])
    mk.dram_out("out", [BL, D, T])


def emit_block1b(mk, upto=None, skip=()):
    stages = B1B_STAGES if upto is None else B1B_STAGES[:B1B_STAGES.index(upto) + 1]
    for s in stages:
        if s in skip:
            continue
        if s == "prep2":
            if "g64" not in mk.strip:
                mk.load_strip("g64", "c_g64", 2)
            for li in range(2):
                mk.load_strip(f"qs_dup_{li}", f"qs_dup_{li}", 1)
                mk.load_strip(f"qsrot_dup_{li}", f"qsrot_dup_{li}", 1)
                mk.load_strip(f"ks_dup_{li}", f"ks_dup_{li}", 1)
                mk.load_strip(f"ksrot_dup_{li}", f"ksrot_dup_{li}", 1)
                mk.load_strip(f"lng{li}", f"lng{li}_s", NT)
                mk.load_strip(f"lnb{li}", f"lnb{li}_s", NT)
                mk.load_strip(f"b1a{li}", f"b1a{li}_s", NW1)
                mk.load_strip(f"b1g{li}", f"b1g{li}_s", NW1)
                mk.load_strip(f"b2{li}", f"b2{li}_s", NT)
        elif s == "qkv0":
            mk.st_ffn_qkv(0, "xn1b")
        elif s == "attn0":
            mk.st_ffn_attn(0)
        elif s == "wo0":
            mk.st_ffn_wo(0, "xn1f", "g0")
        elif s == "lnm0":
            mk.st_ffn_lnmid(0, "g0", "xm0")
        elif s == "w10":
            mk.st_ffn_w1(0, "xm0")
        elif s == "w20":
            mk.st_ffn_w2(0, "g0", "f0", True)
        elif s == "qkv1":
            mk.st_ffn_qkv(1, "f0b")
        elif s == "attn1s":
            mk.st_ffn_attn(1)
        elif s == "wo1":
            mk.st_ffn_wo(1, "f0", "g1")
        elif s == "lnm1":
            mk.st_ffn_lnmid(1, "g1", "xm1")
        elif s == "w11":
            mk.st_ffn_w1(1, "xm1")
        elif s == "w21":
            mk.st_ffn_w2(1, "g1", "f1", False)
        elif s == "final":
            mk.st_final()


def host_prep_block1b(inp, Tcur):
    DIM_HEAD = 64
    WINDOW = 512
    half_idx = np.arange(0, DIM_HEAD, 2, dtype=np.float64)
    inv_freq = 1.0 / 10000.0 ** (half_idx / DIM_HEAD)
    t = np.arange(Tcur, dtype=np.float64)
    freqs = t[:, None] * inv_freq[None, :]
    freqs = np.concatenate([freqs, freqs], axis=-1)
    base = (half_idx + 0.4 * DIM_HEAD) / (1.4 * DIM_HEAD)
    power = (t - Tcur // 2) / float(WINDOW // 2)
    scale = base[None, :] ** power[:, None]
    scale = np.concatenate([scale, scale], axis=-1)
    cos, sin = np.cos(freqs), np.sin(freqs)
    sgn = np.ones((1, 64)); sgn[0, :32] = -1.0
    csq = (cos * scale).T
    ssq = (sin * scale).T * sgn.T
    csk = (cos / scale).T
    ssk = (sin / scale).T * sgn.T
    ii = np.arange(128)
    mnd = np.where(ii[:, None] >= ii[None, :], 0.0, -1e30)
    mnf = np.where(ii[:, None] <= ii[None, :], 0.0, -1e30)
    md2 = (ii[None, :] >= ii[:, None]).astype(np.float32)
    mf2 = (ii[None, :] <= ii[:, None]).astype(np.float32)
    rot = (np.arange(64) + 32) % 64
    m = {
        "L1_ffn_wqkv": inp["L1_ffn_wqkv"], "L1_ffn_wo": inp["L1_ffn_wo"],
        "L1_ffn_w1": inp["L1_ffn_w1"], "L1_ffn_w2": inp["L1_ffn_w2"],
        "c_csq2": np.tile(csq, (2, 1)), "c_ssq2": np.tile(ssq, (2, 1)),
        "c_csk2": np.tile(csk, (2, 1)), "c_ssk2": np.tile(ssk, (2, 1)),
        "c_mnd": mnd, "c_mnf": mnf, "c_md2": md2, "c_mf2": mf2,
    }
    qsc = np.asarray(inp["L1_ffn_q_scale"], np.float32)
    ksc = np.asarray(inp["L1_ffn_k_scale"], np.float32)
    for li in range(2):
        m[f"qs_dup_{li}"] = np.tile(qsc[li], 2).reshape(128, 1)
        m[f"qsrot_dup_{li}"] = np.tile(qsc[li][rot], 2).reshape(128, 1)
        m[f"ks_dup_{li}"] = np.tile(ksc[li], 2).reshape(128, 1)
        m[f"ksrot_dup_{li}"] = np.tile(ksc[li][rot], 2).reshape(128, 1)
        m[f"lng{li}_s"] = vec_strip(np.asarray(inp["L1_ffn_ln_g"])[li], NT)
        m[f"lnb{li}_s"] = vec_strip(np.asarray(inp["L1_ffn_ln_b"])[li], NT)
        b1 = np.asarray(inp["L1_ffn_b1"], np.float32)[li]
        m[f"b1a{li}_s"] = vec_strip(b1[0:HID], NW1)
        m[f"b1g{li}_s"] = vec_strip(b1[HID:2 * HID], NW1)
        m[f"b2{li}_s"] = vec_strip(np.asarray(inp["L1_ffn_b2"])[li], NT)
    return {k: np.ascontiguousarray(np.asarray(v, np.float32)) for k, v in m.items()}


def declare_full(mk):
    declare_block0(mk)
    declare_block1a(mk)
    declare_block1b(mk)


def emit_full(mk):
    mk.setup_consts([EPS_LN, PI / 2, 1.0, 1e-24])
    emit_block0(mk, consts=False)
    emit_block1a(mk)
    emit_block1b(mk)


def host_prep_full(inp, b_sl, Tcur):
    m = host_prep_block0(inp, b_sl, Tcur)
    m.update(host_prep_block1a(inp, Tcur))
    m.update(host_prep_block1b(inp, Tcur))
    g64 = np.zeros((128, 2), np.float32)
    g64[0:64, 0] = 1.0
    g64[64:128, 1] = 1.0
    m["c_g64"] = g64
    return m


from concourse.bass_utils import run_bass_kernel_spmd

N_CORES = 8


def _host_base(inputs):
    inp = {k: np.asarray(v) for k, v in inputs.items()}
    m = host_prep_block0(inp, None, T)
    m.update(host_prep_block1a(inp, T))
    m.update(host_prep_block1b(inp, T))
    g64 = np.zeros((128, 2), np.float32)
    g64[0:64, 0] = 1.0
    g64[64:128, 1] = 1.0
    m["c_g64"] = g64
    return m


_BUILD_CACHE = {}


def _build():
    if "nc" not in _BUILD_CACHE:
        nc = bass.Bass("TRN2", target_bir_lowering=False, debug=False)
        m = MK(nc)
        declare_full(m)
        emit_full(m)
        m.ctx.barrier_end()
        m.ctx.finish()
        _BUILD_CACHE["nc"] = nc
    return _BUILD_CACHE["nc"]


def kernel(**inputs):
    nc = _build()
    base = _host_base(inputs)
    x = np.asarray(inputs["L0_attn_x"], np.float32)
    in_maps = []
    for c in range(N_CORES):
        mm = dict(base)
        mm["x"] = np.ascontiguousarray(x[2 * c:2 * c + 2].transpose(0, 2, 1))
        in_maps.append(mm)
    res = run_bass_kernel_spmd(nc, in_maps, core_ids=list(range(N_CORES)))
    outs = [np.asarray(r["out"], np.float32).transpose(0, 2, 1)
            for r in res.results]
    return np.ascontiguousarray(np.concatenate(outs, axis=0))


# version 06
# speedup vs baseline: 1.1490x; 1.1490x over previous
import concourse.bass as bass
import concourse.mybir as mybir

ENGS = ["T", "V", "S", "P", "IO"]
SECTION = {"T": "tensor", "V": "vector", "S": "scalar", "P": "gpsimd", "IO": "sync"}
DMA_POOL = 48


class Ctx:
    def __init__(self, nc: bass.Bass, dma_pool=DMA_POOL):
        self.nc = nc
        self.sems = {e: nc.alloc_semaphore(f"sem_{e}") for e in ["T", "V", "S", "P"]}
        self.count = {e: 0 for e in ["T", "V", "S", "P"]}
        self.dma_pool_n = dma_pool
        self.dma_sw_n = 24
        tot = dma_pool + self.dma_sw_n
        self.dma_sems = [nc.alloc_semaphore(f"dma{i}") for i in range(tot)]
        self.dma_target = [0] * tot
        self.dma_waited = [0] * tot
        self.dma_rr = 0
        self.dma_rr_sw = dma_pool
        self.clock = {e: {} for e in ENGS}
        self.streams = {e: [] for e in ENGS}
        self.bufs = {}
        self.nops = 0

    def _buf(self, key):
        if key not in self.bufs:
            self.bufs[key] = {"w": None, "r": [], "ww": []}
        return self.bufs[key]

    def _need_edges(self, eng, reads, writes, is_dma=False):
        need = {}
        def add(src, val):
            if src == "T" == eng:
                return
            if val <= self.clock[eng].get(src, 0):
                return
            need[src] = max(need.get(src, 0), val)
        for key in reads:
            b = self._buf(key)
            if b["w"] is not None:
                add(*b["w"])
            for (src, val) in b["ww"]:
                add(src, val)
        for key in writes:
            b = self._buf(key)
            if b["w"] is not None:
                wsrc = b["w"][0]
                if not (is_dma and isinstance(wsrc, tuple)):
                    add(*b["w"])
                    for (src, val) in b["ww"]:
                        add(src, val)
            for (src, val) in b["r"]:
                add(src, val)
        return need

    def _emit_waits(self, eng, need):
        for src, val in sorted(need.items(), key=lambda kv: str(kv[0])):
            self.streams[eng].append(("wait", (src, val)))
            self.clock[eng][src] = val
            if isinstance(src, tuple):
                idx = src[1]
                self.dma_waited[idx] = max(self.dma_waited[idx], val)

    def op(self, eng, fn, reads=(), writes=(), tag=""):
        assert eng in ("T", "V", "S", "P")
        need = self._need_edges(eng, reads, writes)
        self._emit_waits(eng, need)
        self.count[eng] += 1
        myval = self.count[eng]
        self.streams[eng].append(("op", (fn, eng, myval, tag)))
        self.nops += 1
        for key in writes:
            b = self._buf(key)
            b["w"] = (eng, myval)
            b["r"] = []
            b["ww"] = []
        for key in reads:
            self._buf(key)["r"].append((eng, myval))
        return myval

    def dma(self, dst_ap, src_ap, reads=(), writes=(), eng=None, tag="", allow_slow=False):
        if eng is None:
            try:
                is_load = (str(src_ap.space).endswith("DRAM")
                           and str(dst_ap.space).endswith("SBUF"))
            except Exception:
                is_load = True
            eng = "IO" if is_load else "P"
        need = self._need_edges(eng, reads, writes, is_dma=True)
        self._emit_waits(eng, need)
        if eng == "P":
            idx = self.dma_rr_sw
            self.dma_rr_sw = self.dma_pool_n + ((idx - self.dma_pool_n + 1) % self.dma_sw_n)
        else:
            idx = self.dma_rr
            self.dma_rr = (self.dma_rr + 1) % self.dma_pool_n
        if self.dma_waited[idx] < self.dma_target[idx]:
            self._emit_waits(eng, {("D", idx): self.dma_target[idx]})
        self.dma_target[idx] += 16
        target = self.dma_target[idx]
        sem = self.dma_sems[idx]

        def fn(e, dst_ap=dst_ap, src_ap=src_ap, sem=sem):
            kw = dict(allow_slow_non_contiguous=True) if allow_slow else {}
            return e.dma_start(out=dst_ap, in_=src_ap, **kw)

        self.streams[eng].append(("dma", (fn, sem, tag)))
        self.nops += 1
        src = ("D", idx)
        for key in writes:
            b = self._buf(key)
            prev = b["w"]
            b["w"] = (src, target)
            if prev is not None and isinstance(prev[0], tuple):
                b["ww"] = list(b["ww"]) + [prev]
            else:
                b["ww"] = []
            b["r"] = []
        for key in reads:
            self._buf(key)["r"].append((src, target))
        return (src, target)

    def barrier_end(self):
        need = {}
        for e in ["T", "V", "S", "P"]:
            if self.count[e] > self.clock["IO"].get(e, 0):
                need[e] = self.count[e]
        for i in range(len(self.dma_sems)):
            if self.dma_target[i] > self.clock["IO"].get(("D", i), 0):
                need[("D", i)] = self.dma_target[i]
        self._emit_waits("IO", need)

    def finish(self):
        with self.nc.Block() as block:
            for eng in ENGS:
                stream = self.streams[eng]
                if not stream:
                    continue
                section = getattr(block, SECTION[eng])

                def make_body(stream=stream, eng=eng):
                    def body(engine_obj):
                        for kind, payload in stream:
                            if kind == "wait":
                                src, val = payload
                                sem = (self.dma_sems[src[1]] if isinstance(src, tuple)
                                       else self.sems[src])
                                engine_obj.wait_ge(sem, val)
                            elif kind == "dma":
                                fn, sem, tag = payload
                                fn(engine_obj).then_inc(sem, 16)
                            else:
                                fn, e, myval, tag = payload
                                fn(engine_obj).then_inc(self.sems[e], 1)
                    return body

                section(make_body())


"""Megakernel stage library for nn_mk_prb_d2 (per-core program builder).

Layout conventions:
- activations channel-major in HBM scratch: [BL, C, T] (BL=2 local batches)
- host passes x pre-transposed to [BL, 1024, T]; output written [BL, 1024, T]
- per-channel vectors host-reshaped to [128, n_tiles] (column j = tile j)
- weights passed in natural [K, M] (= lhsT-ready) layout except fp4 ffn
  weights which are quantized+transposed on device.
"""
import numpy as np
import concourse.bass as bass
import concourse.mybir as mybir

F32 = mybir.dt.float32
BF16 = mybir.dt.bfloat16
AF = mybir.ActivationFunctionType
ALU = mybir.AluOpType
AX = mybir.AxisListType

BL = 2
T = 4096
D = 1024
NT = 8
FFI = 2816
NFF = 22
CH = 512
NCH = T // CH


def setcfg(t=4096, bl=2):
    global T, BL, NCH
    T, BL, NCH = t, bl, t // CH
MAGIC = float(np.float32(1.5 * 2 ** 23))
EPS_LN = 1e-5


def ceil_div(a, b):
    return (a + b - 1) // b


class Arena:
    def __init__(self, ctx, tile, name, base, cols):
        self.ctx, self.t, self.name = ctx, tile, name
        self.base, self.cols = base, cols
        self.cur = base
        self.entries = []
        self.kid = 0

    def alloc(self, elems, dtype=F32, tag=""):
        dsz = 2 if dtype == BF16 else 4
        fcols = ceil_div(elems * dsz, 4)
        start = (self.cur + 3) & ~3
        end = start + fcols
        assert end <= self.base + self.cols, \
            f"arena {self.name} overflow: {end - self.base} > {self.cols} ({tag})"
        key = f"{self.name}{self.kid}_{tag}"
        self.kid += 1
        olap = [e for e in self.entries if e[0] < end and e[1] > start]
        if olap:
            self.ctx.merge_keys(key, [k for (_, _, k) in olap])
        keep = []
        for (s, e, k) in self.entries:
            if s < end and e > start:
                if s < start:
                    keep.append((s, start, k))
                if e > end:
                    keep.append((end, e, k))
            else:
                keep.append((s, e, k))
        keep.append((start, end, key))
        self.entries = keep
        self.cur = end
        ap = self.t[:, start:end]
        if dtype == BF16:
            ap = ap.bitcast(BF16)
        ap = ap[:, :elems]
        return ap, key

    def reset(self):
        self.cur = self.base


class MK:
    def __init__(self, nc, externalize=(), feed=()):
        self.nc = nc
        self.feed = set(feed)
        self.ctx = Ctx(nc)
        def merge_keys(new, olds):
            b = self.ctx._buf(new)
            rs = list(b["r"])
            for k in olds:
                ob = self.ctx._buf(k)
                if ob["w"] is not None:
                    rs.append(ob["w"])
                rs.extend(ob["r"])
                rs.extend(ob["ww"])
            b["w"] = None
            b["r"] = rs
        self.ctx.merge_keys = merge_keys
        self.externalize = set(externalize)
        self.d = {}
        POOL = 50000
        self.pool = nc.alloc_sbuf_tensor("pool", [128, POOL], F32)
        self.P = Arena(self.ctx, self.pool, "P", 0, 2000)
        self.A = Arena(self.ctx, self.pool, "A", 2000, POOL - 2000)
        self.ps = [nc.alloc_psum_tensor(f"ps{i}", [128, 512], F32) for i in range(8)]
        self.ps_keys = [f"ps{i}" for i in range(8)]
        self.ps_rr = 0
        self.strip = {}
        self.cvals = {}
        self.cpool = None

    def dram_in(self, name, shape, dtype=F32):
        ap = self.nc.dram_tensor(name, list(shape), dtype, kind="ExternalInput").ap()
        self.d[name] = ap
        return ap

    def dram_scratch(self, name, shape, dtype=F32):
        kind = ("ExternalInput" if name in self.feed else
                "ExternalOutput" if name in self.externalize else "Internal")
        ap = self.nc.dram_tensor(name, list(shape), dtype, kind=kind).ap()
        self.d[name] = ap
        return ap

    def dram_out(self, name, shape, dtype=F32):
        ap = self.nc.dram_tensor(name, list(shape), dtype, kind="ExternalOutput").ap()
        self.d[name] = ap
        return ap

    def psum(self):
        i = self.ps_rr
        self.ps_rr = (self.ps_rr + 1) % 8
        return self.ps[i], self.ps_keys[i]

    def setup_consts(self, values):
        vals = sorted(set(float(v) for v in values))
        self.cpool, ck = self.P.alloc(len(vals), F32, "cpool")
        for j, v in enumerate(vals):
            self.cvals[v] = j
            self.nc.const_aps.register(F32, v, self.cpool[:, j:j + 1])
            v32 = float(np.float32(v))
            if v32 != v:
                self.nc.const_aps.register(F32, v32, self.cpool[:, j:j + 1])
            self.ctx.op("V", lambda e, j=j, v=v: e.memset(self.cpool[:, j:j + 1], v),
                        writes=["cpool"])
        self.ck = "cpool"

    def o(self, eng, fn, reads=(), writes=(), tag=""):
        return self.ctx.op(eng, fn, reads=reads, writes=writes, tag=tag)

    def load_strip(self, name, dname, cols, cast_bf16=False):
        ap, key = self.P.alloc(cols, F32, name)
        rows = self.d[dname].shape[0]
        self.ctx.dma(ap[:rows, :], self.d[dname], writes=[key])
        self.strip[name] = (ap, key)
        return ap, key

    def mm_group(self, ps_ap, ps_key, pairs, reads, tag=""):
        def fn(e):
            last = None
            n = len(pairs)
            for i, (lt, rh) in enumerate(pairs):
                last = e.matmul(ps_ap, lt, rh, start=(i == 0), stop=(i == n - 1))
            return last
        self.o("T", fn, reads=reads, writes=[ps_key], tag=tag)

    def load_w_lhsT_bf16(self, dram_name, K, M, tag, stg_n=2):
        ntk = ceil_div(K, 128)
        stg = [self.A.alloc(M, F32, f"{tag}wstg{i}") for i in range(stg_n)]
        out = []
        for k in range(ntk):
            kk = min(128, K - k * 128)
            f, fk = stg[k % stg_n]
            b, bk = self.A.alloc(M, BF16, f"{tag}wb{k}")
            self.ctx.dma(f[:kk, :], self.d[dram_name][k * 128:k * 128 + kk, :], writes=[fk])
            self.o("V", lambda e, b=b, f=f, kk=kk: e.tensor_copy(b[:kk, :], f[:kk, :]),
                   reads=[fk], writes=[bk])
            out.append((b, bk, kk))
        return out

    def st_prep(self):
        P = self.P
        self.load_strip("b_in", "L0_attn_b_in", NT)
        self.load_strip("B_vec", "L0_attn_B_vec", NT)
        self.load_strip("log_A", "L0_attn_log_A", NT)
        self.load_strip("Dv", "L0_attn_D", NT)
        a_ap, a_key = P.alloc(NT, F32, "Astrip")
        la, lak = self.strip["log_A"]
        self.o("S", lambda e: e.activation(out=a_ap[:, :], in_=la[:, :], func=AF.Exp),
               reads=[lak], writes=[a_key])
        self.o("V", lambda e: e.tensor_scalar_mul(a_ap[:, :], a_ap[:, :], -1.0),
               reads=[a_key], writes=[a_key])
        self.strip["A"] = (a_ap, a_key)

    def st_ln(self, src, dstf, dstb, tag):
        A = self.A
        A.reset()
        onesw, onesk = A.alloc(2, F32, "ones1")
        self.o("V", lambda e: e.memset(onesw[:, 0:1], 1.0), writes=[onesk])
        xch = [A.alloc(CH, F32, f"xch{i}") for i in range(2)]
        sqch = [A.alloc(CH, F32, f"sqch{i}") for i in range(2)]
        R1, R1k = A.alloc(T, F32, "lnR1")
        R2, R2k = A.alloc(T, F32, "lnR2")
        R3, R3k = A.alloc(T, F32, "lnR3")
        sums, c1 = R1[0:1, :], R3[0:1, :]
        sumsq, c2, msq = R2[0:1, :], R3[0:1, 0:T], R2[0:1, 0:T]
        R4, R4k = A.alloc(T, F32, "lnR4")
        R5, R5k = A.alloc(T, F32, "lnR5")
        c1, c2, msq = R3[0:1, :], R4[0:1, :], R5[0:1, :]
        R1k_c1, R2k_c2, R2k_m = R3k, R4k, R5k
        c1b, c1bk = A.alloc(T, F32, "c1b")
        c2b, c2bk = A.alloc(T, F32, "c2b")
        xfull = [A.alloc(T, F32, f"xf{i}") for i in range(2)]
        xn, xnk = A.alloc(T, F32, "xn")
        xnb, xnbk = A.alloc(T, BF16, "xnb")
        scr_b = self.dram_scratch(f"lnscr_{tag}", [2, T], F32)
        for b in range(BL):
            for c in range(NCH):
                psA, pkA = self.psum()
                psB, pkB = self.psum()
                sl = slice(c * CH, (c + 1) * CH)
                for j in range(NT):
                    xa, xk = xch[j % 2]
                    sa, sk = sqch[j % 2]
                    self.ctx.dma(xa[:, :], self.d[src][b, j * 128:(j + 1) * 128, sl],
                                 writes=[xk], reads=[f"D:{src}"])
                    self.o("S", lambda e, sa=sa, xa=xa: e.activation(
                        out=sa[:, :], in_=xa[:, :], func=AF.Square),
                        reads=[xk], writes=[sk])
                    self.o("T", lambda e, psA=psA, xa=xa, j=j: e.matmul(
                        psA[0:1, :], onesw[:, 0:1], xa[:, :],
                        start=(j == 0), stop=(j == NT - 1)),
                        reads=[onesk, xk], writes=[pkA])
                    self.o("T", lambda e, psB=psB, sa=sa, j=j: e.matmul(
                        psB[0:1, :], onesw[:, 0:1], sa[:, :],
                        start=(j == 0), stop=(j == NT - 1)),
                        reads=[onesk, sk], writes=[pkB])
                self.o("V", lambda e, psA=psA, sl=sl: e.tensor_copy(sums[:, sl], psA[0:1, :]),
                       reads=[pkA], writes=[R1k])
                self.o("V", lambda e, psB=psB, sl=sl: e.tensor_copy(sumsq[:, sl], psB[0:1, :]),
                       reads=[pkB], writes=[R2k])
            self.o("V", lambda e: e.tensor_scalar_mul(sums, sums, 1.0 / D),
                   reads=[R1k], writes=[R1k])
            self.o("S", lambda e: e.activation(out=msq, in_=sums, func=AF.Square),
                   reads=[R1k], writes=[R2k_m])
            self.o("V", lambda e: e.tensor_scalar_mul(sumsq, sumsq, 1.0 / D),
                   reads=[R2k], writes=[R2k])
            self.o("V", lambda e: e.tensor_sub(sumsq, sumsq, msq),
                   reads=[R2k, R2k_m], writes=[R2k])
            self.o("S", lambda e: e.activation(out=sumsq, in_=sumsq, func=AF.Ln, bias=EPS_LN),
                   reads=[R2k, self.ck], writes=[R2k])
            self.o("S", lambda e: e.activation(out=c1, in_=sumsq, func=AF.Exp, scale=-0.5),
                   reads=[R2k], writes=[R1k_c1])
            self.o("V", lambda e: e.scalar_tensor_tensor(out=c2, in0=sums, scalar=-1.0,
                                                         in1=c1, op0=ALU.mult, op1=ALU.mult),
                   reads=[R1k, R1k_c1], writes=[R2k_c2])
            self.ctx.dma(scr_b[0:1, :], c1, reads=[R1k_c1], writes=["lnscr" + tag])
            self.ctx.dma(scr_b[1:2, :], c2, reads=[R2k_c2], writes=["lnscr" + tag])
            self.ctx.dma(c1b[:, :], scr_b[0:1, :].broadcast_to([128, T]),
                         reads=["lnscr" + tag], writes=[c1bk], eng="P")
            self.ctx.dma(c2b[:, :], scr_b[1:2, :].broadcast_to([128, T]),
                         reads=["lnscr" + tag], writes=[c2bk], eng="P")
            for j in range(NT):
                xa, xk = xfull[j % 2]
                self.ctx.dma(xa[:, :], self.d[src][b, j * 128:(j + 1) * 128, :],
                             writes=[xk], reads=[f"D:{src}"])
                self.o("V", lambda e, xa=xa: e.tensor_mul(xn[:, :], xa[:, :], c1b[:, :]),
                       reads=[xk, c1bk], writes=[xnk])
                self.o("V", lambda e: e.tensor_add(xn[:, :], xn[:, :], c2b[:, :]),
                       reads=[xnk, c2bk], writes=[xnk])
                if dstb is not None:
                    self.o("V", lambda e: e.tensor_copy(xnb[:, :], xn[:, :]),
                           reads=[xnk], writes=[xnbk])
                    self.ctx.dma(self.d[dstb][b, j * 128:(j + 1) * 128, :], xnb[:, :],
                                 reads=[xnbk], writes=[f"D:{dstb}"])
                if dstf is not None:
                    self.ctx.dma(self.d[dstf][b, j * 128:(j + 1) * 128, :], xn[:, :],
                                 reads=[xnk], writes=[f"D:{dstf}"])

    def st_l0scan(self):
        A = self.A
        A.reset()
        w = self.load_w_lhsT_bf16("L0_attn_W_in", D, D, "win")
        xnb = []
        for j in range(NT):
            t_ap, t_key = A.alloc(T, BF16, f"xnb{j}")
            xnb.append((t_ap, t_key))
        ub, ubk = A.alloc(T, F32, "ub")
        hb, hbk = A.alloc(T, BF16, "hb")
        b_in, b_in_k = self.strip["b_in"]
        Bv, Bv_k = self.strip["B_vec"]
        Ast, Ast_k = self.strip["A"]
        for b in range(BL):
            for j in range(NT):
                t_ap, t_key = xnb[j]
                self.ctx.dma(t_ap[:, :], self.d["xn0b"][b, j * 128:(j + 1) * 128, :],
                             writes=[t_key], reads=["D:xn0b"])
            for j in range(NT):
                for c in range(NCH):
                    ps, pk = self.psum()
                    sl = slice(c * CH, (c + 1) * CH)
                    self.mm_group(ps[:, :], pk,
                                  [(w[k][0][:, j * 128:(j + 1) * 128], xnb[k][0][:, sl])
                                   for k in range(NT)],
                                  reads=[w[k][1] for k in range(NT)] + [xnb[k][1] for k in range(NT)],
                                  tag=f"u{b}{j}{c}")
                    self.o("V", lambda e, ps=ps, sl=sl, j=j: e.tensor_scalar(
                        out=ub[:, sl], in0=ps[:, :], scalar1=b_in[:, j:j + 1],
                        scalar2=Bv[:, j:j + 1], op0=ALU.add, op1=ALU.mult),
                        reads=[pk, b_in_k, Bv_k], writes=[ubk])
                self.o("V", lambda e, j=j: e.tensor_tensor_scan(
                    out=hb[:, :], data0=Ast[:, j:j + 1].broadcast_to([128, T]),
                    data1=ub[:, :], initial=0.0, op0=ALU.mult, op1=ALU.add),
                    reads=[ubk, Ast_k], writes=[hbk])
                self.ctx.dma(self.d["h0b"][b, j * 128:(j + 1) * 128, :], hb[:, :],
                             reads=[hbk], writes=["D:h0b"])

    def st_l0y(self):
        A = self.A
        A.reset()
        w = self.load_w_lhsT_bf16("L0_attn_C", D, D, "C")
        hbt = []
        for j in range(NT):
            t_ap, t_key = A.alloc(T, BF16, f"hb{j}")
            hbt.append((t_ap, t_key))
        xnbt, xnbk = A.alloc(T, BF16, "xnbt")
        xf, xfk = A.alloc(T, F32, "xf")
        acc, acck = A.alloc(T, F32, "acc")
        Dv, Dv_k = self.strip["Dv"]
        for b in range(BL):
            for j in range(NT):
                t_ap, t_key = hbt[j]
                self.ctx.dma(t_ap[:, :], self.d["h0b"][b, j * 128:(j + 1) * 128, :],
                             writes=[t_key], reads=["D:h0b"])
            for j in range(NT):
                self.ctx.dma(xnbt[:, :], self.d["xn0b"][b, j * 128:(j + 1) * 128, :],
                             writes=[xnbk], reads=["D:xn0b"])
                self.ctx.dma(xf[:, :], self.d["x"][b, j * 128:(j + 1) * 128, :],
                             writes=[xfk], reads=["D:x"])
                for c in range(NCH):
                    ps, pk = self.psum()
                    sl = slice(c * CH, (c + 1) * CH)
                    self.mm_group(ps[:, :], pk,
                                  [(w[k][0][:, j * 128:(j + 1) * 128], hbt[k][0][:, sl])
                                   for k in range(NT)],
                                  reads=[w[k][1] for k in range(NT)] + [hbt[k][1] for k in range(NT)],
                                  tag=f"y{b}{j}{c}")
                    self.o("V", lambda e, ps=ps, sl=sl, j=j: e.scalar_tensor_tensor(
                        out=acc[:, sl], in0=xnbt[:, sl], scalar=Dv[:, j:j + 1],
                        in1=ps[:, :], op0=ALU.mult, op1=ALU.add),
                        reads=[pk, xnbk, Dv_k], writes=[acck])
                self.o("V", lambda e: e.tensor_add(acc[:, :], acc[:, :], xf[:, :]),
                       reads=[acck, xfk], writes=[acck])
                self.ctx.dma(self.d["a0"][b, j * 128:(j + 1) * 128, :], acc[:, :],
                             reads=[acck], writes=["D:a0"])

    def emit_qdq(self, xT, xT_key, out_bf, out_key, Cfree, scr, pool_ops=False):
        nb = Cfree // 16
        x3 = xT[:, :Cfree].rearrange("p (b k) -> p b k", k=16)
        (bm, bmk), (sc, sck), (inv, invk), (t, tk), (a, ak), (sg, sgk), \
            (r1, r1k), (r2, r2k) = scr
        o = self.o
        o("V", lambda e: e.tensor_reduce(out=bm[:, :nb], in_=x3, axis=AX.X,
                                         op=ALU.max, apply_absolute_value=True),
          reads=[xT_key], writes=[bmk])
        o("V", lambda e: e.tensor_scalar(out=sc[:, :nb], in0=bm[:, :nb], scalar1=1.0 / 6.0,
                                         scalar2=1e-12, op0=ALU.mult, op1=ALU.max),
          reads=[bmk], writes=[sck])
        o("V", lambda e: e.reciprocal(inv[:, :nb], sc[:, :nb]), reads=[sck], writes=[invk])
        invr = inv[:, :nb].unsqueeze(2).broadcast_to([128, nb, 16])
        scr_r = sc[:, :nb].unsqueeze(2).broadcast_to([128, nb, 16])
        t3 = t[:, :Cfree].rearrange("p (b k) -> p b k", k=16)
        E1 = "P" if pool_ops else "V"
        o(E1, lambda e: e.tensor_tensor(out=t3, in0=x3, in1=invr, op=ALU.mult),
          reads=[xT_key, invk], writes=[tk])
        o("S", lambda e: e.activation(out=a[:, :Cfree], in_=t[:, :Cfree], func=AF.Abs),
          reads=[tk], writes=[ak])
        o("S", lambda e: e.activation(out=sg[:, :Cfree], in_=t[:, :Cfree], func=AF.Sign),
          reads=[tk], writes=[sgk])
        o("V", lambda e: e.tensor_scalar(out=r1[:, :Cfree], in0=a[:, :Cfree], scalar1=0.5,
                                         scalar2=4.0, op0=ALU.mult, op1=ALU.add),
          reads=[ak], writes=[r1k])
        o("V", lambda e: e.scalar_tensor_tensor(out=r1[:, :Cfree], in0=a[:, :Cfree], scalar=2.0,
                                                in1=r1[:, :Cfree], op0=ALU.mult, op1=ALU.min),
          reads=[ak, r1k], writes=[r1k])
        o("V", lambda e: e.scalar_tensor_tensor(out=r1[:, :Cfree], in0=a[:, :Cfree], scalar=2.0,
                                                in1=r1[:, :Cfree], op0=ALU.add, op1=ALU.min),
          reads=[ak, r1k], writes=[r1k])
        o("V", lambda e: e.tensor_scalar(out=r1[:, :Cfree], in0=r1[:, :Cfree], scalar1=MAGIC,
                                         scalar2=MAGIC, op0=ALU.add, op1=ALU.subtract),
          reads=[r1k], writes=[r1k])
        o("V", lambda e: e.tensor_scalar(out=r2[:, :Cfree], in0=r1[:, :Cfree], scalar1=0.5,
                                         scalar2=None, op0=ALU.mult),
          reads=[r1k], writes=[r2k])
        o("V", lambda e: e.scalar_tensor_tensor(out=r2[:, :Cfree], in0=r1[:, :Cfree], scalar=-2.0,
                                                in1=r2[:, :Cfree], op0=ALU.add, op1=ALU.max),
          reads=[r1k, r2k], writes=[r2k])
        o("V", lambda e: e.tensor_scalar(out=r1[:, :Cfree], in0=r1[:, :Cfree], scalar1=2.0,
                                         scalar2=8.0, op0=ALU.mult, op1=ALU.subtract),
          reads=[r1k], writes=[r1k])
        o("V", lambda e: e.tensor_max(r2[:, :Cfree], r2[:, :Cfree], r1[:, :Cfree]),
          reads=[r1k, r2k], writes=[r2k])
        o(E1, lambda e: e.tensor_mul(r2[:, :Cfree], r2[:, :Cfree], sg[:, :Cfree]),
          reads=[r2k, sgk], writes=[r2k])
        r23 = r2[:, :Cfree].rearrange("p (b k) -> p b k", k=16)
        ob3 = out_bf[:, :Cfree].rearrange("p (b k) -> p b k", k=16)
        o(E1, lambda e: e.tensor_tensor(out=ob3, in0=r23, in1=scr_r, op=ALU.mult),
          reads=[r2k, sck], writes=[out_key])

    def qdq_scratch(self, Cfree):
        A = self.A
        nb = Cfree // 16
        return [A.alloc(nb, F32, "bm"), A.alloc(nb, F32, "sc"), A.alloc(nb, F32, "inv"),
                A.alloc(Cfree, F32, "t"), A.alloc(Cfree, F32, "a"), A.alloc(Cfree, F32, "sg"),
                A.alloc(Cfree, F32, "r1"), A.alloc(Cfree, F32, "r2")]

    def st_quant_act(self, src, dst, C):
        A = self.A
        A.reset()
        ntc = C // 128
        idb, idbk = self.ident_bf16()
        sets = []
        for i in range(2):
            xin = A.alloc(128 * ntc, F32, f"qin{i}")
            xT = A.alloc(C, F32, f"xT{i}")
            qT = A.alloc(C, BF16, f"qT{i}")
            qcm = A.alloc(128 * ntc, BF16, f"qcm{i}")
            scr = self.qdq_scratch(C)
            sets.append((xin, xT, qT, qcm, scr))
        it = 0
        for b in range(BL):
            for tc in range(T // 128):
                (xin, xink), (xT, xTk), (qT, qTk), (qcm, qcmk), scr = sets[it % 2]
                it += 1
                tsl = slice(tc * 128, (tc + 1) * 128)
                for cj in range(ntc):
                    self.ctx.dma(xin[:, cj * 128:(cj + 1) * 128],
                                 self.d[src][b, cj * 128:(cj + 1) * 128, tsl],
                                 writes=[xink], reads=[f"D:{src}"])
                for cj in range(ntc):
                    ps, pk = self.psum()
                    self.o("T", lambda e, ps=ps, cj=cj, xin=xin: e.transpose(
                        ps[:, 0:128], xin[:, cj * 128:(cj + 1) * 128], self.ident_f32()[0]),
                        reads=[xink, self.ident_f32()[1]], writes=[pk])
                    self.o("V", lambda e, ps=ps, cj=cj, xT=xT: e.tensor_copy(
                        xT[:, cj * 128:(cj + 1) * 128], ps[:, 0:128]),
                        reads=[pk], writes=[xTk])
                self.emit_qdq(xT, xTk, qT, qTk, C, scr, pool_ops=True)
                for cj in range(ntc):
                    ps, pk = self.psum()
                    psb = ps[:, :].bitcast(BF16)
                    self.o("T", lambda e, psb=psb, cj=cj, qT=qT: e.transpose(
                        psb[:, 0:128], qT[:, cj * 128:(cj + 1) * 128], idb[:, :]),
                        reads=[qTk, idbk], writes=[pk])
                    self.o("V", lambda e, psb=psb, cj=cj, qcm=qcm: e.tensor_copy(
                        qcm[:, cj * 128:(cj + 1) * 128], psb[:, 0:128]),
                        reads=[pk], writes=[qcmk])
                for cj in range(ntc):
                    self.ctx.dma(self.d[dst][b, cj * 128:(cj + 1) * 128, tsl],
                                 qcm[:, cj * 128:(cj + 1) * 128],
                                 reads=[qcmk], writes=[f"D:{dst}"])

    def ident_f32(self):
        if "identf" not in self.strip:
            ap, key = self.P.alloc(128, F32, "identf")
            self.ctx.dma(ap[:, :], self.d["c_ident"], writes=[key])
            self.strip["identf"] = (ap, key)
        return self.strip["identf"]

    def ident_bf16(self):
        if "identb" not in self.strip:
            f, fk = self.ident_f32()
            ap, key = self.P.alloc(128, BF16, "identb")
            self.o("V", lambda e: e.tensor_copy(ap[:, :], f[:, :]), reads=[fk], writes=[key])
            self.strip["identb"] = (ap, key)
        return self.strip["identb"]

    def st_quant_w(self, src, dst, R, C):
        A = self.A
        A.reset()
        idb, idbk = self.ident_bf16()
        ntr, ntc = R // 128, C // 128
        sets = [(A.alloc(C, F32, f"wrow{i}"), A.alloc(C, BF16, f"qrow{i}"),
                 A.alloc(128, BF16, f"tout{i}"), self.qdq_scratch(C)) for i in range(2)]
        for r in range(ntr):
            (wrow, wrowk), (qrow, qrowk), (tout, toutk), scr = sets[r % 2]
            self.ctx.dma(wrow[:, :], self.d[src][r * 128:(r + 1) * 128, :],
                         writes=[wrowk], reads=[f"D:{src}"])
            self.emit_qdq(wrow, wrowk, qrow, qrowk, C, scr, pool_ops=True)
            for cj in range(ntc):
                ps, pk = self.psum()
                psb = ps[:, :].bitcast(BF16)
                self.o("T", lambda e, psb=psb, cj=cj, qrow=qrow: e.transpose(
                    psb[:, 0:128], qrow[:, cj * 128:(cj + 1) * 128], idb[:, :]),
                    reads=[qrowk, idbk], writes=[pk])
                self.o("V", lambda e, psb=psb, tout=tout: e.tensor_copy(tout[:, :], psb[:, 0:128]),
                       reads=[pk], writes=[toutk])
                self.ctx.dma(self.d[dst][cj * 128:(cj + 1) * 128, r * 128:(r + 1) * 128],
                             tout[:, :], reads=[toutk], writes=[f"D:{dst}"])

    def st_gateup(self):
        A = self.A
        A.reset()
        wg = []
        for k in range(NT):
            ap, key = A.alloc(FFI, BF16, f"wg{k}")
            self.ctx.dma(ap[:, :], self.d["wgq"][k * 128:(k + 1) * 128, :],
                         writes=[key], reads=["D:wgq"])
            wg.append((ap, key))
        wu = []
        for k in range(NT):
            ap, key = A.alloc(FFI, BF16, f"wu{k}")
            self.ctx.dma(ap[:, :], self.d["wuq"][k * 128:(k + 1) * 128, :],
                         writes=[key], reads=["D:wuq"])
            wu.append((ap, key))
        xq = []
        for j in range(NT):
            ap, key = A.alloc(T, BF16, f"xq{j}")
            xq.append((ap, key))
        sg, sgk = A.alloc(CH, F32, "sg")
        hm, hmk = A.alloc(T, F32, "hm")
        for b in range(BL):
            for j in range(NT):
                ap, key = xq[j]
                self.ctx.dma(ap[:, :], self.d["xq0"][b, j * 128:(j + 1) * 128, :],
                             writes=[key], reads=["D:xq0"])
            for m in range(NFF):
                msl = slice(m * 128, (m + 1) * 128)
                for c in range(NCH):
                    sl = slice(c * CH, (c + 1) * CH)
                    psG, pkG = self.psum()
                    psU, pkU = self.psum()
                    self.mm_group(psG[:, :], pkG,
                                  [(wg[k][0][:, msl], xq[k][0][:, sl]) for k in range(NT)],
                                  reads=[wg[k][1] for k in range(NT)] + [xq[k][1] for k in range(NT)])
                    self.mm_group(psU[:, :], pkU,
                                  [(wu[k][0][:, msl], xq[k][0][:, sl]) for k in range(NT)],
                                  reads=[wu[k][1] for k in range(NT)] + [xq[k][1] for k in range(NT)])
                    self.o("S", lambda e, psG=psG: e.activation(out=sg[:, :], in_=psG[:, :],
                                                                func=AF.Silu),
                           reads=[pkG], writes=[sgk])
                    self.o("V", lambda e, psU=psU, sl=sl: e.tensor_mul(hm[:, sl], psU[:, :], sg[:, :]),
                           reads=[pkU, sgk], writes=[hmk])
                self.ctx.dma(self.d["h0f"][b, m * 128:(m + 1) * 128, :], hm[:, :],
                             reads=[hmk], writes=["D:h0f"])

    def st_down(self):
        A = self.A
        A.reset()
        wd = []
        for k in range(NFF):
            ap, key = A.alloc(D, BF16, f"wd{k}")
            self.ctx.dma(ap[:, :], self.d["wdq"][k * 128:(k + 1) * 128, :],
                         writes=[key], reads=["D:wdq"])
            wd.append((ap, key))
        hq = []
        for k in range(NFF):
            ap, key = A.alloc(CH, BF16, f"hq{k}")
            hq.append((ap, key))
        a0t, a0k = A.alloc(CH, F32, "a0t")
        x1t, x1k = A.alloc(CH, F32, "x1t")
        for b in range(BL):
            for c in range(NCH):
                sl = slice(c * CH, (c + 1) * CH)
                for k in range(NFF):
                    ap, key = hq[k]
                    self.ctx.dma(ap[:, :], self.d["hq0"][b, k * 128:(k + 1) * 128, sl],
                                 writes=[key], reads=["D:hq0"])
                for m in range(NT):
                    msl = slice(m * 128, (m + 1) * 128)
                    ps, pk = self.psum()
                    self.mm_group(ps[:, :], pk,
                                  [(wd[k][0][:, msl], hq[k][0][:, :]) for k in range(NFF)],
                                  reads=[wd[k][1] for k in range(NFF)] + [hq[k][1] for k in range(NFF)])
                    self.ctx.dma(a0t[:, :], self.d["a0"][b, msl, sl],
                                 writes=[a0k], reads=["D:a0"])
                    self.o("V", lambda e, ps=ps: e.tensor_add(x1t[:, :], ps[:, :], a0t[:, :]),
                           reads=[pk, a0k], writes=[x1k])
                    self.ctx.dma(self.d["x1"][b, msl, sl], x1t[:, :],
                                 reads=[x1k], writes=["D:x1"])



B0_STAGES = ["prep", "wq", "ln0", "xq", "scan", "y", "gateup", "hq", "down"]


def declare_block0(mk):
    mk.dram_in("x", [BL, D, T])
    mk.dram_in("c_ident", [128, 128])
    mk.dram_in("L0_attn_W_in", [D, D])
    mk.dram_in("L0_attn_C", [D, D])
    for nm in ["L0_attn_b_in", "L0_attn_B_vec", "L0_attn_log_A", "L0_attn_D"]:
        mk.dram_in(nm, [128, NT])
    mk.dram_in("L0_ffn_w_gate", [FFI, D])
    mk.dram_in("L0_ffn_w_up", [FFI, D])
    mk.dram_in("L0_ffn_w_down", [D, FFI])
    mk.dram_scratch("xn0f", [BL, D, T])
    mk.dram_scratch("xn0b", [BL, D, T], BF16)
    mk.dram_scratch("h0b", [BL, D, T], BF16)
    mk.dram_scratch("a0", [BL, D, T])
    mk.dram_scratch("xq0", [BL, D, T], BF16)
    mk.dram_scratch("wgq", [D, FFI], BF16)
    mk.dram_scratch("wuq", [D, FFI], BF16)
    mk.dram_scratch("wdq", [FFI, D], BF16)
    mk.dram_scratch("h0f", [BL, FFI, T])
    mk.dram_scratch("hq0", [BL, FFI, T], BF16)
    mk.dram_scratch("x1", [BL, D, T])


def emit_block0(mk, upto=None, consts=True):
    stages = B0_STAGES if upto is None else B0_STAGES[:B0_STAGES.index(upto) + 1]
    if consts:
        mk.setup_consts([EPS_LN])
    for s in stages:
        if s == "prep":
            mk.st_prep()
        elif s == "wq":
            mk.st_quant_w("L0_ffn_w_gate", "wgq", FFI, D)
            mk.st_quant_w("L0_ffn_w_up", "wuq", FFI, D)
            mk.st_quant_w("L0_ffn_w_down", "wdq", D, FFI)
        elif s == "ln0":
            mk.st_ln("x", "xn0f", "xn0b", "ln0")
        elif s == "xq":
            mk.st_quant_act("xn0f", "xq0", D)
        elif s == "scan":
            mk.st_l0scan()
        elif s == "y":
            mk.st_l0y()
        elif s == "gateup":
            mk.st_gateup()
        elif s == "hq":
            mk.st_quant_act("h0f", "hq0", FFI)
        elif s == "down":
            mk.st_down()


def vec_strip(v, n_tiles):
    v = np.asarray(v, np.float32).ravel()
    out = np.zeros((128, n_tiles), np.float32)
    full = v.reshape(-1)
    for j in range(n_tiles):
        seg = full[j * 128:(j + 1) * 128]
        out[:len(seg), j] = seg
    return np.ascontiguousarray(out)


def host_prep_block0(inp, b_sl, Tcur):
    m = {
        "c_ident": np.eye(128, dtype=np.float32),
        "L0_attn_W_in": np.asarray(inp["L0_attn_W_in"]),
        "L0_attn_C": np.asarray(inp["L0_attn_C"]),
        "L0_attn_b_in": vec_strip(inp["L0_attn_b_in"], NT),
        "L0_attn_B_vec": vec_strip(inp["L0_attn_B_vec"], NT),
        "L0_attn_log_A": vec_strip(inp["L0_attn_log_A"], NT),
        "L0_attn_D": vec_strip(inp["L0_attn_D"], NT),
        "L0_ffn_w_gate": np.asarray(inp["L0_ffn_w_gate"]),
        "L0_ffn_w_up": np.asarray(inp["L0_ffn_w_up"]),
        "L0_ffn_w_down": np.asarray(inp["L0_ffn_w_down"]),
    }
    return {k: np.ascontiguousarray(np.asarray(v, np.float32)) for k, v in m.items()}


DI = 2048
NDI = 16
CPROJ = 4128
NPROJ = 33
NHEAD = 32
PI = float(np.pi)
INV2PI = float(np.float32(1.0 / (2 * np.pi)))
TWOPI = float(np.float32(2 * np.pi))


class MK1:
    pass


def _mixin(cls):
    for k, v in list(cls.__dict__.items()):
        if not k.startswith("__"):
            setattr(MK, k, v)
    return cls


@_mixin
class _Block1Attn:
    def st_l1proj(self):
        A = self.A
        A.reset()
        w = self.load_w_lhsT_bf16("L1_attn_in_proj_w", D, CPROJ, "iproj", stg_n=1)
        xnb = []
        for j in range(NT):
            ap, key = A.alloc(T, BF16, f"x1nb{j}")
            xnb.append((ap, key))
        zbuf, zbk = A.alloc(T + 3, F32, "zbuf")
        acc, acck = A.alloc(T, F32, "cacc")
        ob, obk = A.alloc(T, BF16, "convb")
        ck = [self.strip[f"conv_k{i}"] for i in range(4)]
        cb, cbk = self.strip["conv_b"]
        for b in range(BL):
            for j in range(NT):
                ap, key = xnb[j]
                self.ctx.dma(ap[:, :], self.d["xn1b"][b, j * 128:(j + 1) * 128, :],
                             writes=[key], reads=["D:xn1b"])
            for m in range(NPROJ):
                mm = min(128, CPROJ - m * 128)
                msl = slice(m * 128, m * 128 + mm)
                self.o("V", lambda e: e.memset(zbuf[:, 0:3], 0.0), writes=[zbk])
                for c in range(NCH):
                    ps, pk = self.psum()
                    self.mm_group(ps[:mm, :], pk,
                                  [(w[k][0][:, msl], xnb[k][0][:, c * CH:(c + 1) * CH])
                                   for k in range(NT)],
                                  reads=[w[k][1] for k in range(NT)] + [x[1] for x in xnb])
                    self.o("V", lambda e, ps=ps, c=c, mm=mm: e.tensor_copy(
                        zbuf[:mm, 3 + c * CH:3 + (c + 1) * CH], ps[:mm, :]),
                        reads=[pk], writes=[zbk])
                self.o("V", lambda e, m=m, mm=mm: e.tensor_scalar(
                    out=acc[:mm, :], in0=zbuf[:mm, 0:T], scalar1=ck[0][0][:mm, m:m + 1],
                    scalar2=cb[:mm, m:m + 1], op0=ALU.mult, op1=ALU.add),
                    reads=[zbk, ck[0][1], cbk], writes=[acck])
                for i in (1, 2, 3):
                    self.o("V", lambda e, i=i, m=m, mm=mm: e.scalar_tensor_tensor(
                        out=acc[:mm, :], in0=zbuf[:mm, i:i + T], scalar=ck[i][0][:mm, m:m + 1],
                        in1=acc[:mm, :], op0=ALU.mult, op1=ALU.add),
                        reads=[zbk, ck[i][1], acck], writes=[acck])
                if m < NDI:
                    self.o("V", lambda e: e.tensor_copy(ob[:, :], acc[:, :]),
                           reads=[acck], writes=[obk])
                    self.ctx.dma(self.d["xval"][b, m * 128:(m + 1) * 128, :], ob[:, :],
                                 reads=[obk], writes=["D:xval"])
                elif m < 2 * NDI:
                    self.o("S", lambda e: e.activation(out=ob[:, :], in_=acc[:, :],
                                                       func=AF.Silu),
                           reads=[acck], writes=[obk])
                    self.ctx.dma(self.d["xgate"][b, (m - NDI) * 128:(m - NDI + 1) * 128, :],
                                 ob[:, :], reads=[obk], writes=["D:xgate"])
                else:
                    self.ctx.dma(self.d["sraw"][b, :, :], acc[0:32, :],
                                 reads=[acck], writes=["D:sraw"])

    def _softplus_strip(self, dst, dst_k, src, src_k, n):
        t1, t1k = self.A.alloc(4, F32, "spt1")
        t2, t2k = self.A.alloc(4, F32, "spt2")
        self.o("S", lambda e: e.activation(out=t1[:n, 0:1], in_=src[:n, 0:1], func=AF.Abs),
               reads=[src_k], writes=[t1k])
        self.o("S", lambda e: e.activation(out=t1[:n, 0:1], in_=t1[:n, 0:1], func=AF.Exp,
                                           scale=-1.0), reads=[t1k], writes=[t1k])
        self.o("S", lambda e: e.activation(out=t1[:n, 0:1], in_=t1[:n, 0:1], func=AF.Ln,
                                           bias=1.0), reads=[t1k, self.ck], writes=[t1k])
        self.o("V", lambda e: e.tensor_scalar_max(t2[:n, 0:1], src[:n, 0:1], 0.0),
               reads=[src_k], writes=[t2k])
        self.o("V", lambda e: e.tensor_add(dst[:n, 0:1], t1[:n, 0:1], t2[:n, 0:1]),
               reads=[t1k, t2k], writes=[dst_k])

    def st_l1mixA(self):
        A = self.A
        A.reset()
        dec, deck = self.strip["decay_s"]
        anc, anck = self.strip["anchor_s"]
        ssc, ssck = self.strip["score_ss"]
        nsp, nspk = A.alloc(4, F32, "negsp")
        spd, spdk = A.alloc(4, F32, "spd")
        spa, spak = A.alloc(4, F32, "spa")
        self._softplus_strip(spd, spdk, dec, deck, 28)
        self._softplus_strip(spa, spak, anc, anck, 4)
        self.o("V", lambda e: e.tensor_scalar_mul(nsp[0:28, 0:1], spd[0:28, 0:1], -1.0),
               reads=[spdk], writes=[nspk])
        self.o("V", lambda e: e.tensor_scalar_mul(spa[0:4, 0:1], spa[0:4, 0:1], -1.0),
               reads=[spak], writes=[spak])
        nspscr = self.dram_scratch("nspscr", [4, 1], F32)
        self.ctx.dma(nspscr[0:4, :], spa[0:4, 0:1], reads=[spak], writes=["D:nspscr"])
        self.ctx.dma(nsp[28:32, 0:1], nspscr[0:4, :], reads=["D:nspscr"], writes=[nspk], eng="P")
        dp, dpk = A.alloc(T, F32, "distpos")
        self.ctx.dma(dp[0:28, :], self.d["c_dist"][0:1, :].broadcast_to([28, T]),
                     reads=["D:c_dist"], writes=[dpk])
        self.ctx.dma(dp[28:32, :], self.d["c_pos"][0:1, :].broadcast_to([4, T]),
                     reads=["D:c_pos"], writes=[dpk])
        targ, targk = A.alloc(T, F32, "targ")
        tw, twk = A.alloc(T, F32, "tw")
        sr, srk = A.alloc(T, F32, "srA")
        pw, pwk = A.alloc(T, F32, "pwA")
        z32, z32k = A.alloc(T, F32, "z32")
        den, denk = A.alloc(T, F32, "denA")
        self.o("V", lambda e: e.memset(z32[0:32, :], 0.0), writes=[z32k])
        self.o("V", lambda e: e.tensor_scalar_mul(targ[0:32, :], dp[0:32, :], nsp[0:32, 0:1]),
               reads=[dpk, nspk], writes=[targk])
        self.o("S", lambda e: e.activation(out=tw[0:32, :], in_=targ[0:32, :], func=AF.Exp),
               reads=[targk], writes=[twk])
        for b in range(BL):
            self.ctx.dma(sr[0:32, :], self.d["sraw"][b, :, :], writes=[srk], reads=["D:sraw"])
            self.o("V", lambda e: e.tensor_scalar(out=pw[0:32, :], in0=sr[0:32, :],
                                                  scalar1=ssc[0:32, 0:1], scalar2=20.0,
                                                  op0=ALU.mult, op1=ALU.min),
                   reads=[srk, ssck], writes=[pwk])
            self.o("V", lambda e: e.tensor_scalar_max(pw[0:32, :], pw[0:32, :], -20.0),
                   reads=[pwk], writes=[pwk])
            self.o("S", lambda e: e.activation(out=pw[0:32, :], in_=pw[0:32, :], func=AF.Exp),
                   reads=[pwk], writes=[pwk])
            self.o("V", lambda e: e.tensor_mul(pw[0:32, :], pw[0:32, :], tw[0:32, :]),
                   reads=[pwk, twk], writes=[pwk])
            self.ctx.dma(self.d["pwdram"][b, :, :], pw[0:32, :], reads=[pwk], writes=["D:pwdram"])
            self.o("V", lambda e: e.tensor_tensor_scan(out=den[0:32, :], data0=pw[0:32, :],
                                                       data1=z32[0:32, :], initial=0.0,
                                                       op0=ALU.add, op1=ALU.add),
                   reads=[pwk, z32k], writes=[denk])
            self.o("V", lambda e: e.tensor_scalar_max(den[0:32, :], den[0:32, :], 1e-4),
                   reads=[denk], writes=[denk])
            self.o("V", lambda e: e.reciprocal(den[0:32, :], den[0:32, :]),
                   reads=[denk], writes=[denk])
            self.ctx.dma(self.d["invdram"][b, :, :], den[0:32, :], reads=[denk],
                         writes=["D:invdram"])

    def st_l1mixB(self):
        A = self.A
        A.reset()
        th, thk = self.strip["theta_s"]
        dl, dlk = A.alloc(4, F32, "dlog")
        self.ctx.dma(dl[0:1, 0:3], self.d["deriv_l"], writes=[dlk], reads=["D:deriv_l"])
        dmx, dmxk = A.alloc(4, F32, "dmax")
        self.o("V", lambda e: e.tensor_reduce(out=dmx[0:1, 0:1], in_=dl[0:1, 0:3],
                                              axis=AX.X, op=ALU.max),
               reads=[dlk], writes=[dmxk])
        self.o("V", lambda e: e.scalar_tensor_tensor(out=dl[0:1, 0:3], in0=dmx[0:1, 0:1].broadcast_to([1, 3]),
                                                     scalar=-1.0, in1=dl[0:1, 0:3],
                                                     op0=ALU.mult, op1=ALU.add),
               reads=[dlk, dmxk], writes=[dlk])
        self.o("S", lambda e: e.activation(out=dl[0:1, 0:3], in_=dl[0:1, 0:3], func=AF.Exp),
               reads=[dlk], writes=[dlk])
        self.o("V", lambda e: e.tensor_reduce(out=dmx[0:1, 0:1], in_=dl[0:1, 0:3],
                                              axis=AX.X, op=ALU.add),
               reads=[dlk], writes=[dmxk])
        self.o("V", lambda e: e.reciprocal(dmx[0:1, 0:1], dmx[0:1, 0:1]),
               reads=[dmxk], writes=[dmxk])
        self.o("V", lambda e: e.tensor_scalar_mul(dl[0:1, 0:3], dl[0:1, 0:3], dmx[0:1, 0:1]),
               reads=[dlk, dmxk], writes=[dlk])
        self.o("V", lambda e: e.tensor_scalar_mul(dl[0:1, 3:4], dl[0:1, 2:3], -1.0),
               reads=[dlk], writes=[dlk])
        wscr = self.dram_scratch("wscr", [1, 4], F32)
        self.ctx.dma(wscr[0:1, :], dl[0:1, 0:4], reads=[dlk], writes=["D:wscr"])
        wb_, wbk_ = A.alloc(4, F32, "wbb")
        for j in range(3):
            self.ctx.dma(wb_[:, j:j + 1], wscr[0:1, j if j < 2 else 3].unsqueeze(1).broadcast_to([128, 1]),
                         reads=["D:wscr"], writes=[wbk_]) if False else None
        for j, col in enumerate([0, 1, 3]):
            self.ctx.dma(wb_[:, j:j + 1], wscr[0:1, col:col + 1].broadcast_to([128, 1]),
                         reads=["D:wscr"], writes=[wbk_], eng="P")
        wre, wrek = A.alloc(128, F32, "wreF")
        wim, wimk = A.alloc(128, F32, "wimF")
        wreb, wrebk = A.alloc(128, BF16, "wreB")
        wimb, wimbk = A.alloc(128, BF16, "wimB")
        nsre, nsrek = self.strip["ns_re"]
        nsim, nsimk = self.strip["ns_im"]
        for (wf, wfk, dn) in ((wre, wrek, "L1_attn_W_re"), (wim, wimk, "L1_attn_W_im")):
            self.o("V", lambda e, wf=wf: e.memset(wf[:, :], 0.0), writes=[wfk])
            self.ctx.dma(wf[0:64, 0:64], self.d[dn], reads=[f"D:{dn}"], writes=[wfk])
            self.ctx.dma(wf[64:128, 64:128], self.d[dn], reads=[f"D:{dn}"], writes=[wfk])
        self.o("V", lambda e: e.tensor_scalar_mul(wre[:, :], wre[:, :], nsre[:, 0:1]),
               reads=[wrek, nsrek], writes=[wrek])
        self.o("V", lambda e: e.tensor_scalar_mul(wim[:, :], wim[:, :], nsim[:, 0:1]),
               reads=[wimk, nsimk], writes=[wimk])
        self.o("V", lambda e: e.tensor_copy(wreb[:, :], wre[:, :]), reads=[wrek], writes=[wrebk])
        self.o("V", lambda e: e.tensor_copy(wimb[:, :], wim[:, :]), reads=[wimk], writes=[wimbk])
        xvt, xvtk = A.alloc(T, BF16, "xvt")
        B = [A.alloc(T, F32, f"B{i}") for i in range(6)]
        z128, z128k = A.alloc(T, F32, "z128")
        nre, nrek = A.alloc(T, BF16, "nreb")
        nim, nimk = A.alloc(T, BF16, "nimb")
        mst, mstk = A.alloc(T, F32, "mst")
        yrw, yrwk = A.alloc(T, BF16, "yrw")
        g64, g64k = self.strip["g64"]
        self.o("V", lambda e: e.memset(z128[:, :], 0.0), writes=[z128k])
        o = self.o
        for b in range(BL):
            for jt in range(NDI):
                k0 = 2 * jt
                (b1, b1k), (b2, b2k), (b3, b3k), (b4, b4k), (b5, b5k), (b6, b6k) = B
                self.ctx.dma(xvt[:, :], self.d["xval"][b, jt * 128:(jt + 1) * 128, :],
                             writes=[xvtk], reads=["D:xval"])
                o("V", lambda e, jt=jt: e.tensor_scalar_mul(b1[:, :], xvt[:, :], th[:, jt:jt + 1]),
                  reads=[xvtk, thk], writes=[b1k])
                o("V", lambda e: e.tensor_scalar(out=b2[:, :], in0=b1[:, :], scalar1=INV2PI,
                                                 scalar2=MAGIC, op0=ALU.mult, op1=ALU.add),
                  reads=[b1k], writes=[b2k])
                o("V", lambda e: e.tensor_scalar(out=b2[:, :], in0=b2[:, :], scalar1=MAGIC,
                                                 scalar2=None, op0=ALU.subtract),
                  reads=[b2k], writes=[b2k])
                o("V", lambda e: e.scalar_tensor_tensor(out=b3[:, :], in0=b2[:, :], scalar=-TWOPI,
                                                        in1=b1[:, :], op0=ALU.mult, op1=ALU.add),
                  reads=[b2k, b1k], writes=[b3k])
                o("V", lambda e: e.tensor_scalar(out=b2[:, :], in0=b1[:, :], scalar1=INV2PI,
                                                 scalar2=0.25, op0=ALU.mult, op1=ALU.add),
                  reads=[b1k], writes=[b2k])
                o("V", lambda e: e.tensor_scalar(out=b2[:, :], in0=b2[:, :], scalar1=MAGIC,
                                                 scalar2=MAGIC, op0=ALU.add, op1=ALU.subtract),
                  reads=[b2k], writes=[b2k])
                o("V", lambda e: e.scalar_tensor_tensor(out=b4[:, :], in0=b2[:, :], scalar=-TWOPI,
                                                        in1=b1[:, :], op0=ALU.mult, op1=ALU.add),
                  reads=[b2k, b1k], writes=[b4k])
                o("S", lambda e: e.activation(out=b2[:, :], in_=b4[:, :], func=AF.Sin,
                                              bias=PI / 2),
                  reads=[b4k, self.ck], writes=[b2k])
                o("S", lambda e: e.activation(out=b5[:, :], in_=b3[:, :], func=AF.Sin),
                  reads=[b3k], writes=[b5k])
                o("V", lambda e: e.tensor_scalar(out=b4[:, :], in0=xvt[:, :],
                                                 scalar1=wb_[:, 1:2], scalar2=wb_[:, 0:1],
                                                 op0=ALU.mult, op1=ALU.add),
                  reads=[xvtk, wbk_], writes=[b4k])
                o("S", lambda e: e.activation(out=b6[:, :], in_=xvt[:, :], func=AF.Square),
                  reads=[xvtk], writes=[b6k])
                o("V", lambda e: e.scalar_tensor_tensor(out=b4[:, :], in0=b6[:, :],
                                                        scalar=wb_[:, 2:3], in1=b4[:, :],
                                                        op0=ALU.mult, op1=ALU.add),
                  reads=[b6k, wbk_, b4k], writes=[b4k])
                o("V", lambda e: e.tensor_mul(b6[:, :], b4[:, :], b2[:, :]),
                  reads=[b4k, b2k], writes=[b6k])
                o("V", lambda e: e.tensor_mul(b2[:, :], b4[:, :], b5[:, :]),
                  reads=[b4k, b5k], writes=[b2k])
                self.ctx.dma(b4[0:64, :], self.d["pwdram"][b, k0:k0 + 1, :].broadcast_to([64, T]),
                             reads=["D:pwdram"], writes=[b4k])
                self.ctx.dma(b4[64:128, :], self.d["pwdram"][b, k0 + 1:k0 + 2, :].broadcast_to([64, T]),
                             reads=["D:pwdram"], writes=[b4k])
                o("V", lambda e: e.tensor_mul(b3[:, :], b6[:, :], b4[:, :]),
                  reads=[b6k, b4k], writes=[b3k])
                o("V", lambda e: e.tensor_mul(b5[:, :], b2[:, :], b4[:, :]),
                  reads=[b2k, b4k], writes=[b5k])
                o("V", lambda e: e.tensor_tensor_scan(out=b6[:, :], data0=b3[:, :],
                                                      data1=z128[:, :], initial=0.0,
                                                      op0=ALU.add, op1=ALU.add),
                  reads=[b3k, z128k], writes=[b6k])
                o("V", lambda e: e.tensor_tensor_scan(out=b2[:, :], data0=b5[:, :],
                                                      data1=z128[:, :], initial=0.0,
                                                      op0=ALU.add, op1=ALU.add),
                  reads=[b5k, z128k], writes=[b2k])
                self.ctx.dma(b4[0:64, :], self.d["invdram"][b, k0:k0 + 1, :].broadcast_to([64, T]),
                             reads=["D:invdram"], writes=[b4k])
                self.ctx.dma(b4[64:128, :], self.d["invdram"][b, k0 + 1:k0 + 2, :].broadcast_to([64, T]),
                             reads=["D:invdram"], writes=[b4k])
                o("V", lambda e: e.tensor_mul(b6[:, :], b6[:, :], b4[:, :]),
                  reads=[b6k, b4k], writes=[b6k])
                o("V", lambda e: e.tensor_mul(b2[:, :], b2[:, :], b4[:, :]),
                  reads=[b2k, b4k], writes=[b2k])
                o("V", lambda e: e.tensor_copy(nre[:, :], b6[:, :]), reads=[b6k], writes=[nrek])
                o("V", lambda e: e.tensor_copy(nim[:, :], b2[:, :]), reads=[b2k], writes=[nimk])
                o("S", lambda e: e.activation(out=b3[:, :], in_=b6[:, :], func=AF.Square),
                  reads=[b6k], writes=[b3k])
                o("S", lambda e: e.activation(out=b5[:, :], in_=b2[:, :], func=AF.Square),
                  reads=[b2k], writes=[b5k])
                for c in range(NCH):
                    sl = slice(c * CH, (c + 1) * CH)
                    psM, pkM = self.psum()
                    self.o("T", lambda e, psM=psM, sl=sl: (
                        e.matmul(psM[0:2, :], g64[:, 0:2], b3[:, sl], start=True, stop=False),
                        e.matmul(psM[0:2, :], g64[:, 0:2], b5[:, sl], start=False, stop=True))[-1],
                        reads=[g64k, b3k, b5k], writes=[pkM])
                    self.o("V", lambda e, psM=psM, sl=sl: e.tensor_copy(mst[0:2, sl], psM[0:2, :]),
                           reads=[pkM], writes=[mstk])
                    psY, pkY = self.psum()
                    self.o("T", lambda e, psY=psY, sl=sl: (
                        e.matmul(psY[:, :], wreb[:, :], nre[:, sl], start=True, stop=False),
                        e.matmul(psY[:, :], wimb[:, :], nim[:, sl], start=False, stop=True))[-1],
                        reads=[wrebk, wimbk, nrek, nimk], writes=[pkY])
                    self.o("V", lambda e, psY=psY, sl=sl: e.tensor_copy(yrw[:, sl], psY[:, :]),
                           reads=[pkY], writes=[yrwk])
                self.ctx.dma(self.d["msd"][b, k0:k0 + 2, :], mst[0:2, :],
                             reads=[mstk], writes=["D:msd"])
                self.ctx.dma(self.d["yraw"][b, jt * 128:(jt + 1) * 128, :], yrw[:, :],
                             reads=[yrwk], writes=["D:yraw"])

    def st_l1mixC(self):
        A = self.A
        A.reset()
        ms, msk = A.alloc(T, F32, "msC")
        inv, invk = A.alloc(T, F32, "invC")
        for b in range(BL):
            self.ctx.dma(ms[0:32, :], self.d["msd"][b, :, :], writes=[msk], reads=["D:msd"])
            self.o("S", lambda e: e.activation(out=ms[0:32, :], in_=ms[0:32, :], func=AF.Ln,
                                               scale=1.0 / 128.0, bias=EPS_LN),
                   reads=[msk, self.ck], writes=[msk])
            self.o("S", lambda e: e.activation(out=ms[0:32, :], in_=ms[0:32, :], func=AF.Exp,
                                               scale=-0.5),
                   reads=[msk], writes=[msk])
            self.ctx.dma(self.d["combd"][b, :, :], ms[0:32, :], reads=[msk], writes=["D:combd"])

    def st_l1out(self):
        A = self.A
        A.reset()
        combb, combbk = A.alloc(T, F32, "combb")
        xgt, xgtk = A.alloc(T, BF16, "xgt")
        yrt, yrtk = A.alloc(T, BF16, "yrt")
        g2, g2k = A.alloc(T, F32, "g2")
        ygt, ygtk = A.alloc(T, BF16, "ygt")
        for b in range(BL):
            for jt in range(NDI):
                k0 = 2 * jt
                self.ctx.dma(combb[0:64, :], self.d["combd"][b, k0:k0 + 1, :].broadcast_to([64, T]),
                             reads=["D:combd"], writes=[combbk])
                self.ctx.dma(combb[64:128, :], self.d["combd"][b, k0 + 1:k0 + 2, :].broadcast_to([64, T]),
                             reads=["D:combd"], writes=[combbk])
                self.ctx.dma(xgt[:, :], self.d["xgate"][b, jt * 128:(jt + 1) * 128, :],
                             writes=[xgtk], reads=["D:xgate"])
                self.ctx.dma(yrt[:, :], self.d["yraw"][b, jt * 128:(jt + 1) * 128, :],
                             writes=[yrtk], reads=["D:yraw"])
                self.o("V", lambda e: e.tensor_mul(g2[:, :], xgt[:, :], combb[:, :]),
                       reads=[xgtk, combbk], writes=[g2k])
                self.o("V", lambda e: e.tensor_mul(ygt[:, :], yrt[:, :], g2[:, :]),
                       reads=[yrtk, g2k], writes=[ygtk])
                self.ctx.dma(self.d["ygb"][b, jt * 128:(jt + 1) * 128, :], ygt[:, :],
                             reads=[ygtk], writes=["D:ygb"])
        A.reset()
        w = self.load_w_lhsT_bf16("L1_attn_out_proj_w", DI, D, "oproj")
        ygc = [A.alloc(CH, BF16, f"ygc{k}") for k in range(NDI)]
        at, atk = A.alloc(CH, BF16, "attn1t")
        for b in range(BL):
            for c in range(NCH):
                sl = slice(c * CH, (c + 1) * CH)
                for k in range(NDI):
                    ap, key = ygc[k]
                    self.ctx.dma(ap[:, :], self.d["ygb"][b, k * 128:(k + 1) * 128, sl],
                                 writes=[key], reads=["D:ygb"])
                for m in range(NT):
                    msl = slice(m * 128, (m + 1) * 128)
                    ps, pk = self.psum()
                    self.mm_group(ps[:, :], pk,
                                  [(w[k][0][:, msl], ygc[k][0][:, :]) for k in range(NDI)],
                                  reads=[w[k][1] for k in range(NDI)] + [y[1] for y in ygc])
                    self.o("V", lambda e, ps=ps: e.tensor_copy(at[:, :], ps[:, :]),
                           reads=[pk], writes=[atk])
                    self.ctx.dma(self.d["attn1"][b, msl, sl], at[:, :],
                                 reads=[atk], writes=["D:attn1"])


@_mixin
class _Block1Decl:
    def st_prep1(self):
        self.load_strip("conv_b", "conv_b_s", NPROJ)
        for i in range(4):
            self.load_strip(f"conv_k{i}", f"conv_k{i}_s", NPROJ)
        self.load_strip("theta_s", "theta_s", NDI)
        self.load_strip("score_ss", "score_ss", 1)
        self.load_strip("decay_s", "decay_s", 1)
        self.load_strip("anchor_s", "anchor_s", 1)
        self.load_strip("ns_re", "ns_re", 1)
        self.load_strip("ns_im", "ns_im", 1)
        self.load_strip("g64", "c_g64", 2)


B1A_STAGES = ["prep1", "ln1", "proj", "mixA", "mixB", "mixC", "out1"]


def declare_block1a(mk, feed_xn=False):
    for nm, shape in [("L1_attn_in_proj_w", [D, CPROJ]), ("conv_b_s", [128, NPROJ]),
                      ("conv_k0_s", [128, NPROJ]), ("conv_k1_s", [128, NPROJ]),
                      ("conv_k2_s", [128, NPROJ]), ("conv_k3_s", [128, NPROJ]),
                      ("theta_s", [128, NDI]), ("score_ss", [32, 1]),
                      ("decay_s", [28, 1]), ("anchor_s", [4, 1]),
                      ("deriv_l", [1, 3]), ("L1_attn_W_re", [64, 64]),
                      ("L1_attn_W_im", [64, 64]), ("ns_re", [128, 1]), ("ns_im", [128, 1]),
                      ("L1_attn_out_proj_w", [DI, D]), ("c_pos", [1, T]),
                      ("c_dist", [1, T]), ("c_g64", [128, 2])]:
        mk.dram_in(nm, shape)
    mk.dram_scratch("xn1f", [BL, D, T])
    mk.dram_scratch("xn1b", [BL, D, T], BF16)
    mk.dram_scratch("xval", [BL, DI, T], BF16)
    mk.dram_scratch("xgate", [BL, DI, T], BF16)
    mk.dram_scratch("sraw", [BL, 32, T])
    mk.dram_scratch("pwdram", [BL, 32, T])
    mk.dram_scratch("invdram", [BL, 32, T])
    mk.dram_scratch("msd", [BL, 32, T])
    mk.dram_scratch("combd", [BL, 32, T])
    mk.dram_scratch("yraw", [BL, DI, T], BF16)
    mk.dram_scratch("ygb", [BL, DI, T], BF16)
    mk.dram_scratch("attn1", [BL, D, T], BF16)


def emit_block1a(mk, upto=None, skip=()):
    stages = B1A_STAGES if upto is None else B1A_STAGES[:B1A_STAGES.index(upto) + 1]
    for s in stages:
        if s in skip:
            continue
        if s == "prep1":
            mk.st_prep1()
        elif s == "ln1":
            mk.st_ln("x1", "xn1f", "xn1b", "ln1")
        elif s == "proj":
            mk.st_l1proj()
        elif s == "mixA":
            mk.st_l1mixA()
        elif s == "mixB":
            mk.st_l1mixB()
        elif s == "mixC":
            mk.st_l1mixC()
        elif s == "out1":
            mk.st_l1out()


def host_prep_block1a(inp, Tcur):
    ck = np.asarray(inp["L1_attn_conv_k"], np.float32).reshape(4, CPROJ)
    theta = np.asarray(inp["L1_attn_theta"], np.float32).reshape(NHEAD * 64)
    ns = np.asarray(inp["L1_attn_norm_scale"], np.float32)
    pos = np.arange(Tcur, dtype=np.float32)
    g64 = np.zeros((128, 2), np.float32)
    g64[0:64, 0] = 1.0
    g64[64:128, 1] = 1.0
    m = {
        "L1_attn_in_proj_w": inp["L1_attn_in_proj_w"],
        "conv_b_s": vec_strip(inp["L1_attn_conv_b"], NPROJ),
        "theta_s": vec_strip(theta, NDI),
        "score_ss": np.asarray(inp["L1_attn_score_scale"], np.float32).reshape(32, 1),
        "decay_s": np.asarray(inp["L1_attn_decay_slopes"], np.float32).reshape(28, 1),
        "anchor_s": np.asarray(inp["L1_attn_anchor_slopes"], np.float32).reshape(4, 1),
        "deriv_l": np.asarray(inp["L1_attn_deriv_logits"], np.float32).reshape(1, 3),
        "L1_attn_W_re": inp["L1_attn_W_re"],
        "L1_attn_W_im": inp["L1_attn_W_im"],
        "ns_re": np.concatenate([ns[0:64], ns[0:64]]).reshape(128, 1),
        "ns_im": np.concatenate([ns[64:128], ns[64:128]]).reshape(128, 1),
        "L1_attn_out_proj_w": inp["L1_attn_out_proj_w"],
        "c_pos": pos.reshape(1, Tcur),
        "c_dist": np.maximum(np.float32(Tcur - 1) - pos, 0.0).reshape(1, Tcur),
        "c_g64": g64,
    }
    for i in range(4):
        m[f"conv_k{i}_s"] = vec_strip(ck[i], NPROJ)
    return {k: np.ascontiguousarray(np.asarray(v, np.float32)) for k, v in m.items()}


INNER = 512
NIN = 4
HID = 2730
NW1 = 22
QKS = 8.0


@_mixin
class _Block1Ffn:
    def load_w_ap_lhsT_bf16(self, dram_ap, dkey, K, M, tag, stg_n=2):
        ntk = ceil_div(K, 128)
        stg = [self.A.alloc(M, F32, f"{tag}wstg{i}") for i in range(stg_n)]
        out = []
        for k in range(ntk):
            kk = min(128, K - k * 128)
            f, fk = stg[k % stg_n]
            bb, bk = self.A.alloc(M, BF16, f"{tag}wb{k}")
            self.ctx.dma(f[:kk, :], dram_ap[k * 128:k * 128 + kk, :], writes=[fk], reads=[dkey])
            self.o("V", lambda e, bb=bb, f=f, kk=kk: e.tensor_copy(bb[:kk, :], f[:kk, :]),
                   reads=[fk], writes=[bk])
            out.append((bb, bk, kk))
        return out

    def st_ffn_qkv(self, li, fin_b):
        A = self.A
        A.reset()
        w = self.load_w_ap_lhsT_bf16(self.d["L1_ffn_wqkv"][li], "D:L1_ffn_wqkv",
                                     D, 3 * INNER, f"wqkv{li}", stg_n=1)
        fb = []
        for k in range(NT):
            ap, key = A.alloc(T + 1, BF16, f"fqkv{k}")
            fb.append((ap, key))
        qf2 = [A.alloc(T, F32, f"qkvqf{i}") for i in range(2)]
        vb2 = [A.alloc(T, BF16, f"qkvvb{i}") for i in range(2)]
        vtb2 = [A.alloc(128, BF16, f"qkvvtb{i}") for i in range(2)]
        idb, idbk = self.ident_bf16()
        for b in range(BL):
            for k in range(NT):
                ap, key = fb[k]
                self.o("V", lambda e, ap=ap: e.memset(ap[:, 0:1], 0.0), writes=[key])
                self.ctx.dma(ap[:, 1:T + 1], self.d[fin_b][b, k * 128:(k + 1) * 128, :],
                             writes=[key], reads=[f"D:{fin_b}"])
            for m in range(12):
                msl = slice(m * 128, (m + 1) * 128)
                is_v = m >= 8
                qf, qfk = qf2[m % 2]
                vb, vbk = vb2[m % 2]
                for c in range(NCH):
                    ps, pk = self.psum()
                    pairs = []
                    for k in range(NT):
                        off = 1 if k < 4 else 0
                        pairs.append((w[k][0][:, msl],
                                      fb[k][0][:, off + c * CH: off + c * CH + CH]))
                    self.mm_group(ps[:, :], pk, pairs,
                                  reads=[w[k][1] for k in range(NT)] + [f[1] for f in fb])
                    if is_v:
                        self.o("V", lambda e, ps=ps, c=c, vb=vb: e.tensor_copy(
                            vb[:, c * CH:(c + 1) * CH], ps[:, :]), reads=[pk], writes=[vbk])
                    else:
                        self.o("V", lambda e, ps=ps, c=c, qf=qf: e.tensor_copy(
                            qf[:, c * CH:(c + 1) * CH], ps[:, :]), reads=[pk], writes=[qfk])
                if is_v:
                    for tc in range(T // 128):
                        ps, pk = self.psum()
                        psb = ps[:, :].bitcast(BF16)
                        vtb, vtbk = vtb2[tc % 2]
                        self.o("T", lambda e, psb=psb, tc=tc, vb=vb: e.transpose(
                            psb[:, 0:128], vb[:, tc * 128:(tc + 1) * 128], idb[:, :]),
                            reads=[vbk, idbk], writes=[pk])
                        self.o("V", lambda e, psb=psb, vtb=vtb: e.tensor_copy(
                            vtb[:, :], psb[:, 0:128]),
                            reads=[pk], writes=[vtbk])
                        self.ctx.dma(self.d[f"vT{li}"][b, m - 8, tc * 128:(tc + 1) * 128, :],
                                     vtb[:, :], reads=[vtbk], writes=[f"D:vT{li}"])
                else:
                    dst = f"qraw{li}" if m < 4 else f"kraw{li}"
                    mm = m if m < 4 else m - 4
                    self.ctx.dma(self.d[dst][b, mm * 128:(mm + 1) * 128, :], qf[:, :],
                                 reads=[qfk], writes=[f"D:{dst}"])
        A.reset()
        csb, csbk = A.alloc(T, BF16, "csbufB")
        ssb, ssbk = A.alloc(T, BF16, "ssbufB")
        cstage, cstagek = A.alloc(T, F32, "cstage")
        qfB = [A.alloc(T, F32, f"qfB{i}") for i in range(2)]
        sqc2 = [A.alloc(CH, F32, f"sqcB{i}") for i in range(2)]
        rsbB = [A.alloc(T, F32, f"rsbB{i}") for i in range(2)]
        ss2B = rsbB
        qrotB = [A.alloc(T, F32, f"qrotB{i}") for i in range(2)]
        m1B = [A.alloc(T, F32, f"m1B{i}") for i in range(2)]
        qobB = [A.alloc(T, BF16, f"qobB{i}") for i in range(2)]
        g64, g64k = self.strip["g64"]
        rs_scr = self.dram_scratch(f"rs_scr{li}", [4, 2, T], F32)

        def load_consts(c1n, c2n):
            self.ctx.dma(cstage[:, :], self.d[c1n], writes=[cstagek], reads=[f"D:{c1n}"])
            self.o("V", lambda e: e.tensor_copy(csb[:, :], cstage[:, :]),
                   reads=[cstagek], writes=[csbk])
            self.ctx.dma(cstage[:, :], self.d[c2n], writes=[cstagek], reads=[f"D:{c2n}"])
            self.o("V", lambda e: e.tensor_copy(ssb[:, :], cstage[:, :]),
                   reads=[cstagek], writes=[ssbk])
        it = 0
        for qk in range(2):
            srcn = f"qraw{li}" if qk == 0 else f"kraw{li}"
            dstn = f"qb{li}" if qk == 0 else f"kb{li}"
            load_consts("c_csq2" if qk == 0 else "c_csk2",
                        "c_ssq2" if qk == 0 else "c_ssk2")
            dupn = f"qs_dup_{li}" if qk == 0 else f"ks_dup_{li}"
            rotn = f"qsrot_dup_{li}" if qk == 0 else f"ksrot_dup_{li}"
            dup, dupk = self.strip[dupn]
            rot, rotk = self.strip[rotn]
            for b in range(BL):
                for m in range(4):
                    qf, qfk = qfB[it % 2]
                    sqc, sqck = sqc2[it % 2]
                    ss2, ss2k = ss2B[it % 2]
                    rsb, rsbk = rsbB[it % 2]
                    qrot, qrotk = qrotB[it % 2]
                    m1, m1k = m1B[it % 2]
                    m2, m2k = qrot, qrotk
                    qob, qobk = qobB[it % 2]
                    scr_i = it % 4
                    it += 1
                    self.ctx.dma(qf[:, :], self.d[srcn][b, m * 128:(m + 1) * 128, :],
                                 writes=[qfk], reads=[f"D:{srcn}"])
                    for c in range(NCH):
                        ps2, pk2 = self.psum()
                        self.o("S", lambda e, c=c, sqc=sqc, qf=qf: e.activation(
                            out=sqc[:, :], in_=qf[:, c * CH:(c + 1) * CH], func=AF.Square),
                            reads=[qfk], writes=[sqck])
                        self.o("T", lambda e, ps2=ps2, sqc=sqc: e.matmul(
                            ps2[0:2, :], g64[:, 0:2], sqc[:, :], start=True, stop=True),
                            reads=[g64k, sqck], writes=[pk2])
                        self.o("V", lambda e, ps2=ps2, c=c, ss2=ss2: e.tensor_copy(
                            ss2[0:2, c * CH:(c + 1) * CH], ps2[0:2, :]),
                            reads=[pk2], writes=[ss2k])
                    self.o("S", lambda e, ss2=ss2: e.activation(out=ss2[0:2, :], in_=ss2[0:2, :],
                                                       func=AF.Ln, bias=1e-24),
                           reads=[ss2k, self.ck], writes=[ss2k])
                    self.o("S", lambda e, ss2=ss2: e.activation(out=ss2[0:2, :], in_=ss2[0:2, :],
                                                       func=AF.Exp, scale=-0.5),
                           reads=[ss2k], writes=[ss2k])
                    self.ctx.dma(rs_scr[scr_i, 0:2, :], ss2[0:2, :], reads=[ss2k],
                                 writes=[f"D:rs{li}{scr_i}"])
                    self.ctx.dma(rsb[0:64, :], rs_scr[scr_i, 0:1, :].broadcast_to([64, T]),
                                 reads=[f"D:rs{li}{scr_i}"], writes=[rsbk], eng="P")
                    self.ctx.dma(rsb[64:128, :], rs_scr[scr_i, 1:2, :].broadcast_to([64, T]),
                                 reads=[f"D:rs{li}{scr_i}"], writes=[rsbk], eng="P")
                    for (d0, s0) in [(0, 32), (32, 0), (64, 96), (96, 64)]:
                        self.ctx.dma(qrot[d0:d0 + 32, :], qf[s0:s0 + 32, :],
                                     reads=[qfk], writes=[qrotk])
                    self.o("V", lambda e, m1=m1, qf=qf: e.tensor_mul(m1[:, :], qf[:, :], csb[:, :]),
                           reads=[qfk, csbk], writes=[m1k])
                    self.o("V", lambda e, m1=m1, dup=dup: e.tensor_scalar_mul(
                        m1[:, :], m1[:, :], dup[:, 0:1]),
                           reads=[m1k, dupk], writes=[m1k])
                    self.o("P", lambda e, m2=m2, qrot=qrot: e.tensor_mul(
                        m2[:, :], qrot[:, :], ssb[:, :]),
                           reads=[qrotk, ssbk], writes=[m2k])
                    self.o("P", lambda e, m2=m2, rot=rot: e.tensor_scalar_mul(
                        m2[:, :], m2[:, :], rot[:, 0:1]),
                           reads=[m2k, rotk], writes=[m2k])
                    self.o("V", lambda e, m1=m1, m2=m2: e.tensor_add(m1[:, :], m1[:, :], m2[:, :]),
                           reads=[m1k, m2k], writes=[m1k])
                    self.o("V", lambda e, qob=qob, m1=m1, rsb=rsb: e.tensor_mul(
                        qob[:, :], m1[:, :], rsb[:, :]),
                           reads=[m1k, rsbk], writes=[qobk])
                    self.ctx.dma(self.d[dstn][b, m * 128:(m + 1) * 128, :], qob[:, :],
                                 reads=[qobk], writes=[f"D:{dstn}"])

    def strip_big(self, name):
        if name not in self.strip:
            ap, key = self.P.alloc(T, F32, name)
            self.ctx.dma(ap[:, :], self.d[name], writes=[key])
            self.strip[name] = (ap, key)
        return self.strip[name]

    def st_ffn_attn(self, li):
        A = self.A
        A.reset()
        NKC = T // 128
        NQC = T // 128
        NQG = T // CH
        sets = []
        for i in range(2):
            qaug = A.alloc(T, BF16, f"qaug{i}")
            kaug = A.alloc(T, BF16, f"kaug{i}")
            vtl, _vk0 = A.alloc(65 * NKC, BF16, f"vtl{i}")
            vtlk = [f"vtl{i}_{kc}" for kc in range(NKC)]
            for _k in vtlk:
                self.ctx.merge_keys(_k, [_vk0])
            scrE = A.alloc(256, F32, f"scrE{i}")
            rmxb = A.alloc(NQC + 4, F32, f"rmxb{i}")
            rmx2 = A.alloc(4, F32, f"rmx2{i}")
            rmte = A.alloc(128, F32, f"rmte{i}")
            rmf = A.alloc(T, F32, f"rmf{i}")
            nrm = A.alloc(T, BF16, f"nrm{i}")
            expb2 = [A.alloc(CH, BF16, f"expb{i}_{j}") for j in range(2)]
            oacc = A.alloc(T, F32, f"oacc{i}")
            rss = A.alloc(T, F32, f"rss{i}")
            sets.append(dict(qaug=qaug, kaug=kaug, vtl=(vtl, _vk0, vtlk), scrE=scrE,
                             rmxb=rmxb, rmx2=rmx2, rmte=rmte, rmf=rmf, nrm=nrm,
                             expb2=expb2, oacc=oacc, rss=rss, av=i, p1b=(2 + 2 * i, 3 + 2 * i)))
            self.o("V", lambda e, vtl=vtl: e.memset(vtl[:, :], 1.0), writes=vtlk + [_vk0])
        onesb, onesbk = A.alloc(2, BF16, "onesb")
        self.o("V", lambda e: e.memset(onesb[:, 0:1], 1.0), writes=[onesbk])
        mnd, mndk = self.strip_c128("c_mnd")
        mnf, mnfk = self.strip_c128("c_mnf")
        md2, md2k = self.strip_c128("c_md2")
        mf2, mf2k = self.strip_c128("c_mf2")
        idf, idfk = self.ident_f32()
        it = 0
        for b in range(BL):
            for h in range(8):
                S = sets[it % 2]
                it += 1
                (qaug, qaugk), (kaug, kaugk) = S["qaug"], S["kaug"]
                vtl, _vtlk0, vtlk = S["vtl"]
                scrE, scrEk = S["scrE"]
                rmxb, rmxbk = S["rmxb"]
                rmx2, rmx2k = S["rmx2"]
                rmte, rmtek = S["rmte"]
                rmf, rmfk = S["rmf"]
                nrm, nrmk = S["nrm"]
                expb2 = S["expb2"]
                oacc, oacck = S["oacc"]
                rss, rssk = S["rss"]
                psAV, pkAV = self.ps[S["av"]], self.ps_keys[S["av"]]
                pb0, pb1 = S["p1b"]
                mv, half = h // 2, h % 2
                self.ctx.dma(qaug[0:64, :], self.d[f"qb{li}"][b, half * 64 + mv * 128:
                                                             half * 64 + mv * 128 + 64, :],
                             writes=[qaugk], reads=[f"D:qb{li}"])
                self.ctx.dma(kaug[0:64, :], self.d[f"kb{li}"][b, half * 64 + mv * 128:
                                                              half * 64 + mv * 128 + 64, :],
                             writes=[kaugk], reads=[f"D:kb{li}"])
                self.o("V", lambda e, kaug=kaug: e.memset(kaug[64:65, :], 1.0), writes=[kaugk])
                for kc in range(NKC):
                    self.ctx.dma(vtl[:, kc * 65:kc * 65 + 64],
                                 self.d[f"vT{li}"][b, mv, kc * 128:(kc + 1) * 128,
                                                   half * 64:half * 64 + 64],
                                 writes=[vtlk[kc]], reads=[f"D:vT{li}", _vtlk0])
                for qc in range(NQC):
                    j0 = max(0, qc - 4) * 128
                    nj = (qc + 1) * 128 - j0
                    njA = min(nj, 512)
                    njB = nj - njA
                    pb = pb0 if qc % 2 == 0 else pb1
                    psA, pkA = self.ps[pb], self.ps_keys[pb]
                    psB, pkB = self.ps[pb1 if qc % 2 == 0 else pb0], \
                               self.ps_keys[pb1 if qc % 2 == 0 else pb0]
                    qsl = slice(qc * 128, (qc + 1) * 128)
                    def p1(e, qsl=qsl, j0=j0, njA=njA, njB=njB, psA=psA, psB=psB,
                           qaug=qaug, kaug=kaug):
                        last = e.matmul(psA[:, 0:njA], qaug[0:64, qsl],
                                        kaug[0:64, j0:j0 + njA], start=True, stop=True)
                        if njB:
                            last = e.matmul(psB[:, 0:njB], qaug[0:64, qsl],
                                            kaug[0:64, j0 + njA:j0 + njA + njB],
                                            start=True, stop=True)
                        return last
                    self.o("T", p1, reads=[qaugk, kaugk],
                           writes=[pkA] + ([pkB] if njB else []))
                    if njB:
                        self.o("V", lambda e, psB=psB, scrE=scrE: e.tensor_add(
                            scrE[:, 0:128], psB[:, 0:128], mnd[:, :]),
                            reads=[pkB, mndk], writes=[scrEk])
                        self.o("V", lambda e, psA=psA, scrE=scrE: e.tensor_add(
                            scrE[:, 128:256], psA[:, 0:128], mnf[:, :]),
                            reads=[pkA, mnfk], writes=[scrEk])
                        self.o("V", lambda e, qc=qc, scrE=scrE, rmxb=rmxb: e.tensor_reduce(
                            out=rmxb[:, qc:qc + 1], in_=scrE[:, 0:256], axis=AX.X, op=ALU.max),
                            reads=[scrEk], writes=[rmxbk])
                        self.o("V", lambda e, psA=psA, rmx2=rmx2: e.tensor_reduce(
                            out=rmx2[:, 0:1], in_=psA[:, 128:512], axis=AX.X, op=ALU.max),
                            reads=[pkA], writes=[rmx2k])
                        self.o("V", lambda e, qc=qc, rmxb=rmxb, rmx2=rmx2: e.tensor_max(
                            rmxb[:, qc:qc + 1], rmxb[:, qc:qc + 1], rmx2[:, 0:1]),
                               reads=[rmxbk, rmx2k], writes=[rmxbk])
                    else:
                        dcol = qc * 128
                        self.o("V", lambda e, psA=psA, dcol=dcol, scrE=scrE: e.tensor_add(
                            scrE[:, 0:128], psA[:, dcol:dcol + 128], mnd[:, :]),
                            reads=[pkA, mndk], writes=[scrEk])
                        self.o("V", lambda e, qc=qc, scrE=scrE, rmxb=rmxb: e.tensor_reduce(
                            out=rmxb[:, qc:qc + 1], in_=scrE[:, 0:128], axis=AX.X, op=ALU.max),
                            reads=[scrEk], writes=[rmxbk])
                        if dcol > 0:
                            self.o("V", lambda e, psA=psA, dcol=dcol, rmx2=rmx2: e.tensor_reduce(
                                out=rmx2[:, 0:1], in_=psA[:, 0:dcol], axis=AX.X, op=ALU.max),
                                reads=[pkA], writes=[rmx2k])
                            self.o("V", lambda e, qc=qc, rmxb=rmxb, rmx2=rmx2: e.tensor_max(
                                rmxb[:, qc:qc + 1], rmxb[:, qc:qc + 1], rmx2[:, 0:1]),
                                   reads=[rmxbk, rmx2k], writes=[rmxbk])
                pst, pkt = self.ps[6 + (it % 2)], self.ps_keys[6 + (it % 2)]
                self.o("T", lambda e, pst=pst, rmxb=rmxb: e.transpose(
                    pst[0:NQC, 0:128], rmxb[:, 0:NQC], idf[:, :]),
                    reads=[rmxbk, idfk], writes=[pkt])
                self.o("V", lambda e, pst=pst, rmte=rmte: e.tensor_scalar_mul(
                    rmte[0:NQC, :], pst[0:NQC, 0:128], -1.0),
                    reads=[pkt], writes=[rmtek])
                self.ctx.dma(self.d["rmx"][b, h, :].rearrange("(a c) -> a c", c=128),
                             rmte[0:NQC, :], reads=[rmtek], writes=["D:rmx"])
                self.ctx.dma(rmf[0:1, :], self.d["rmx"][b, h, :].unsqueeze(0),
                             reads=["D:rmx"], writes=[rmfk], eng="P")
                self.o("V", lambda e, nrm=nrm, rmf=rmf: e.tensor_copy(nrm[0:1, :], rmf[0:1, :]),
                       reads=[rmfk], writes=[nrmk])
                self.ctx.dma(qaug[64:65, :], nrm[0:1, :], reads=[nrmk], writes=[qaugk])
                for qg in range(NQG):
                    kcs = [kc for kc in range(max(0, qg * 4 - 4), qg * 4 + 4)]
                    kcs.sort(key=lambda kc: -(min(3, kc - qg * 4 + 4)
                                              - max(0, kc - qg * 4)))
                    qsl = slice(qg * CH, (qg + 1) * CH)
                    pend = None
                    def emit_av(p, last):
                        ilo_, wdt_, expb_, expbk_, kc_, first_ = p
                        self.o("T", lambda e, kc=kc_, first=first_, last=last, psAV=psAV,
                               ilo=ilo_, wdt=wdt_, expb=expb_, vtl=vtl: e.matmul(
                            psAV[0:65, ilo:ilo + wdt], vtl[:, kc * 65:(kc + 1) * 65],
                            expb[:, ilo:ilo + wdt], start=first, stop=last),
                            reads=[vtlk[kc_], _vtlk0, expbk_], writes=[pkAV])
                    for ki, kc in enumerate(kcs):
                        si_lo = max(0, kc - qg * 4)
                        si_hi = min(3, kc - qg * 4 + 4)
                        ilo = si_lo * 128
                        wdt = (si_hi - si_lo + 1) * 128
                        psS, pkS = (self.ps[pb0], self.ps_keys[pb0]) if ki % 2 == 0 else \
                                   (self.ps[pb1], self.ps_keys[pb1])
                        expb, expbk = expb2[ki % 2]
                        self.o("T", lambda e, psS=psS, kc=kc, qg=qg, ilo=ilo, wdt=wdt,
                               kaug=kaug, qaug=qaug: e.matmul(
                            psS[:, 0:wdt], kaug[0:65, kc * 128:(kc + 1) * 128],
                            qaug[0:65, qg * CH + ilo:qg * CH + ilo + wdt],
                            start=True, stop=True),
                            reads=[kaugk, qaugk], writes=[pkS])
                        self.o("S", lambda e, psS=psS, ilo=ilo, wdt=wdt, expb=expb: e.activation(
                            out=expb[:, ilo:ilo + wdt], in_=psS[:, 0:wdt],
                            func=AF.Exp, scale=QKS),
                            reads=[pkS], writes=[expbk])
                        for si in range(si_lo, si_hi + 1):
                            rel = (qg * 4 + si) - kc
                            if rel == 0:
                                self.o("P", lambda e, si=si, expb=expb: e.tensor_mul(
                                    expb[:, si * 128:(si + 1) * 128],
                                    expb[:, si * 128:(si + 1) * 128], md2[:, :]),
                                    reads=[expbk, md2k], writes=[expbk])
                            elif rel == 4:
                                self.o("P", lambda e, si=si, expb=expb: e.tensor_mul(
                                    expb[:, si * 128:(si + 1) * 128],
                                    expb[:, si * 128:(si + 1) * 128], mf2[:, :]),
                                    reads=[expbk, mf2k], writes=[expbk])
                        if pend is not None:
                            emit_av(pend, False)
                        pend = (ilo, wdt, expb, expbk, kc, ki == 0)
                    emit_av(pend, True)
                    self.o("V", lambda e, psAV=psAV, qsl=qsl, oacc=oacc: e.tensor_copy(
                        oacc[0:64, qsl], psAV[0:64, :]), reads=[pkAV], writes=[oacck])
                    self.o("V", lambda e, psAV=psAV, qsl=qsl, rss=rss: e.tensor_copy(
                        rss[0:1, qsl], psAV[64:65, :]), reads=[pkAV], writes=[rssk])
                self.ctx.dma(self.d[f"oraw{li}"][b, h * 64:(h + 1) * 64, :], oacc[0:64, :],
                             reads=[oacck], writes=[f"D:oraw{li}"])
                self.ctx.dma(self.d[f"rsum{li}"][b, h, :].unsqueeze(0), rss[0:1, :],
                             reads=[rssk], writes=[f"D:rsum{li}"])

    def strip_c128(self, name):
        if name not in self.strip:
            ap, key = self.P.alloc(128, F32, name)
            self.ctx.dma(ap[:, :], self.d[name], writes=[key])
            self.strip[name] = (ap, key)
        return self.strip[name]

    def st_ffn_wo(self, li, fin_f, gout):
        A = self.A
        A.reset()
        w = self.load_w_ap_lhsT_bf16(self.d["L1_ffn_wo"][li], "D:L1_ffn_wo",
                                     INNER, D, f"wo{li}")
        rcp, rcpk = A.alloc(T, F32, "rcp")
        rcb, rcbk = A.alloc(T, F32, "rcb")
        orm, ormk = A.alloc(T, F32, "orm")
        onb = [A.alloc(T, BF16, f"onb{k}") for k in range(NIN)]
        ft, ftk = A.alloc(T, F32, "ftwo")
        go, gok = A.alloc(T, F32, "gowo")
        gob, gobk = A.alloc(T, BF16, "gobwo")
        rc_scr = self.dram_scratch(f"rc_scr{li}", [8, T], F32)
        for b in range(BL):
            self.ctx.dma(rcp[0:8, :], self.d[f"rsum{li}"][b, :, :], writes=[rcpk],
                         reads=[f"D:rsum{li}"])
            self.o("V", lambda e: e.reciprocal(rcp[0:8, :], rcp[0:8, :]),
                   reads=[rcpk], writes=[rcpk])
            self.ctx.dma(rc_scr[0:8, :], rcp[0:8, :], reads=[rcpk], writes=[f"D:rc{li}"])
            for mv in range(NIN):
                h0 = mv * 2
                self.ctx.dma(rcb[0:64, :], rc_scr[h0:h0 + 1, :].broadcast_to([64, T]),
                             reads=[f"D:rc{li}"], writes=[rcbk], eng="P")
                self.ctx.dma(rcb[64:128, :], rc_scr[h0 + 1:h0 + 2, :].broadcast_to([64, T]),
                             reads=[f"D:rc{li}"], writes=[rcbk], eng="P")
                self.ctx.dma(orm[:, :], self.d[f"oraw{li}"][b, mv * 128:(mv + 1) * 128, :],
                             writes=[ormk], reads=[f"D:oraw{li}"])
                ap, key = onb[mv]
                self.o("V", lambda e, ap=ap: e.tensor_mul(ap[:, :], orm[:, :], rcb[:, :]),
                       reads=[ormk, rcbk], writes=[key])
            for m in range(NT):
                msl = slice(m * 128, (m + 1) * 128)
                self.ctx.dma(ft[:, :], self.d[fin_f][b, msl, :], writes=[ftk],
                             reads=[f"D:{fin_f}"])
                for c in range(NCH):
                    ps, pk = self.psum()
                    sl = slice(c * CH, (c + 1) * CH)
                    self.mm_group(ps[:, :], pk,
                                  [(w[k][0][:, msl], onb[k][0][:, sl]) for k in range(NIN)],
                                  reads=[w[k][1] for k in range(NIN)] + [x[1] for x in onb])
                    self.o("V", lambda e, ps=ps, sl=sl: e.tensor_add(go[:, sl], ps[:, :],
                                                                     ft[:, sl]),
                           reads=[pk, ftk], writes=[gok])
                self.o("V", lambda e: e.tensor_copy(gob[:, :], go[:, :]),
                       reads=[gok], writes=[gobk])
                self.ctx.dma(self.d[gout][b, msl, :], go[:, :], reads=[gok],
                             writes=[f"D:{gout}"])
                self.ctx.dma(self.d[gout + "b"][b, msl, :], gob[:, :], reads=[gobk],
                             writes=[f"D:{gout}b"])

    def st_ffn_lnmid(self, li, gsrc, xmdst):
        A = self.A
        A.reset()
        lng, lngk = self.strip[f"lng{li}"]
        lnb, lnbk = self.strip[f"lnb{li}"]
        onesw, onesk = A.alloc(2, F32, "onesm")
        self.o("V", lambda e: e.memset(onesw[:, 0:1], 1.0), writes=[onesk])
        xch = [A.alloc(CH + 1, F32, f"mch{i}") for i in range(2)]
        sqch, sqchk = A.alloc(CH, F32, "msq")
        R1, R1k = A.alloc(T, F32, "mR1")
        R2, R2k = A.alloc(T, F32, "mR2")
        R3, R3k = A.alloc(T, F32, "mR3")
        R4, R4k = A.alloc(T, F32, "mR4")
        R5, R5k = A.alloc(T, F32, "mR5")
        sums, sumsq = R1[0:1, :], R2[0:1, :]
        c1, c2, msq2 = R3[0:1, :], R4[0:1, :], R5[0:1, :]
        R1k_c1, R2k_c2, R2k_m = R3k, R4k, R5k
        c1b, c1bk = A.alloc(T, F32, "mc1b")
        c2b, c2bk = A.alloc(T, F32, "mc2b")
        xfull = [A.alloc(T + 1, F32, f"mxf{i}") for i in range(2)]
        t1, t1k = A.alloc(T, F32, "mt1")
        xmb, xmbk = A.alloc(T, BF16, "mxmb")
        scr_b = self.dram_scratch(f"mscr_{li}", [2, T], F32)
        for b in range(BL):
            for c in range(NCH):
                psA, pkA = self.psum()
                psB, pkB = self.psum()
                for j in range(NT):
                    xa, xk = xch[j % 2]
                    if c == 0:
                        self.o("V", lambda e, xa=xa: e.memset(xa[:, 0:1], 0.0), writes=[xk])
                        self.ctx.dma(xa[:, 1:CH + 1], self.d[gsrc][b, j * 128:(j + 1) * 128,
                                                                   0:CH],
                                     writes=[xk], reads=[f"D:{gsrc}"])
                    else:
                        self.ctx.dma(xa[:, 0:CH + 1], self.d[gsrc][b, j * 128:(j + 1) * 128,
                                                                   c * CH - 1:c * CH + CH],
                                     writes=[xk], reads=[f"D:{gsrc}"])
                    off = 1 if j < 4 else 0
                    self.o("S", lambda e, xa=xa, off=off: e.activation(
                        out=sqch[:, :], in_=xa[:, off:off + CH], func=AF.Square),
                        reads=[xk], writes=[sqchk])
                    self.o("T", lambda e, psA=psA, xa=xa, off=off, j=j: e.matmul(
                        psA[0:1, :], onesw[:, 0:1], xa[:, off:off + CH],
                        start=(j == 0), stop=(j == NT - 1)),
                        reads=[onesk, xk], writes=[pkA])
                    self.o("T", lambda e, psB=psB, j=j: e.matmul(
                        psB[0:1, :], onesw[:, 0:1], sqch[:, :],
                        start=(j == 0), stop=(j == NT - 1)),
                        reads=[onesk, sqchk], writes=[pkB])
                self.o("V", lambda e, psA=psA, c=c: e.tensor_copy(
                    sums[:, c * CH:(c + 1) * CH], psA[0:1, :]), reads=[pkA], writes=[R1k])
                self.o("V", lambda e, psB=psB, c=c: e.tensor_copy(
                    sumsq[:, c * CH:(c + 1) * CH], psB[0:1, :]), reads=[pkB], writes=[R2k])
            self.o("V", lambda e: e.tensor_scalar_mul(sums, sums, 1.0 / D),
                   reads=[R1k], writes=[R1k])
            self.o("S", lambda e: e.activation(out=msq2, in_=sums, func=AF.Square),
                   reads=[R1k], writes=[R2k_m])
            self.o("V", lambda e: e.tensor_scalar_mul(sumsq, sumsq, 1.0 / D),
                   reads=[R2k], writes=[R2k])
            self.o("V", lambda e: e.tensor_sub(sumsq, sumsq, msq2),
                   reads=[R2k, R2k_m], writes=[R2k])
            self.o("S", lambda e: e.activation(out=sumsq, in_=sumsq, func=AF.Ln, bias=EPS_LN),
                   reads=[R2k, self.ck], writes=[R2k])
            self.o("S", lambda e: e.activation(out=c1, in_=sumsq, func=AF.Exp, scale=-0.5),
                   reads=[R2k], writes=[R1k_c1])
            self.o("V", lambda e: e.scalar_tensor_tensor(out=c2, in0=sums, scalar=-1.0,
                                                         in1=c1, op0=ALU.mult, op1=ALU.mult),
                   reads=[R1k, R1k_c1], writes=[R2k_c2])
            self.ctx.dma(scr_b[0:1, :], c1, reads=[R1k_c1], writes=[f"D:mscr{li}"])
            self.ctx.dma(scr_b[1:2, :], c2, reads=[R2k_c2], writes=[f"D:mscr{li}"])
            self.ctx.dma(c1b[:, :], scr_b[0:1, :].broadcast_to([128, T]),
                         reads=[f"D:mscr{li}"], writes=[c1bk], eng="P")
            self.ctx.dma(c2b[:, :], scr_b[1:2, :].broadcast_to([128, T]),
                         reads=[f"D:mscr{li}"], writes=[c2bk], eng="P")
            for j in range(NT):
                xa, xk = xfull[j % 2]
                self.o("V", lambda e, xa=xa: e.memset(xa[:, 0:1], 0.0), writes=[xk])
                self.ctx.dma(xa[:, 1:T + 1], self.d[gsrc][b, j * 128:(j + 1) * 128, :],
                             writes=[xk], reads=[f"D:{gsrc}"])
                off = 1 if j < 4 else 0
                self.o("V", lambda e, xa=xa, off=off: e.tensor_mul(
                    t1[:, :], xa[:, off:off + T], c1b[:, :]),
                    reads=[xk, c1bk], writes=[t1k])
                self.o("V", lambda e: e.tensor_add(t1[:, :], t1[:, :], c2b[:, :]),
                       reads=[t1k, c2bk], writes=[t1k])
                self.o("V", lambda e, j=j: e.tensor_scalar(
                    out=xmb[:, :], in0=t1[:, :], scalar1=lng[:, j:j + 1],
                    scalar2=lnb[:, j:j + 1], op0=ALU.mult, op1=ALU.add),
                    reads=[t1k, lngk, lnbk], writes=[xmbk])
                self.ctx.dma(self.d[xmdst][b, j * 128:(j + 1) * 128, :], xmb[:, :],
                             reads=[xmbk], writes=[f"D:{xmdst}"])

    def st_ffn_w1(self, li, xmsrc):
        A = self.A
        A.reset()
        w = self.load_w_ap_lhsT_bf16(self.d["L1_ffn_w1"][li], "D:L1_ffn_w1",
                                     D, 2 * HID, f"w1{li}", stg_n=1)
        b1a, b1ak = self.strip[f"b1a{li}"]
        b1g, b1gk = self.strip[f"b1g{li}"]
        xm = []
        for k in range(NT):
            ap, key = A.alloc(T, BF16, f"xm{k}")
            xm.append((ap, key))
        gl, glk = A.alloc(CH, F32, "gelu")
        ta, tak = A.alloc(CH, F32, "ta")
        hgm, hgmk = A.alloc(T, BF16, "hgm")
        for b in range(BL):
            for k in range(NT):
                ap, key = xm[k]
                self.ctx.dma(ap[:, :], self.d[xmsrc][b, k * 128:(k + 1) * 128, :],
                             writes=[key], reads=[f"D:{xmsrc}"])
            for m in range(NW1):
                mm = min(128, HID - m * 128)
                asl = slice(m * 128, m * 128 + mm)
                gsl = slice(HID + m * 128, HID + m * 128 + mm)
                for c in range(NCH):
                    sl = slice(c * CH, (c + 1) * CH)
                    psA, pkA = self.psum()
                    psG, pkG = self.psum()
                    self.mm_group(psA[:mm, :], pkA,
                                  [(w[k][0][:, asl], xm[k][0][:, sl]) for k in range(NT)],
                                  reads=[w[k][1] for k in range(NT)] + [x[1] for x in xm])
                    self.mm_group(psG[:mm, :], pkG,
                                  [(w[k][0][:, gsl], xm[k][0][:, sl]) for k in range(NT)],
                                  reads=[w[k][1] for k in range(NT)] + [x[1] for x in xm])
                    self.o("S", lambda e, psG=psG, m=m, mm=mm: e.activation(
                        out=gl[:mm, :], in_=psG[:mm, :], func=AF.Gelu,
                        bias=b1g[:mm, m:m + 1]),
                        reads=[pkG, b1gk], writes=[glk])
                    self.o("V", lambda e, psA=psA, m=m, mm=mm: e.tensor_scalar(
                        out=ta[:mm, :], in0=psA[:mm, :], scalar1=b1a[:mm, m:m + 1],
                        scalar2=None, op0=ALU.add),
                        reads=[pkA, b1ak], writes=[tak])
                    self.o("V", lambda e, sl=sl, mm=mm: e.tensor_mul(
                        hgm[:mm, sl], ta[:mm, :], gl[:mm, :]),
                        reads=[tak, glk], writes=[hgmk])
                self.ctx.dma(self.d[f"hgb{li}"][b, m * 128:m * 128 + mm, :], hgm[:mm, :],
                             reads=[hgmk], writes=[f"D:hgb{li}"])

    def st_ffn_w2(self, li, gsrc, fdst, also_bf16):
        A = self.A
        A.reset()
        w = self.load_w_ap_lhsT_bf16(self.d["L1_ffn_w2"][li], "D:L1_ffn_w2",
                                     HID, D, f"w2{li}")
        b2, b2k = self.strip[f"b2{li}"]
        hq = [A.alloc(CH, BF16, f"hgc{k}") for k in range(NW1)]
        gt, gtk = A.alloc(CH, F32, "gt2")
        f2, f2k = A.alloc(CH, F32, "f2t")
        f2b, f2bk = A.alloc(CH, BF16, "f2bt")
        for b in range(BL):
            for c in range(NCH):
                sl = slice(c * CH, (c + 1) * CH)
                for k in range(NW1):
                    kk = min(128, HID - k * 128)
                    ap, key = hq[k]
                    self.ctx.dma(ap[:kk, :], self.d[f"hgb{li}"][b, k * 128:k * 128 + kk, sl],
                                 writes=[key], reads=[f"D:hgb{li}"])
                for m in range(NT):
                    msl = slice(m * 128, (m + 1) * 128)
                    ps, pk = self.psum()
                    pairs = []
                    for k in range(NW1):
                        kk = min(128, HID - k * 128)
                        pairs.append((w[k][0][:kk, msl], hq[k][0][:kk, :]))
                    self.mm_group(ps[:, :], pk, pairs,
                                  reads=[w[k][1] for k in range(NW1)] + [x[1] for x in hq])
                    self.ctx.dma(gt[:, :], self.d[gsrc][b, msl, sl], writes=[gtk],
                                 reads=[f"D:{gsrc}"])
                    self.o("V", lambda e, ps=ps, m=m: e.tensor_scalar(
                        out=f2[:, :], in0=ps[:, :], scalar1=b2[:, m:m + 1],
                        scalar2=None, op0=ALU.add),
                        reads=[pk, b2k], writes=[f2k])
                    self.o("V", lambda e: e.tensor_add(f2[:, :], f2[:, :], gt[:, :]),
                           reads=[f2k, gtk], writes=[f2k])
                    self.ctx.dma(self.d[fdst][b, msl, sl], f2[:, :], reads=[f2k],
                                 writes=[f"D:{fdst}"])
                    if also_bf16:
                        self.o("V", lambda e: e.tensor_copy(f2b[:, :], f2[:, :]),
                               reads=[f2k], writes=[f2bk])
                        self.ctx.dma(self.d[fdst + "b"][b, msl, sl], f2b[:, :],
                                     reads=[f2bk], writes=[f"D:{fdst}b"])

    def st_final(self):
        A = self.A
        A.reset()
        x1t, x1k = A.alloc(T, F32, "fx1")
        att, attk = A.alloc(T, BF16, "fatt")
        f1t, f1k = A.alloc(T, F32, "ff1")
        ot, otk = A.alloc(T, F32, "fout")
        for b in range(BL):
            for j in range(NT):
                msl = slice(j * 128, (j + 1) * 128)
                self.ctx.dma(x1t[:, :], self.d["x1"][b, msl, :], writes=[x1k], reads=["D:x1"])
                self.ctx.dma(att[:, :], self.d["attn1"][b, msl, :], writes=[attk],
                             reads=["D:attn1"])
                self.ctx.dma(f1t[:, :], self.d["f1"][b, msl, :], writes=[f1k], reads=["D:f1"])
                self.o("V", lambda e: e.tensor_add(ot[:, :], x1t[:, :], f1t[:, :]),
                       reads=[x1k, f1k], writes=[otk])
                self.o("V", lambda e: e.tensor_add(ot[:, :], ot[:, :], att[:, :]),
                       reads=[otk, attk], writes=[otk])
                self.ctx.dma(self.d["out"][b, msl, :], ot[:, :], reads=[otk], writes=["D:out"])


B1B_STAGES = ["prep2",
              "qkv0", "attn0", "wo0", "lnm0", "w10", "w20",
              "qkv1", "attn1s", "wo1", "lnm1", "w11", "w21", "final"]


def declare_block1b(mk):
    for nm, shape in [("L1_ffn_wqkv", [2, D, 3 * INNER]), ("L1_ffn_wo", [2, INNER, D]),
                      ("L1_ffn_w1", [2, D, 2 * HID]), ("L1_ffn_w2", [2, HID, D]),
                      ("c_csq2", [128, T]), ("c_ssq2", [128, T]),
                      ("c_csk2", [128, T]), ("c_ssk2", [128, T]),
                      ("c_mnd", [128, 128]), ("c_mnf", [128, 128]),
                      ("c_md2", [128, 128]), ("c_mf2", [128, 128]),
                      ("qs_dup_0", [128, 1]), ("qsrot_dup_0", [128, 1]),
                      ("ks_dup_0", [128, 1]), ("ksrot_dup_0", [128, 1]),
                      ("qs_dup_1", [128, 1]), ("qsrot_dup_1", [128, 1]),
                      ("ks_dup_1", [128, 1]), ("ksrot_dup_1", [128, 1]),
                      ("lng0_s", [128, NT]), ("lnb0_s", [128, NT]),
                      ("lng1_s", [128, NT]), ("lnb1_s", [128, NT]),
                      ("b1a0_s", [128, NW1]), ("b1g0_s", [128, NW1]),
                      ("b1a1_s", [128, NW1]), ("b1g1_s", [128, NW1]),
                      ("b20_s", [128, NT]), ("b21_s", [128, NT])]:
        mk.dram_in(nm, shape)
    for li in range(2):
        mk.dram_scratch(f"qraw{li}", [BL, INNER, T])
        mk.dram_scratch(f"kraw{li}", [BL, INNER, T])
        mk.dram_scratch(f"qb{li}", [BL, INNER, T], BF16)
        mk.dram_scratch(f"kb{li}", [BL, INNER, T], BF16)
        mk.dram_scratch(f"vT{li}", [BL, NIN, T, 128], BF16)
        mk.dram_scratch(f"oraw{li}", [BL, INNER, T])
        mk.dram_scratch(f"rsum{li}", [BL, 8, T])
        mk.dram_scratch(f"g{li}", [BL, D, T])
        mk.dram_scratch(f"g{li}b", [BL, D, T], BF16)
        mk.dram_scratch(f"xm{li}", [BL, D, T], BF16)
        mk.dram_scratch(f"hgb{li}", [BL, HID, T], BF16)
    mk.dram_scratch("rmx", [BL, 8, T])
    mk.dram_scratch("f0", [BL, D, T])
    mk.dram_scratch("f0b", [BL, D, T], BF16)
    mk.dram_scratch("f1", [BL, D, T])
    mk.dram_out("out", [BL, D, T])


def emit_block1b(mk, upto=None, skip=()):
    stages = B1B_STAGES if upto is None else B1B_STAGES[:B1B_STAGES.index(upto) + 1]
    for s in stages:
        if s in skip:
            continue
        if s == "prep2":
            if "g64" not in mk.strip:
                mk.load_strip("g64", "c_g64", 2)
            for li in range(2):
                mk.load_strip(f"qs_dup_{li}", f"qs_dup_{li}", 1)
                mk.load_strip(f"qsrot_dup_{li}", f"qsrot_dup_{li}", 1)
                mk.load_strip(f"ks_dup_{li}", f"ks_dup_{li}", 1)
                mk.load_strip(f"ksrot_dup_{li}", f"ksrot_dup_{li}", 1)
                mk.load_strip(f"lng{li}", f"lng{li}_s", NT)
                mk.load_strip(f"lnb{li}", f"lnb{li}_s", NT)
                mk.load_strip(f"b1a{li}", f"b1a{li}_s", NW1)
                mk.load_strip(f"b1g{li}", f"b1g{li}_s", NW1)
                mk.load_strip(f"b2{li}", f"b2{li}_s", NT)
        elif s == "qkv0":
            mk.st_ffn_qkv(0, "xn1b")
        elif s == "attn0":
            mk.st_ffn_attn(0)
        elif s == "wo0":
            mk.st_ffn_wo(0, "xn1f", "g0")
        elif s == "lnm0":
            mk.st_ffn_lnmid(0, "g0", "xm0")
        elif s == "w10":
            mk.st_ffn_w1(0, "xm0")
        elif s == "w20":
            mk.st_ffn_w2(0, "g0", "f0", True)
        elif s == "qkv1":
            mk.st_ffn_qkv(1, "f0b")
        elif s == "attn1s":
            mk.st_ffn_attn(1)
        elif s == "wo1":
            mk.st_ffn_wo(1, "f0", "g1")
        elif s == "lnm1":
            mk.st_ffn_lnmid(1, "g1", "xm1")
        elif s == "w11":
            mk.st_ffn_w1(1, "xm1")
        elif s == "w21":
            mk.st_ffn_w2(1, "g1", "f1", False)
        elif s == "final":
            mk.st_final()


def host_prep_block1b(inp, Tcur):
    DIM_HEAD = 64
    WINDOW = 512
    half_idx = np.arange(0, DIM_HEAD, 2, dtype=np.float64)
    inv_freq = 1.0 / 10000.0 ** (half_idx / DIM_HEAD)
    t = np.arange(Tcur, dtype=np.float64)
    freqs = t[:, None] * inv_freq[None, :]
    freqs = np.concatenate([freqs, freqs], axis=-1)
    base = (half_idx + 0.4 * DIM_HEAD) / (1.4 * DIM_HEAD)
    power = (t - Tcur // 2) / float(WINDOW // 2)
    scale = base[None, :] ** power[:, None]
    scale = np.concatenate([scale, scale], axis=-1)
    cos, sin = np.cos(freqs), np.sin(freqs)
    sgn = np.ones((1, 64)); sgn[0, :32] = -1.0
    csq = (cos * scale).T
    ssq = (sin * scale).T * sgn.T
    csk = (cos / scale).T
    ssk = (sin / scale).T * sgn.T
    ii = np.arange(128)
    mnd = np.where(ii[:, None] >= ii[None, :], 0.0, -1e30)
    mnf = np.where(ii[:, None] <= ii[None, :], 0.0, -1e30)
    md2 = (ii[None, :] >= ii[:, None]).astype(np.float32)
    mf2 = (ii[None, :] <= ii[:, None]).astype(np.float32)
    rot = (np.arange(64) + 32) % 64
    m = {
        "L1_ffn_wqkv": inp["L1_ffn_wqkv"], "L1_ffn_wo": inp["L1_ffn_wo"],
        "L1_ffn_w1": inp["L1_ffn_w1"], "L1_ffn_w2": inp["L1_ffn_w2"],
        "c_csq2": np.tile(csq, (2, 1)), "c_ssq2": np.tile(ssq, (2, 1)),
        "c_csk2": np.tile(csk, (2, 1)), "c_ssk2": np.tile(ssk, (2, 1)),
        "c_mnd": mnd, "c_mnf": mnf, "c_md2": md2, "c_mf2": mf2,
    }
    qsc = np.asarray(inp["L1_ffn_q_scale"], np.float32)
    ksc = np.asarray(inp["L1_ffn_k_scale"], np.float32)
    for li in range(2):
        m[f"qs_dup_{li}"] = np.tile(qsc[li], 2).reshape(128, 1)
        m[f"qsrot_dup_{li}"] = np.tile(qsc[li][rot], 2).reshape(128, 1)
        m[f"ks_dup_{li}"] = np.tile(ksc[li], 2).reshape(128, 1)
        m[f"ksrot_dup_{li}"] = np.tile(ksc[li][rot], 2).reshape(128, 1)
        m[f"lng{li}_s"] = vec_strip(np.asarray(inp["L1_ffn_ln_g"])[li], NT)
        m[f"lnb{li}_s"] = vec_strip(np.asarray(inp["L1_ffn_ln_b"])[li], NT)
        b1 = np.asarray(inp["L1_ffn_b1"], np.float32)[li]
        m[f"b1a{li}_s"] = vec_strip(b1[0:HID], NW1)
        m[f"b1g{li}_s"] = vec_strip(b1[HID:2 * HID], NW1)
        m[f"b2{li}_s"] = vec_strip(np.asarray(inp["L1_ffn_b2"])[li], NT)
    return {k: np.ascontiguousarray(np.asarray(v, np.float32)) for k, v in m.items()}


def declare_full(mk):
    declare_block0(mk)
    declare_block1a(mk)
    declare_block1b(mk)


def emit_full(mk):
    mk.setup_consts([EPS_LN, PI / 2, 1.0, 1e-24])
    emit_block0(mk, consts=False)
    emit_block1a(mk)
    emit_block1b(mk)


def host_prep_full(inp, b_sl, Tcur):
    m = host_prep_block0(inp, b_sl, Tcur)
    m.update(host_prep_block1a(inp, Tcur))
    m.update(host_prep_block1b(inp, Tcur))
    g64 = np.zeros((128, 2), np.float32)
    g64[0:64, 0] = 1.0
    g64[64:128, 1] = 1.0
    m["c_g64"] = g64
    return m


from concourse.bass_utils import run_bass_kernel_spmd

N_CORES = 8


def _host_base(inputs):
    inp = {k: np.asarray(v) for k, v in inputs.items()}
    m = host_prep_block0(inp, None, T)
    m.update(host_prep_block1a(inp, T))
    m.update(host_prep_block1b(inp, T))
    g64 = np.zeros((128, 2), np.float32)
    g64[0:64, 0] = 1.0
    g64[64:128, 1] = 1.0
    m["c_g64"] = g64
    return m


_BUILD_CACHE = {}


def _build():
    if "nc" not in _BUILD_CACHE:
        nc = bass.Bass("TRN2", target_bir_lowering=False, debug=False)
        m = MK(nc)
        declare_full(m)
        emit_full(m)
        m.ctx.barrier_end()
        m.ctx.finish()
        _BUILD_CACHE["nc"] = nc
    return _BUILD_CACHE["nc"]


def kernel(**inputs):
    nc = _build()
    base = _host_base(inputs)
    x = np.asarray(inputs["L0_attn_x"], np.float32)
    in_maps = []
    for c in range(N_CORES):
        mm = dict(base)
        mm["x"] = np.ascontiguousarray(x[2 * c:2 * c + 2].transpose(0, 2, 1))
        in_maps.append(mm)
    res = run_bass_kernel_spmd(nc, in_maps, core_ids=list(range(N_CORES)))
    outs = [np.asarray(r["out"], np.float32).transpose(0, 2, 1)
            for r in res.results]
    return np.ascontiguousarray(np.concatenate(outs, axis=0))


# version 08
# speedup vs baseline: 1.1525x; 1.0031x over previous
import concourse.bass as bass
import concourse.mybir as mybir

ENGS = ["T", "V", "S", "P", "IO"]
SECTION = {"T": "tensor", "V": "vector", "S": "scalar", "P": "gpsimd", "IO": "sync"}
DMA_POOL = 48


class Ctx:
    def __init__(self, nc: bass.Bass, dma_pool=DMA_POOL):
        self.nc = nc
        self.sems = {e: nc.alloc_semaphore(f"sem_{e}") for e in ["T", "V", "S", "P"]}
        self.count = {e: 0 for e in ["T", "V", "S", "P"]}
        self.dma_pool_n = dma_pool
        self.dma_sw_n = 24
        tot = dma_pool + self.dma_sw_n
        self.dma_sems = [nc.alloc_semaphore(f"dma{i}") for i in range(tot)]
        self.dma_target = [0] * tot
        self.dma_waited = [0] * tot
        self.dma_rr = 0
        self.dma_rr_sw = dma_pool
        self.clock = {e: {} for e in ENGS}
        self.streams = {e: [] for e in ENGS}
        self.bufs = {}
        self.nops = 0

    def _buf(self, key):
        if key not in self.bufs:
            self.bufs[key] = {"w": None, "r": [], "ww": []}
        return self.bufs[key]

    def _need_edges(self, eng, reads, writes, is_dma=False):
        need = {}
        def add(src, val):
            if src == "T" == eng:
                return
            if val <= self.clock[eng].get(src, 0):
                return
            need[src] = max(need.get(src, 0), val)
        for key in reads:
            b = self._buf(key)
            if b["w"] is not None:
                add(*b["w"])
            for (src, val) in b["ww"]:
                add(src, val)
        for key in writes:
            b = self._buf(key)
            if b["w"] is not None:
                wsrc = b["w"][0]
                if not (is_dma and isinstance(wsrc, tuple)):
                    add(*b["w"])
                    for (src, val) in b["ww"]:
                        add(src, val)
            for (src, val) in b["r"]:
                add(src, val)
        return need

    def _emit_waits(self, eng, need):
        for src, val in sorted(need.items(), key=lambda kv: str(kv[0])):
            self.streams[eng].append(("wait", (src, val)))
            self.clock[eng][src] = val
            if isinstance(src, tuple):
                idx = src[1]
                self.dma_waited[idx] = max(self.dma_waited[idx], val)

    def op(self, eng, fn, reads=(), writes=(), tag=""):
        assert eng in ("T", "V", "S", "P")
        need = self._need_edges(eng, reads, writes)
        self._emit_waits(eng, need)
        self.count[eng] += 1
        myval = self.count[eng]
        self.streams[eng].append(("op", (fn, eng, myval, tag)))
        self.nops += 1
        for key in writes:
            b = self._buf(key)
            b["w"] = (eng, myval)
            b["r"] = []
            b["ww"] = []
        for key in reads:
            self._buf(key)["r"].append((eng, myval))
        return myval

    def dma(self, dst_ap, src_ap, reads=(), writes=(), eng=None, tag="", allow_slow=False):
        if eng is None:
            try:
                is_load = (str(src_ap.space).endswith("DRAM")
                           and str(dst_ap.space).endswith("SBUF"))
            except Exception:
                is_load = True
            eng = "IO" if is_load else "P"
        need = self._need_edges(eng, reads, writes, is_dma=True)
        self._emit_waits(eng, need)
        if eng == "P":
            idx = self.dma_rr_sw
            self.dma_rr_sw = self.dma_pool_n + ((idx - self.dma_pool_n + 1) % self.dma_sw_n)
        else:
            idx = self.dma_rr
            self.dma_rr = (self.dma_rr + 1) % self.dma_pool_n
        if self.clock[eng].get(("D", idx), 0) < self.dma_target[idx]:
            self._emit_waits(eng, {("D", idx): self.dma_target[idx]})
        self.dma_target[idx] += 16
        target = self.dma_target[idx]
        sem = self.dma_sems[idx]

        def fn(e, dst_ap=dst_ap, src_ap=src_ap, sem=sem):
            kw = dict(allow_slow_non_contiguous=True) if allow_slow else {}
            return e.dma_start(out=dst_ap, in_=src_ap, **kw)

        self.streams[eng].append(("dma", (fn, sem, tag)))
        self.nops += 1
        src = ("D", idx)
        for key in writes:
            b = self._buf(key)
            prev = b["w"]
            b["w"] = (src, target)
            if prev is not None and isinstance(prev[0], tuple):
                b["ww"] = list(b["ww"]) + [prev]
            else:
                b["ww"] = []
            b["r"] = []
        for key in reads:
            self._buf(key)["r"].append((src, target))
        return (src, target)

    def barrier_end(self):
        need = {}
        for e in ["T", "V", "S", "P"]:
            if self.count[e] > self.clock["IO"].get(e, 0):
                need[e] = self.count[e]
        for i in range(len(self.dma_sems)):
            if self.dma_target[i] > self.clock["IO"].get(("D", i), 0):
                need[("D", i)] = self.dma_target[i]
        self._emit_waits("IO", need)

    def finish(self):
        with self.nc.Block() as block:
            for eng in ENGS:
                stream = self.streams[eng]
                if not stream:
                    continue
                section = getattr(block, SECTION[eng])

                def make_body(stream=stream, eng=eng):
                    def body(engine_obj):
                        for kind, payload in stream:
                            if kind == "wait":
                                src, val = payload
                                sem = (self.dma_sems[src[1]] if isinstance(src, tuple)
                                       else self.sems[src])
                                engine_obj.wait_ge(sem, val)
                            elif kind == "dma":
                                fn, sem, tag = payload
                                fn(engine_obj).then_inc(sem, 16)
                            else:
                                fn, e, myval, tag = payload
                                fn(engine_obj).then_inc(self.sems[e], 1)
                    return body

                section(make_body())


"""Megakernel stage library for nn_mk_prb_d2 (per-core program builder).

Layout conventions:
- activations channel-major in HBM scratch: [BL, C, T] (BL=2 local batches)
- host passes x pre-transposed to [BL, 1024, T]; output written [BL, 1024, T]
- per-channel vectors host-reshaped to [128, n_tiles] (column j = tile j)
- weights passed in natural [K, M] (= lhsT-ready) layout except fp4 ffn
  weights which are quantized+transposed on device.
"""
import numpy as np
import concourse.bass as bass
import concourse.mybir as mybir

F32 = mybir.dt.float32
BF16 = mybir.dt.bfloat16
AF = mybir.ActivationFunctionType
ALU = mybir.AluOpType
AX = mybir.AxisListType

BL = 2
T = 4096
D = 1024
NT = 8
FFI = 2816
NFF = 22
CH = 512
NCH = T // CH


def setcfg(t=4096, bl=2):
    global T, BL, NCH
    T, BL, NCH = t, bl, t // CH
MAGIC = float(np.float32(1.5 * 2 ** 23))
EPS_LN = 1e-5


def ceil_div(a, b):
    return (a + b - 1) // b


class Arena:
    def __init__(self, ctx, tile, name, base, cols):
        self.ctx, self.t, self.name = ctx, tile, name
        self.base, self.cols = base, cols
        self.cur = base
        self.entries = []
        self.kid = 0
        self.stage = 0

    def alloc(self, elems, dtype=F32, tag=""):
        dsz = 2 if dtype == BF16 else 4
        fcols = ceil_div(elems * dsz, 4)
        assert fcols <= self.cols, f"arena {self.name}: {tag} too big"
        start = (self.cur + 3) & ~3
        if start + fcols > self.base + self.cols:
            start = self.base
        end = start + fcols
        key = f"{self.name}{self.kid}_{tag}"
        self.kid += 1
        olap = [e for e in self.entries if e[0] < end and e[1] > start]
        for e in olap:
            assert e[3] < self.stage, \
                f"arena ring lapped a live same-stage buffer: {e[2]} by {tag}"
        if olap:
            self.ctx.merge_keys(key, [k for (_, _, k, _) in olap])
        keep = []
        for (s, e, k, st) in self.entries:
            if s < end and e > start:
                if s < start:
                    keep.append((s, start, k, st))
                if e > end:
                    keep.append((end, e, k, st))
            else:
                keep.append((s, e, k, st))
        keep.append((start, end, key, self.stage))
        self.entries = keep
        self.cur = end
        ap = self.t[:, start:end]
        if dtype == BF16:
            ap = ap.bitcast(BF16)
        ap = ap[:, :elems]
        return ap, key

    def reset(self):
        self.stage += 1


class MK:
    def __init__(self, nc, externalize=(), feed=()):
        self.nc = nc
        self.feed = set(feed)
        self.ctx = Ctx(nc)
        def merge_keys(new, olds):
            b = self.ctx._buf(new)
            rs = list(b["r"])
            for k in olds:
                ob = self.ctx._buf(k)
                if ob["w"] is not None:
                    rs.append(ob["w"])
                rs.extend(ob["r"])
                rs.extend(ob["ww"])
            b["w"] = None
            b["r"] = rs
        self.ctx.merge_keys = merge_keys
        self.externalize = set(externalize)
        self.d = {}
        POOL = 53000
        self.pool = nc.alloc_sbuf_tensor("pool", [128, POOL], F32)
        self.P = Arena(self.ctx, self.pool, "P", 0, 2000)
        self.A = Arena(self.ctx, self.pool, "A", 2000, POOL - 2000)
        self.ps = [nc.alloc_psum_tensor(f"ps{i}", [128, 512], F32) for i in range(8)]
        self.ps_keys = [f"ps{i}" for i in range(8)]
        self.ps_rr = 0
        self.strip = {}
        self.cvals = {}
        self.cpool = None

    def dram_in(self, name, shape, dtype=F32):
        ap = self.nc.dram_tensor(name, list(shape), dtype, kind="ExternalInput").ap()
        self.d[name] = ap
        return ap

    def dram_scratch(self, name, shape, dtype=F32):
        kind = ("ExternalInput" if name in self.feed else
                "ExternalOutput" if name in self.externalize else "Internal")
        ap = self.nc.dram_tensor(name, list(shape), dtype, kind=kind).ap()
        self.d[name] = ap
        return ap

    def dram_out(self, name, shape, dtype=F32):
        ap = self.nc.dram_tensor(name, list(shape), dtype, kind="ExternalOutput").ap()
        self.d[name] = ap
        return ap

    def psum(self):
        i = self.ps_rr
        self.ps_rr = (self.ps_rr + 1) % 8
        return self.ps[i], self.ps_keys[i]

    def setup_consts(self, values):
        vals = sorted(set(float(v) for v in values))
        self.cpool, ck = self.P.alloc(len(vals), F32, "cpool")
        for j, v in enumerate(vals):
            self.cvals[v] = j
            self.nc.const_aps.register(F32, v, self.cpool[:, j:j + 1])
            v32 = float(np.float32(v))
            if v32 != v:
                self.nc.const_aps.register(F32, v32, self.cpool[:, j:j + 1])
            self.ctx.op("V", lambda e, j=j, v=v: e.memset(self.cpool[:, j:j + 1], v),
                        writes=["cpool"])
        self.ck = "cpool"

    def o(self, eng, fn, reads=(), writes=(), tag=""):
        return self.ctx.op(eng, fn, reads=reads, writes=writes, tag=tag)

    def load_strip(self, name, dname, cols, cast_bf16=False):
        ap, key = self.P.alloc(cols, F32, name)
        rows = self.d[dname].shape[0]
        self.ctx.dma(ap[:rows, :], self.d[dname], writes=[key])
        self.strip[name] = (ap, key)
        return ap, key

    def mm_group(self, ps_ap, ps_key, pairs, reads, tag=""):
        def fn(e):
            last = None
            n = len(pairs)
            for i, (lt, rh) in enumerate(pairs):
                last = e.matmul(ps_ap, lt, rh, start=(i == 0), stop=(i == n - 1))
            return last
        self.o("T", fn, reads=reads, writes=[ps_key], tag=tag)

    def load_w_lhsT_bf16(self, dram_name, K, M, tag, stg_n=2):
        ntk = ceil_div(K, 128)
        stg = [self.A.alloc(M, F32, f"{tag}wstg{i}") for i in range(stg_n)]
        out = []
        for k in range(ntk):
            kk = min(128, K - k * 128)
            f, fk = stg[k % stg_n]
            b, bk = self.A.alloc(M, BF16, f"{tag}wb{k}")
            self.ctx.dma(f[:kk, :], self.d[dram_name][k * 128:k * 128 + kk, :], writes=[fk])
            self.o("V", lambda e, b=b, f=f, kk=kk: e.tensor_copy(b[:kk, :], f[:kk, :]),
                   reads=[fk], writes=[bk])
            out.append((b, bk, kk))
        return out

    def st_prep(self):
        P = self.P
        self.load_strip("b_in", "L0_attn_b_in", NT)
        self.load_strip("B_vec", "L0_attn_B_vec", NT)
        self.load_strip("log_A", "L0_attn_log_A", NT)
        self.load_strip("Dv", "L0_attn_D", NT)
        a_ap, a_key = P.alloc(NT, F32, "Astrip")
        la, lak = self.strip["log_A"]
        self.o("S", lambda e: e.activation(out=a_ap[:, :], in_=la[:, :], func=AF.Exp),
               reads=[lak], writes=[a_key])
        self.o("V", lambda e: e.tensor_scalar_mul(a_ap[:, :], a_ap[:, :], -1.0),
               reads=[a_key], writes=[a_key])
        self.strip["A"] = (a_ap, a_key)

    def st_ln(self, src, dstf, dstb, tag):
        A = self.A
        A.reset()
        onesw, onesk = A.alloc(2, F32, "ones1")
        self.o("V", lambda e: e.memset(onesw[:, 0:1], 1.0), writes=[onesk])
        xch = [A.alloc(CH, F32, f"xch{i}") for i in range(2)]
        sqch = [A.alloc(CH, F32, f"sqch{i}") for i in range(2)]
        R1, R1k = A.alloc(T, F32, "lnR1")
        R2, R2k = A.alloc(T, F32, "lnR2")
        R3, R3k = A.alloc(T, F32, "lnR3")
        sums, c1 = R1[0:1, :], R3[0:1, :]
        sumsq, c2, msq = R2[0:1, :], R3[0:1, 0:T], R2[0:1, 0:T]
        R4, R4k = A.alloc(T, F32, "lnR4")
        R5, R5k = A.alloc(T, F32, "lnR5")
        c1, c2, msq = R3[0:1, :], R4[0:1, :], R5[0:1, :]
        R1k_c1, R2k_c2, R2k_m = R3k, R4k, R5k
        c1b, c1bk = A.alloc(T, F32, "c1b")
        c2b, c2bk = A.alloc(T, F32, "c2b")
        xfull = [A.alloc(T, F32, f"xf{i}") for i in range(2)]
        xn, xnk = A.alloc(T, F32, "xn")
        xnb, xnbk = A.alloc(T, BF16, "xnb")
        scr_b = self.dram_scratch(f"lnscr_{tag}", [2, T], F32)
        for b in range(BL):
            for c in range(NCH):
                psA, pkA = self.psum()
                psB, pkB = self.psum()
                sl = slice(c * CH, (c + 1) * CH)
                for j in range(NT):
                    xa, xk = xch[j % 2]
                    sa, sk = sqch[j % 2]
                    self.ctx.dma(xa[:, :], self.d[src][b, j * 128:(j + 1) * 128, sl],
                                 writes=[xk], reads=[f"D:{src}"])
                    self.o("S", lambda e, sa=sa, xa=xa: e.activation(
                        out=sa[:, :], in_=xa[:, :], func=AF.Square),
                        reads=[xk], writes=[sk])
                    self.o("T", lambda e, psA=psA, xa=xa, j=j: e.matmul(
                        psA[0:1, :], onesw[:, 0:1], xa[:, :],
                        start=(j == 0), stop=(j == NT - 1)),
                        reads=[onesk, xk], writes=[pkA])
                    self.o("T", lambda e, psB=psB, sa=sa, j=j: e.matmul(
                        psB[0:1, :], onesw[:, 0:1], sa[:, :],
                        start=(j == 0), stop=(j == NT - 1)),
                        reads=[onesk, sk], writes=[pkB])
                self.o("V", lambda e, psA=psA, sl=sl: e.tensor_copy(sums[:, sl], psA[0:1, :]),
                       reads=[pkA], writes=[R1k])
                self.o("V", lambda e, psB=psB, sl=sl: e.tensor_copy(sumsq[:, sl], psB[0:1, :]),
                       reads=[pkB], writes=[R2k])
            self.o("V", lambda e: e.tensor_scalar_mul(sums, sums, 1.0 / D),
                   reads=[R1k], writes=[R1k])
            self.o("S", lambda e: e.activation(out=msq, in_=sums, func=AF.Square),
                   reads=[R1k], writes=[R2k_m])
            self.o("V", lambda e: e.tensor_scalar_mul(sumsq, sumsq, 1.0 / D),
                   reads=[R2k], writes=[R2k])
            self.o("V", lambda e: e.tensor_sub(sumsq, sumsq, msq),
                   reads=[R2k, R2k_m], writes=[R2k])
            self.o("S", lambda e: e.activation(out=sumsq, in_=sumsq, func=AF.Ln, bias=EPS_LN),
                   reads=[R2k, self.ck], writes=[R2k])
            self.o("S", lambda e: e.activation(out=c1, in_=sumsq, func=AF.Exp, scale=-0.5),
                   reads=[R2k], writes=[R1k_c1])
            self.o("V", lambda e: e.scalar_tensor_tensor(out=c2, in0=sums, scalar=-1.0,
                                                         in1=c1, op0=ALU.mult, op1=ALU.mult),
                   reads=[R1k, R1k_c1], writes=[R2k_c2])
            self.ctx.dma(scr_b[0:1, :], c1, reads=[R1k_c1], writes=["lnscr" + tag])
            self.ctx.dma(scr_b[1:2, :], c2, reads=[R2k_c2], writes=["lnscr" + tag])
            self.ctx.dma(c1b[:, :], scr_b[0:1, :].broadcast_to([128, T]),
                         reads=["lnscr" + tag], writes=[c1bk], eng="P")
            self.ctx.dma(c2b[:, :], scr_b[1:2, :].broadcast_to([128, T]),
                         reads=["lnscr" + tag], writes=[c2bk], eng="P")
            for j in range(NT):
                xa, xk = xfull[j % 2]
                self.ctx.dma(xa[:, :], self.d[src][b, j * 128:(j + 1) * 128, :],
                             writes=[xk], reads=[f"D:{src}"])
                self.o("V", lambda e, xa=xa: e.tensor_mul(xn[:, :], xa[:, :], c1b[:, :]),
                       reads=[xk, c1bk], writes=[xnk])
                self.o("V", lambda e: e.tensor_add(xn[:, :], xn[:, :], c2b[:, :]),
                       reads=[xnk, c2bk], writes=[xnk])
                if dstb is not None:
                    self.o("V", lambda e: e.tensor_copy(xnb[:, :], xn[:, :]),
                           reads=[xnk], writes=[xnbk])
                    self.ctx.dma(self.d[dstb][b, j * 128:(j + 1) * 128, :], xnb[:, :],
                                 reads=[xnbk], writes=[f"D:{dstb}"])
                if dstf is not None:
                    self.ctx.dma(self.d[dstf][b, j * 128:(j + 1) * 128, :], xn[:, :],
                                 reads=[xnk], writes=[f"D:{dstf}"])

    def st_l0scan(self):
        A = self.A
        A.reset()
        w = self.load_w_lhsT_bf16("L0_attn_W_in", D, D, "win")
        xnb = []
        for j in range(NT):
            t_ap, t_key = A.alloc(T, BF16, f"xnb{j}")
            xnb.append((t_ap, t_key))
        ub, ubk = A.alloc(T, F32, "ub")
        hb, hbk = A.alloc(T, BF16, "hb")
        b_in, b_in_k = self.strip["b_in"]
        Bv, Bv_k = self.strip["B_vec"]
        Ast, Ast_k = self.strip["A"]
        for b in range(BL):
            for j in range(NT):
                t_ap, t_key = xnb[j]
                self.ctx.dma(t_ap[:, :], self.d["xn0b"][b, j * 128:(j + 1) * 128, :],
                             writes=[t_key], reads=["D:xn0b"])
            for j in range(NT):
                for c in range(NCH):
                    ps, pk = self.psum()
                    sl = slice(c * CH, (c + 1) * CH)
                    self.mm_group(ps[:, :], pk,
                                  [(w[k][0][:, j * 128:(j + 1) * 128], xnb[k][0][:, sl])
                                   for k in range(NT)],
                                  reads=[w[k][1] for k in range(NT)] + [xnb[k][1] for k in range(NT)],
                                  tag=f"u{b}{j}{c}")
                    self.o("V", lambda e, ps=ps, sl=sl, j=j: e.tensor_scalar(
                        out=ub[:, sl], in0=ps[:, :], scalar1=b_in[:, j:j + 1],
                        scalar2=Bv[:, j:j + 1], op0=ALU.add, op1=ALU.mult),
                        reads=[pk, b_in_k, Bv_k], writes=[ubk])
                self.o("V", lambda e, j=j: e.tensor_tensor_scan(
                    out=hb[:, :], data0=Ast[:, j:j + 1].broadcast_to([128, T]),
                    data1=ub[:, :], initial=0.0, op0=ALU.mult, op1=ALU.add),
                    reads=[ubk, Ast_k], writes=[hbk])
                self.ctx.dma(self.d["h0b"][b, j * 128:(j + 1) * 128, :], hb[:, :],
                             reads=[hbk], writes=["D:h0b"])

    def st_l0y(self):
        A = self.A
        A.reset()
        w = self.load_w_lhsT_bf16("L0_attn_C", D, D, "C")
        hbt = []
        for j in range(NT):
            t_ap, t_key = A.alloc(T, BF16, f"hb{j}")
            hbt.append((t_ap, t_key))
        xnbt, xnbk = A.alloc(T, BF16, "xnbt")
        xf, xfk = A.alloc(T, F32, "xf")
        acc, acck = A.alloc(T, F32, "acc")
        Dv, Dv_k = self.strip["Dv"]
        for b in range(BL):
            for j in range(NT):
                t_ap, t_key = hbt[j]
                self.ctx.dma(t_ap[:, :], self.d["h0b"][b, j * 128:(j + 1) * 128, :],
                             writes=[t_key], reads=["D:h0b"])
            for j in range(NT):
                self.ctx.dma(xnbt[:, :], self.d["xn0b"][b, j * 128:(j + 1) * 128, :],
                             writes=[xnbk], reads=["D:xn0b"])
                self.ctx.dma(xf[:, :], self.d["x"][b, j * 128:(j + 1) * 128, :],
                             writes=[xfk], reads=["D:x"])
                for c in range(NCH):
                    ps, pk = self.psum()
                    sl = slice(c * CH, (c + 1) * CH)
                    self.mm_group(ps[:, :], pk,
                                  [(w[k][0][:, j * 128:(j + 1) * 128], hbt[k][0][:, sl])
                                   for k in range(NT)],
                                  reads=[w[k][1] for k in range(NT)] + [hbt[k][1] for k in range(NT)],
                                  tag=f"y{b}{j}{c}")
                    self.o("V", lambda e, ps=ps, sl=sl, j=j: e.scalar_tensor_tensor(
                        out=acc[:, sl], in0=xnbt[:, sl], scalar=Dv[:, j:j + 1],
                        in1=ps[:, :], op0=ALU.mult, op1=ALU.add),
                        reads=[pk, xnbk, Dv_k], writes=[acck])
                self.o("V", lambda e: e.tensor_add(acc[:, :], acc[:, :], xf[:, :]),
                       reads=[acck, xfk], writes=[acck])
                self.ctx.dma(self.d["a0"][b, j * 128:(j + 1) * 128, :], acc[:, :],
                             reads=[acck], writes=["D:a0"])

    def emit_qdq(self, xT, xT_key, out_bf, out_key, Cfree, scr, pool_ops=False):
        nb = Cfree // 16
        x3 = xT[:, :Cfree].rearrange("p (b k) -> p b k", k=16)
        (bm, bmk), (sc, sck), (inv, invk), (t, tk), (a, ak), (sg, sgk), \
            (r1, r1k), (r2, r2k) = scr
        o = self.o
        o("V", lambda e: e.tensor_reduce(out=bm[:, :nb], in_=x3, axis=AX.X,
                                         op=ALU.max, apply_absolute_value=True),
          reads=[xT_key], writes=[bmk])
        o("V", lambda e: e.tensor_scalar(out=sc[:, :nb], in0=bm[:, :nb], scalar1=1.0 / 6.0,
                                         scalar2=1e-12, op0=ALU.mult, op1=ALU.max),
          reads=[bmk], writes=[sck])
        o("V", lambda e: e.reciprocal(inv[:, :nb], sc[:, :nb]), reads=[sck], writes=[invk])
        invr = inv[:, :nb].unsqueeze(2).broadcast_to([128, nb, 16])
        scr_r = sc[:, :nb].unsqueeze(2).broadcast_to([128, nb, 16])
        t3 = t[:, :Cfree].rearrange("p (b k) -> p b k", k=16)
        E1 = "P" if pool_ops else "V"
        o(E1, lambda e: e.tensor_tensor(out=t3, in0=x3, in1=invr, op=ALU.mult),
          reads=[xT_key, invk], writes=[tk])
        o("S", lambda e: e.activation(out=a[:, :Cfree], in_=t[:, :Cfree], func=AF.Abs),
          reads=[tk], writes=[ak])
        o("S", lambda e: e.activation(out=sg[:, :Cfree], in_=t[:, :Cfree], func=AF.Sign),
          reads=[tk], writes=[sgk])
        o("V", lambda e: e.tensor_scalar(out=r1[:, :Cfree], in0=a[:, :Cfree], scalar1=0.5,
                                         scalar2=4.0, op0=ALU.mult, op1=ALU.add),
          reads=[ak], writes=[r1k])
        o("V", lambda e: e.scalar_tensor_tensor(out=r1[:, :Cfree], in0=a[:, :Cfree], scalar=2.0,
                                                in1=r1[:, :Cfree], op0=ALU.mult, op1=ALU.min),
          reads=[ak, r1k], writes=[r1k])
        o("V", lambda e: e.scalar_tensor_tensor(out=r1[:, :Cfree], in0=a[:, :Cfree], scalar=2.0,
                                                in1=r1[:, :Cfree], op0=ALU.add, op1=ALU.min),
          reads=[ak, r1k], writes=[r1k])
        o("V", lambda e: e.tensor_scalar(out=r1[:, :Cfree], in0=r1[:, :Cfree], scalar1=MAGIC,
                                         scalar2=MAGIC, op0=ALU.add, op1=ALU.subtract),
          reads=[r1k], writes=[r1k])
        o("V", lambda e: e.tensor_scalar(out=r2[:, :Cfree], in0=r1[:, :Cfree], scalar1=0.5,
                                         scalar2=None, op0=ALU.mult),
          reads=[r1k], writes=[r2k])
        o("V", lambda e: e.scalar_tensor_tensor(out=r2[:, :Cfree], in0=r1[:, :Cfree], scalar=-2.0,
                                                in1=r2[:, :Cfree], op0=ALU.add, op1=ALU.max),
          reads=[r1k, r2k], writes=[r2k])
        o("V", lambda e: e.tensor_scalar(out=r1[:, :Cfree], in0=r1[:, :Cfree], scalar1=2.0,
                                         scalar2=8.0, op0=ALU.mult, op1=ALU.subtract),
          reads=[r1k], writes=[r1k])
        o("V", lambda e: e.tensor_max(r2[:, :Cfree], r2[:, :Cfree], r1[:, :Cfree]),
          reads=[r1k, r2k], writes=[r2k])
        o(E1, lambda e: e.tensor_mul(r2[:, :Cfree], r2[:, :Cfree], sg[:, :Cfree]),
          reads=[r2k, sgk], writes=[r2k])
        r23 = r2[:, :Cfree].rearrange("p (b k) -> p b k", k=16)
        ob3 = out_bf[:, :Cfree].rearrange("p (b k) -> p b k", k=16)
        o(E1, lambda e: e.tensor_tensor(out=ob3, in0=r23, in1=scr_r, op=ALU.mult),
          reads=[r2k, sck], writes=[out_key])

    def qdq_scratch(self, Cfree):
        A = self.A
        nb = Cfree // 16
        return [A.alloc(nb, F32, "bm"), A.alloc(nb, F32, "sc"), A.alloc(nb, F32, "inv"),
                A.alloc(Cfree, F32, "t"), A.alloc(Cfree, F32, "a"), A.alloc(Cfree, F32, "sg"),
                A.alloc(Cfree, F32, "r1"), A.alloc(Cfree, F32, "r2")]

    def st_quant_act(self, src, dst, C):
        A = self.A
        A.reset()
        ntc = C // 128
        idb, idbk = self.ident_bf16()
        sets = []
        for i in range(2):
            xin = A.alloc(128 * ntc, F32, f"qin{i}")
            xT = A.alloc(C, F32, f"xT{i}")
            qT = A.alloc(C, BF16, f"qT{i}")
            qcm = A.alloc(128 * ntc, BF16, f"qcm{i}")
            scr = self.qdq_scratch(C)
            sets.append((xin, xT, qT, qcm, scr))
        it = 0
        for b in range(BL):
            for tc in range(T // 128):
                (xin, xink), (xT, xTk), (qT, qTk), (qcm, qcmk), scr = sets[it % 2]
                it += 1
                tsl = slice(tc * 128, (tc + 1) * 128)
                for cj in range(ntc):
                    self.ctx.dma(xin[:, cj * 128:(cj + 1) * 128],
                                 self.d[src][b, cj * 128:(cj + 1) * 128, tsl],
                                 writes=[xink], reads=[f"D:{src}"])
                for cj in range(ntc):
                    ps, pk = self.psum()
                    self.o("T", lambda e, ps=ps, cj=cj, xin=xin: e.transpose(
                        ps[:, 0:128], xin[:, cj * 128:(cj + 1) * 128], self.ident_f32()[0]),
                        reads=[xink, self.ident_f32()[1]], writes=[pk])
                    self.o("V", lambda e, ps=ps, cj=cj, xT=xT: e.tensor_copy(
                        xT[:, cj * 128:(cj + 1) * 128], ps[:, 0:128]),
                        reads=[pk], writes=[xTk])
                self.emit_qdq(xT, xTk, qT, qTk, C, scr, pool_ops=True)
                for cj in range(ntc):
                    ps, pk = self.psum()
                    psb = ps[:, :].bitcast(BF16)
                    self.o("T", lambda e, psb=psb, cj=cj, qT=qT: e.transpose(
                        psb[:, 0:128], qT[:, cj * 128:(cj + 1) * 128], idb[:, :]),
                        reads=[qTk, idbk], writes=[pk])
                    self.o("V", lambda e, psb=psb, cj=cj, qcm=qcm: e.tensor_copy(
                        qcm[:, cj * 128:(cj + 1) * 128], psb[:, 0:128]),
                        reads=[pk], writes=[qcmk])
                for cj in range(ntc):
                    self.ctx.dma(self.d[dst][b, cj * 128:(cj + 1) * 128, tsl],
                                 qcm[:, cj * 128:(cj + 1) * 128],
                                 reads=[qcmk], writes=[f"D:{dst}"])

    def ident_f32(self):
        if "identf" not in self.strip:
            ap, key = self.P.alloc(128, F32, "identf")
            self.ctx.dma(ap[:, :], self.d["c_ident"], writes=[key])
            self.strip["identf"] = (ap, key)
        return self.strip["identf"]

    def ident_bf16(self):
        if "identb" not in self.strip:
            f, fk = self.ident_f32()
            ap, key = self.P.alloc(128, BF16, "identb")
            self.o("V", lambda e: e.tensor_copy(ap[:, :], f[:, :]), reads=[fk], writes=[key])
            self.strip["identb"] = (ap, key)
        return self.strip["identb"]

    def st_quant_w(self, src, dst, R, C):
        A = self.A
        A.reset()
        idb, idbk = self.ident_bf16()
        ntr, ntc = R // 128, C // 128
        sets = [(A.alloc(C, F32, f"wrow{i}"), A.alloc(C, BF16, f"qrow{i}"),
                 A.alloc(128, BF16, f"tout{i}"), self.qdq_scratch(C)) for i in range(2)]
        for r in range(ntr):
            (wrow, wrowk), (qrow, qrowk), (tout, toutk), scr = sets[r % 2]
            self.ctx.dma(wrow[:, :], self.d[src][r * 128:(r + 1) * 128, :],
                         writes=[wrowk], reads=[f"D:{src}"])
            self.emit_qdq(wrow, wrowk, qrow, qrowk, C, scr, pool_ops=True)
            for cj in range(ntc):
                ps, pk = self.psum()
                psb = ps[:, :].bitcast(BF16)
                self.o("T", lambda e, psb=psb, cj=cj, qrow=qrow: e.transpose(
                    psb[:, 0:128], qrow[:, cj * 128:(cj + 1) * 128], idb[:, :]),
                    reads=[qrowk, idbk], writes=[pk])
                self.o("V", lambda e, psb=psb, tout=tout: e.tensor_copy(tout[:, :], psb[:, 0:128]),
                       reads=[pk], writes=[toutk])
                self.ctx.dma(self.d[dst][cj * 128:(cj + 1) * 128, r * 128:(r + 1) * 128],
                             tout[:, :], reads=[toutk], writes=[f"D:{dst}"])

    def st_gateup(self):
        A = self.A
        A.reset()
        wg = []
        for k in range(NT):
            ap, key = A.alloc(FFI, BF16, f"wg{k}")
            self.ctx.dma(ap[:, :], self.d["wgq"][k * 128:(k + 1) * 128, :],
                         writes=[key], reads=["D:wgq"])
            wg.append((ap, key))
        wu = []
        for k in range(NT):
            ap, key = A.alloc(FFI, BF16, f"wu{k}")
            self.ctx.dma(ap[:, :], self.d["wuq"][k * 128:(k + 1) * 128, :],
                         writes=[key], reads=["D:wuq"])
            wu.append((ap, key))
        xq = []
        for j in range(NT):
            ap, key = A.alloc(T, BF16, f"xq{j}")
            xq.append((ap, key))
        sg, sgk = A.alloc(CH, F32, "sg")
        hm, hmk = A.alloc(T, F32, "hm")
        for b in range(BL):
            for j in range(NT):
                ap, key = xq[j]
                self.ctx.dma(ap[:, :], self.d["xq0"][b, j * 128:(j + 1) * 128, :],
                             writes=[key], reads=["D:xq0"])
            for m in range(NFF):
                msl = slice(m * 128, (m + 1) * 128)
                for c in range(NCH):
                    sl = slice(c * CH, (c + 1) * CH)
                    psG, pkG = self.psum()
                    psU, pkU = self.psum()
                    self.mm_group(psG[:, :], pkG,
                                  [(wg[k][0][:, msl], xq[k][0][:, sl]) for k in range(NT)],
                                  reads=[wg[k][1] for k in range(NT)] + [xq[k][1] for k in range(NT)])
                    self.mm_group(psU[:, :], pkU,
                                  [(wu[k][0][:, msl], xq[k][0][:, sl]) for k in range(NT)],
                                  reads=[wu[k][1] for k in range(NT)] + [xq[k][1] for k in range(NT)])
                    self.o("S", lambda e, psG=psG: e.activation(out=sg[:, :], in_=psG[:, :],
                                                                func=AF.Silu),
                           reads=[pkG], writes=[sgk])
                    self.o("V", lambda e, psU=psU, sl=sl: e.tensor_mul(hm[:, sl], psU[:, :], sg[:, :]),
                           reads=[pkU, sgk], writes=[hmk])
                self.ctx.dma(self.d["h0f"][b, m * 128:(m + 1) * 128, :], hm[:, :],
                             reads=[hmk], writes=["D:h0f"])

    def st_down(self):
        A = self.A
        A.reset()
        wd = []
        for k in range(NFF):
            ap, key = A.alloc(D, BF16, f"wd{k}")
            self.ctx.dma(ap[:, :], self.d["wdq"][k * 128:(k + 1) * 128, :],
                         writes=[key], reads=["D:wdq"])
            wd.append((ap, key))
        hq = []
        for k in range(NFF):
            ap, key = A.alloc(CH, BF16, f"hq{k}")
            hq.append((ap, key))
        a0t, a0k = A.alloc(CH, F32, "a0t")
        x1t, x1k = A.alloc(CH, F32, "x1t")
        for b in range(BL):
            for c in range(NCH):
                sl = slice(c * CH, (c + 1) * CH)
                for k in range(NFF):
                    ap, key = hq[k]
                    self.ctx.dma(ap[:, :], self.d["hq0"][b, k * 128:(k + 1) * 128, sl],
                                 writes=[key], reads=["D:hq0"])
                for m in range(NT):
                    msl = slice(m * 128, (m + 1) * 128)
                    ps, pk = self.psum()
                    self.mm_group(ps[:, :], pk,
                                  [(wd[k][0][:, msl], hq[k][0][:, :]) for k in range(NFF)],
                                  reads=[wd[k][1] for k in range(NFF)] + [hq[k][1] for k in range(NFF)])
                    self.ctx.dma(a0t[:, :], self.d["a0"][b, msl, sl],
                                 writes=[a0k], reads=["D:a0"])
                    self.o("V", lambda e, ps=ps: e.tensor_add(x1t[:, :], ps[:, :], a0t[:, :]),
                           reads=[pk, a0k], writes=[x1k])
                    self.ctx.dma(self.d["x1"][b, msl, sl], x1t[:, :],
                                 reads=[x1k], writes=["D:x1"])



B0_STAGES = ["prep", "wq", "ln0", "xq", "scan", "y", "gateup", "hq", "down"]


def declare_block0(mk):
    mk.dram_in("x", [BL, D, T])
    mk.dram_in("c_ident", [128, 128])
    mk.dram_in("L0_attn_W_in", [D, D])
    mk.dram_in("L0_attn_C", [D, D])
    for nm in ["L0_attn_b_in", "L0_attn_B_vec", "L0_attn_log_A", "L0_attn_D"]:
        mk.dram_in(nm, [128, NT])
    mk.dram_in("L0_ffn_w_gate", [FFI, D])
    mk.dram_in("L0_ffn_w_up", [FFI, D])
    mk.dram_in("L0_ffn_w_down", [D, FFI])
    mk.dram_scratch("xn0f", [BL, D, T])
    mk.dram_scratch("xn0b", [BL, D, T], BF16)
    mk.dram_scratch("h0b", [BL, D, T], BF16)
    mk.dram_scratch("a0", [BL, D, T])
    mk.dram_scratch("xq0", [BL, D, T], BF16)
    mk.dram_scratch("wgq", [D, FFI], BF16)
    mk.dram_scratch("wuq", [D, FFI], BF16)
    mk.dram_scratch("wdq", [FFI, D], BF16)
    mk.dram_scratch("h0f", [BL, FFI, T])
    mk.dram_scratch("hq0", [BL, FFI, T], BF16)
    mk.dram_scratch("x1", [BL, D, T])


def emit_block0(mk, upto=None, consts=True):
    stages = B0_STAGES if upto is None else B0_STAGES[:B0_STAGES.index(upto) + 1]
    if consts:
        mk.setup_consts([EPS_LN])
    for s in stages:
        if s == "prep":
            mk.st_prep()
        elif s == "wq":
            mk.st_quant_w("L0_ffn_w_gate", "wgq", FFI, D)
            mk.st_quant_w("L0_ffn_w_up", "wuq", FFI, D)
            mk.st_quant_w("L0_ffn_w_down", "wdq", D, FFI)
        elif s == "ln0":
            mk.st_ln("x", "xn0f", "xn0b", "ln0")
        elif s == "xq":
            mk.st_quant_act("xn0f", "xq0", D)
        elif s == "scan":
            mk.st_l0scan()
        elif s == "y":
            mk.st_l0y()
        elif s == "gateup":
            mk.st_gateup()
        elif s == "hq":
            mk.st_quant_act("h0f", "hq0", FFI)
        elif s == "down":
            mk.st_down()


def vec_strip(v, n_tiles):
    v = np.asarray(v, np.float32).ravel()
    out = np.zeros((128, n_tiles), np.float32)
    full = v.reshape(-1)
    for j in range(n_tiles):
        seg = full[j * 128:(j + 1) * 128]
        out[:len(seg), j] = seg
    return np.ascontiguousarray(out)


def host_prep_block0(inp, b_sl, Tcur):
    m = {
        "c_ident": np.eye(128, dtype=np.float32),
        "L0_attn_W_in": np.asarray(inp["L0_attn_W_in"]),
        "L0_attn_C": np.asarray(inp["L0_attn_C"]),
        "L0_attn_b_in": vec_strip(inp["L0_attn_b_in"], NT),
        "L0_attn_B_vec": vec_strip(inp["L0_attn_B_vec"], NT),
        "L0_attn_log_A": vec_strip(inp["L0_attn_log_A"], NT),
        "L0_attn_D": vec_strip(inp["L0_attn_D"], NT),
        "L0_ffn_w_gate": np.asarray(inp["L0_ffn_w_gate"]),
        "L0_ffn_w_up": np.asarray(inp["L0_ffn_w_up"]),
        "L0_ffn_w_down": np.asarray(inp["L0_ffn_w_down"]),
    }
    return {k: np.ascontiguousarray(np.asarray(v, np.float32)) for k, v in m.items()}


DI = 2048
NDI = 16
CPROJ = 4128
NPROJ = 33
NHEAD = 32
PI = float(np.pi)
INV2PI = float(np.float32(1.0 / (2 * np.pi)))
TWOPI = float(np.float32(2 * np.pi))


class MK1:
    pass


def _mixin(cls):
    for k, v in list(cls.__dict__.items()):
        if not k.startswith("__"):
            setattr(MK, k, v)
    return cls


@_mixin
class _Block1Attn:
    def st_l1proj(self):
        A = self.A
        A.reset()
        w = self.load_w_lhsT_bf16("L1_attn_in_proj_w", D, CPROJ, "iproj", stg_n=1)
        xnb = []
        for j in range(NT):
            ap, key = A.alloc(T, BF16, f"x1nb{j}")
            xnb.append((ap, key))
        zbuf, zbk = A.alloc(T + 3, F32, "zbuf")
        acc, acck = A.alloc(T, F32, "cacc")
        ob, obk = A.alloc(T, BF16, "convb")
        ck = [self.strip[f"conv_k{i}"] for i in range(4)]
        cb, cbk = self.strip["conv_b"]
        for b in range(BL):
            for j in range(NT):
                ap, key = xnb[j]
                self.ctx.dma(ap[:, :], self.d["xn1b"][b, j * 128:(j + 1) * 128, :],
                             writes=[key], reads=["D:xn1b"])
            for m in range(NPROJ):
                mm = min(128, CPROJ - m * 128)
                msl = slice(m * 128, m * 128 + mm)
                self.o("V", lambda e: e.memset(zbuf[:, 0:3], 0.0), writes=[zbk])
                for c in range(NCH):
                    ps, pk = self.psum()
                    self.mm_group(ps[:mm, :], pk,
                                  [(w[k][0][:, msl], xnb[k][0][:, c * CH:(c + 1) * CH])
                                   for k in range(NT)],
                                  reads=[w[k][1] for k in range(NT)] + [x[1] for x in xnb])
                    self.o("V", lambda e, ps=ps, c=c, mm=mm: e.tensor_copy(
                        zbuf[:mm, 3 + c * CH:3 + (c + 1) * CH], ps[:mm, :]),
                        reads=[pk], writes=[zbk])
                self.o("V", lambda e, m=m, mm=mm: e.tensor_scalar(
                    out=acc[:mm, :], in0=zbuf[:mm, 0:T], scalar1=ck[0][0][:mm, m:m + 1],
                    scalar2=cb[:mm, m:m + 1], op0=ALU.mult, op1=ALU.add),
                    reads=[zbk, ck[0][1], cbk], writes=[acck])
                for i in (1, 2, 3):
                    self.o("V", lambda e, i=i, m=m, mm=mm: e.scalar_tensor_tensor(
                        out=acc[:mm, :], in0=zbuf[:mm, i:i + T], scalar=ck[i][0][:mm, m:m + 1],
                        in1=acc[:mm, :], op0=ALU.mult, op1=ALU.add),
                        reads=[zbk, ck[i][1], acck], writes=[acck])
                if m < NDI:
                    self.o("V", lambda e: e.tensor_copy(ob[:, :], acc[:, :]),
                           reads=[acck], writes=[obk])
                    self.ctx.dma(self.d["xval"][b, m * 128:(m + 1) * 128, :], ob[:, :],
                                 reads=[obk], writes=["D:xval"])
                elif m < 2 * NDI:
                    self.o("S", lambda e: e.activation(out=ob[:, :], in_=acc[:, :],
                                                       func=AF.Silu),
                           reads=[acck], writes=[obk])
                    self.ctx.dma(self.d["xgate"][b, (m - NDI) * 128:(m - NDI + 1) * 128, :],
                                 ob[:, :], reads=[obk], writes=["D:xgate"])
                else:
                    self.ctx.dma(self.d["sraw"][b, :, :], acc[0:32, :],
                                 reads=[acck], writes=["D:sraw"])

    def _softplus_strip(self, dst, dst_k, src, src_k, n):
        t1, t1k = self.A.alloc(4, F32, "spt1")
        t2, t2k = self.A.alloc(4, F32, "spt2")
        self.o("S", lambda e: e.activation(out=t1[:n, 0:1], in_=src[:n, 0:1], func=AF.Abs),
               reads=[src_k], writes=[t1k])
        self.o("S", lambda e: e.activation(out=t1[:n, 0:1], in_=t1[:n, 0:1], func=AF.Exp,
                                           scale=-1.0), reads=[t1k], writes=[t1k])
        self.o("S", lambda e: e.activation(out=t1[:n, 0:1], in_=t1[:n, 0:1], func=AF.Ln,
                                           bias=1.0), reads=[t1k, self.ck], writes=[t1k])
        self.o("V", lambda e: e.tensor_scalar_max(t2[:n, 0:1], src[:n, 0:1], 0.0),
               reads=[src_k], writes=[t2k])
        self.o("V", lambda e: e.tensor_add(dst[:n, 0:1], t1[:n, 0:1], t2[:n, 0:1]),
               reads=[t1k, t2k], writes=[dst_k])

    def st_l1mixA(self):
        A = self.A
        A.reset()
        dec, deck = self.strip["decay_s"]
        anc, anck = self.strip["anchor_s"]
        ssc, ssck = self.strip["score_ss"]
        nsp, nspk = A.alloc(4, F32, "negsp")
        spd, spdk = A.alloc(4, F32, "spd")
        spa, spak = A.alloc(4, F32, "spa")
        self._softplus_strip(spd, spdk, dec, deck, 28)
        self._softplus_strip(spa, spak, anc, anck, 4)
        self.o("V", lambda e: e.tensor_scalar_mul(nsp[0:28, 0:1], spd[0:28, 0:1], -1.0),
               reads=[spdk], writes=[nspk])
        self.o("V", lambda e: e.tensor_scalar_mul(spa[0:4, 0:1], spa[0:4, 0:1], -1.0),
               reads=[spak], writes=[spak])
        nspscr = self.dram_scratch("nspscr", [4, 1], F32)
        self.ctx.dma(nspscr[0:4, :], spa[0:4, 0:1], reads=[spak], writes=["D:nspscr"])
        self.ctx.dma(nsp[28:32, 0:1], nspscr[0:4, :], reads=["D:nspscr"], writes=[nspk], eng="P")
        dp, dpk = A.alloc(T, F32, "distpos")
        self.ctx.dma(dp[0:28, :], self.d["c_dist"][0:1, :].broadcast_to([28, T]),
                     reads=["D:c_dist"], writes=[dpk])
        self.ctx.dma(dp[28:32, :], self.d["c_pos"][0:1, :].broadcast_to([4, T]),
                     reads=["D:c_pos"], writes=[dpk])
        targ, targk = A.alloc(T, F32, "targ")
        tw, twk = A.alloc(T, F32, "tw")
        sr, srk = A.alloc(T, F32, "srA")
        pw, pwk = A.alloc(T, F32, "pwA")
        z32, z32k = A.alloc(T, F32, "z32")
        den, denk = A.alloc(T, F32, "denA")
        self.o("V", lambda e: e.memset(z32[0:32, :], 0.0), writes=[z32k])
        self.o("V", lambda e: e.tensor_scalar_mul(targ[0:32, :], dp[0:32, :], nsp[0:32, 0:1]),
               reads=[dpk, nspk], writes=[targk])
        self.o("S", lambda e: e.activation(out=tw[0:32, :], in_=targ[0:32, :], func=AF.Exp),
               reads=[targk], writes=[twk])
        for b in range(BL):
            self.ctx.dma(sr[0:32, :], self.d["sraw"][b, :, :], writes=[srk], reads=["D:sraw"])
            self.o("V", lambda e: e.tensor_scalar(out=pw[0:32, :], in0=sr[0:32, :],
                                                  scalar1=ssc[0:32, 0:1], scalar2=20.0,
                                                  op0=ALU.mult, op1=ALU.min),
                   reads=[srk, ssck], writes=[pwk])
            self.o("V", lambda e: e.tensor_scalar_max(pw[0:32, :], pw[0:32, :], -20.0),
                   reads=[pwk], writes=[pwk])
            self.o("S", lambda e: e.activation(out=pw[0:32, :], in_=pw[0:32, :], func=AF.Exp),
                   reads=[pwk], writes=[pwk])
            self.o("V", lambda e: e.tensor_mul(pw[0:32, :], pw[0:32, :], tw[0:32, :]),
                   reads=[pwk, twk], writes=[pwk])
            self.ctx.dma(self.d["pwdram"][b, :, :], pw[0:32, :], reads=[pwk], writes=["D:pwdram"])
            self.o("V", lambda e: e.tensor_tensor_scan(out=den[0:32, :], data0=pw[0:32, :],
                                                       data1=z32[0:32, :], initial=0.0,
                                                       op0=ALU.add, op1=ALU.add),
                   reads=[pwk, z32k], writes=[denk])
            self.o("V", lambda e: e.tensor_scalar_max(den[0:32, :], den[0:32, :], 1e-4),
                   reads=[denk], writes=[denk])
            self.o("V", lambda e: e.reciprocal(den[0:32, :], den[0:32, :]),
                   reads=[denk], writes=[denk])
            self.ctx.dma(self.d["invdram"][b, :, :], den[0:32, :], reads=[denk],
                         writes=["D:invdram"])

    def st_l1mixB(self):
        A = self.A
        A.reset()
        th, thk = self.strip["theta_s"]
        dl, dlk = A.alloc(4, F32, "dlog")
        self.ctx.dma(dl[0:1, 0:3], self.d["deriv_l"], writes=[dlk], reads=["D:deriv_l"])
        dmx, dmxk = A.alloc(4, F32, "dmax")
        self.o("V", lambda e: e.tensor_reduce(out=dmx[0:1, 0:1], in_=dl[0:1, 0:3],
                                              axis=AX.X, op=ALU.max),
               reads=[dlk], writes=[dmxk])
        self.o("V", lambda e: e.scalar_tensor_tensor(out=dl[0:1, 0:3], in0=dmx[0:1, 0:1].broadcast_to([1, 3]),
                                                     scalar=-1.0, in1=dl[0:1, 0:3],
                                                     op0=ALU.mult, op1=ALU.add),
               reads=[dlk, dmxk], writes=[dlk])
        self.o("S", lambda e: e.activation(out=dl[0:1, 0:3], in_=dl[0:1, 0:3], func=AF.Exp),
               reads=[dlk], writes=[dlk])
        self.o("V", lambda e: e.tensor_reduce(out=dmx[0:1, 0:1], in_=dl[0:1, 0:3],
                                              axis=AX.X, op=ALU.add),
               reads=[dlk], writes=[dmxk])
        self.o("V", lambda e: e.reciprocal(dmx[0:1, 0:1], dmx[0:1, 0:1]),
               reads=[dmxk], writes=[dmxk])
        self.o("V", lambda e: e.tensor_scalar_mul(dl[0:1, 0:3], dl[0:1, 0:3], dmx[0:1, 0:1]),
               reads=[dlk, dmxk], writes=[dlk])
        self.o("V", lambda e: e.tensor_scalar_mul(dl[0:1, 3:4], dl[0:1, 2:3], -1.0),
               reads=[dlk], writes=[dlk])
        wscr = self.dram_scratch("wscr", [1, 4], F32)
        self.ctx.dma(wscr[0:1, :], dl[0:1, 0:4], reads=[dlk], writes=["D:wscr"])
        wb_, wbk_ = A.alloc(4, F32, "wbb")
        for j in range(3):
            self.ctx.dma(wb_[:, j:j + 1], wscr[0:1, j if j < 2 else 3].unsqueeze(1).broadcast_to([128, 1]),
                         reads=["D:wscr"], writes=[wbk_]) if False else None
        for j, col in enumerate([0, 1, 3]):
            self.ctx.dma(wb_[:, j:j + 1], wscr[0:1, col:col + 1].broadcast_to([128, 1]),
                         reads=["D:wscr"], writes=[wbk_], eng="P")
        wre, wrek = A.alloc(128, F32, "wreF")
        wim, wimk = A.alloc(128, F32, "wimF")
        wreb, wrebk = A.alloc(128, BF16, "wreB")
        wimb, wimbk = A.alloc(128, BF16, "wimB")
        nsre, nsrek = self.strip["ns_re"]
        nsim, nsimk = self.strip["ns_im"]
        for (wf, wfk, dn) in ((wre, wrek, "L1_attn_W_re"), (wim, wimk, "L1_attn_W_im")):
            self.o("V", lambda e, wf=wf: e.memset(wf[:, :], 0.0), writes=[wfk])
            self.ctx.dma(wf[0:64, 0:64], self.d[dn], reads=[f"D:{dn}"], writes=[wfk])
            self.ctx.dma(wf[64:128, 64:128], self.d[dn], reads=[f"D:{dn}"], writes=[wfk])
        self.o("V", lambda e: e.tensor_scalar_mul(wre[:, :], wre[:, :], nsre[:, 0:1]),
               reads=[wrek, nsrek], writes=[wrek])
        self.o("V", lambda e: e.tensor_scalar_mul(wim[:, :], wim[:, :], nsim[:, 0:1]),
               reads=[wimk, nsimk], writes=[wimk])
        self.o("V", lambda e: e.tensor_copy(wreb[:, :], wre[:, :]), reads=[wrek], writes=[wrebk])
        self.o("V", lambda e: e.tensor_copy(wimb[:, :], wim[:, :]), reads=[wimk], writes=[wimbk])
        xvt, xvtk = A.alloc(T, BF16, "xvt")
        B = [A.alloc(T, F32, f"B{i}") for i in range(6)]
        z128, z128k = A.alloc(T, F32, "z128")
        nre, nrek = A.alloc(T, BF16, "nreb")
        nim, nimk = A.alloc(T, BF16, "nimb")
        mst, mstk = A.alloc(T, F32, "mst")
        yrw, yrwk = A.alloc(T, BF16, "yrw")
        g64, g64k = self.strip["g64"]
        self.o("V", lambda e: e.memset(z128[:, :], 0.0), writes=[z128k])
        o = self.o
        for b in range(BL):
            for jt in range(NDI):
                k0 = 2 * jt
                (b1, b1k), (b2, b2k), (b3, b3k), (b4, b4k), (b5, b5k), (b6, b6k) = B
                self.ctx.dma(xvt[:, :], self.d["xval"][b, jt * 128:(jt + 1) * 128, :],
                             writes=[xvtk], reads=["D:xval"])
                o("V", lambda e, jt=jt: e.tensor_scalar_mul(b1[:, :], xvt[:, :], th[:, jt:jt + 1]),
                  reads=[xvtk, thk], writes=[b1k])
                o("V", lambda e: e.tensor_scalar(out=b2[:, :], in0=b1[:, :], scalar1=INV2PI,
                                                 scalar2=MAGIC, op0=ALU.mult, op1=ALU.add),
                  reads=[b1k], writes=[b2k])
                o("V", lambda e: e.tensor_scalar(out=b2[:, :], in0=b2[:, :], scalar1=MAGIC,
                                                 scalar2=None, op0=ALU.subtract),
                  reads=[b2k], writes=[b2k])
                o("V", lambda e: e.scalar_tensor_tensor(out=b3[:, :], in0=b2[:, :], scalar=-TWOPI,
                                                        in1=b1[:, :], op0=ALU.mult, op1=ALU.add),
                  reads=[b2k, b1k], writes=[b3k])
                o("V", lambda e: e.tensor_scalar(out=b2[:, :], in0=b1[:, :], scalar1=INV2PI,
                                                 scalar2=0.25, op0=ALU.mult, op1=ALU.add),
                  reads=[b1k], writes=[b2k])
                o("V", lambda e: e.tensor_scalar(out=b2[:, :], in0=b2[:, :], scalar1=MAGIC,
                                                 scalar2=MAGIC, op0=ALU.add, op1=ALU.subtract),
                  reads=[b2k], writes=[b2k])
                o("V", lambda e: e.scalar_tensor_tensor(out=b4[:, :], in0=b2[:, :], scalar=-TWOPI,
                                                        in1=b1[:, :], op0=ALU.mult, op1=ALU.add),
                  reads=[b2k, b1k], writes=[b4k])
                o("S", lambda e: e.activation(out=b2[:, :], in_=b4[:, :], func=AF.Sin,
                                              bias=PI / 2),
                  reads=[b4k, self.ck], writes=[b2k])
                o("S", lambda e: e.activation(out=b5[:, :], in_=b3[:, :], func=AF.Sin),
                  reads=[b3k], writes=[b5k])
                o("V", lambda e: e.tensor_scalar(out=b4[:, :], in0=xvt[:, :],
                                                 scalar1=wb_[:, 1:2], scalar2=wb_[:, 0:1],
                                                 op0=ALU.mult, op1=ALU.add),
                  reads=[xvtk, wbk_], writes=[b4k])
                o("S", lambda e: e.activation(out=b6[:, :], in_=xvt[:, :], func=AF.Square),
                  reads=[xvtk], writes=[b6k])
                o("V", lambda e: e.scalar_tensor_tensor(out=b4[:, :], in0=b6[:, :],
                                                        scalar=wb_[:, 2:3], in1=b4[:, :],
                                                        op0=ALU.mult, op1=ALU.add),
                  reads=[b6k, wbk_, b4k], writes=[b4k])
                o("V", lambda e: e.tensor_mul(b6[:, :], b4[:, :], b2[:, :]),
                  reads=[b4k, b2k], writes=[b6k])
                o("V", lambda e: e.tensor_mul(b2[:, :], b4[:, :], b5[:, :]),
                  reads=[b4k, b5k], writes=[b2k])
                self.ctx.dma(b4[0:64, :], self.d["pwdram"][b, k0:k0 + 1, :].broadcast_to([64, T]),
                             reads=["D:pwdram"], writes=[b4k])
                self.ctx.dma(b4[64:128, :], self.d["pwdram"][b, k0 + 1:k0 + 2, :].broadcast_to([64, T]),
                             reads=["D:pwdram"], writes=[b4k])
                o("V", lambda e: e.tensor_mul(b3[:, :], b6[:, :], b4[:, :]),
                  reads=[b6k, b4k], writes=[b3k])
                o("V", lambda e: e.tensor_mul(b5[:, :], b2[:, :], b4[:, :]),
                  reads=[b2k, b4k], writes=[b5k])
                o("V", lambda e: e.tensor_tensor_scan(out=b6[:, :], data0=b3[:, :],
                                                      data1=z128[:, :], initial=0.0,
                                                      op0=ALU.add, op1=ALU.add),
                  reads=[b3k, z128k], writes=[b6k])
                o("V", lambda e: e.tensor_tensor_scan(out=b2[:, :], data0=b5[:, :],
                                                      data1=z128[:, :], initial=0.0,
                                                      op0=ALU.add, op1=ALU.add),
                  reads=[b5k, z128k], writes=[b2k])
                self.ctx.dma(b4[0:64, :], self.d["invdram"][b, k0:k0 + 1, :].broadcast_to([64, T]),
                             reads=["D:invdram"], writes=[b4k])
                self.ctx.dma(b4[64:128, :], self.d["invdram"][b, k0 + 1:k0 + 2, :].broadcast_to([64, T]),
                             reads=["D:invdram"], writes=[b4k])
                o("V", lambda e: e.tensor_mul(b6[:, :], b6[:, :], b4[:, :]),
                  reads=[b6k, b4k], writes=[b6k])
                o("V", lambda e: e.tensor_mul(b2[:, :], b2[:, :], b4[:, :]),
                  reads=[b2k, b4k], writes=[b2k])
                o("V", lambda e: e.tensor_copy(nre[:, :], b6[:, :]), reads=[b6k], writes=[nrek])
                o("V", lambda e: e.tensor_copy(nim[:, :], b2[:, :]), reads=[b2k], writes=[nimk])
                o("S", lambda e: e.activation(out=b3[:, :], in_=b6[:, :], func=AF.Square),
                  reads=[b6k], writes=[b3k])
                o("S", lambda e: e.activation(out=b5[:, :], in_=b2[:, :], func=AF.Square),
                  reads=[b2k], writes=[b5k])
                for c in range(NCH):
                    sl = slice(c * CH, (c + 1) * CH)
                    psM, pkM = self.psum()
                    self.o("T", lambda e, psM=psM, sl=sl: (
                        e.matmul(psM[0:2, :], g64[:, 0:2], b3[:, sl], start=True, stop=False),
                        e.matmul(psM[0:2, :], g64[:, 0:2], b5[:, sl], start=False, stop=True))[-1],
                        reads=[g64k, b3k, b5k], writes=[pkM])
                    self.o("V", lambda e, psM=psM, sl=sl: e.tensor_copy(mst[0:2, sl], psM[0:2, :]),
                           reads=[pkM], writes=[mstk])
                    psY, pkY = self.psum()
                    self.o("T", lambda e, psY=psY, sl=sl: (
                        e.matmul(psY[:, :], wreb[:, :], nre[:, sl], start=True, stop=False),
                        e.matmul(psY[:, :], wimb[:, :], nim[:, sl], start=False, stop=True))[-1],
                        reads=[wrebk, wimbk, nrek, nimk], writes=[pkY])
                    self.o("V", lambda e, psY=psY, sl=sl: e.tensor_copy(yrw[:, sl], psY[:, :]),
                           reads=[pkY], writes=[yrwk])
                self.ctx.dma(self.d["msd"][b, k0:k0 + 2, :], mst[0:2, :],
                             reads=[mstk], writes=["D:msd"])
                self.ctx.dma(self.d["yraw"][b, jt * 128:(jt + 1) * 128, :], yrw[:, :],
                             reads=[yrwk], writes=["D:yraw"])

    def st_l1mixC(self):
        A = self.A
        A.reset()
        ms, msk = A.alloc(T, F32, "msC")
        inv, invk = A.alloc(T, F32, "invC")
        for b in range(BL):
            self.ctx.dma(ms[0:32, :], self.d["msd"][b, :, :], writes=[msk], reads=["D:msd"])
            self.o("S", lambda e: e.activation(out=ms[0:32, :], in_=ms[0:32, :], func=AF.Ln,
                                               scale=1.0 / 128.0, bias=EPS_LN),
                   reads=[msk, self.ck], writes=[msk])
            self.o("S", lambda e: e.activation(out=ms[0:32, :], in_=ms[0:32, :], func=AF.Exp,
                                               scale=-0.5),
                   reads=[msk], writes=[msk])
            self.ctx.dma(self.d["combd"][b, :, :], ms[0:32, :], reads=[msk], writes=["D:combd"])

    def st_l1out(self):
        A = self.A
        A.reset()
        combb, combbk = A.alloc(T, F32, "combb")
        xgt, xgtk = A.alloc(T, BF16, "xgt")
        yrt, yrtk = A.alloc(T, BF16, "yrt")
        g2, g2k = A.alloc(T, F32, "g2")
        ygt, ygtk = A.alloc(T, BF16, "ygt")
        for b in range(BL):
            for jt in range(NDI):
                k0 = 2 * jt
                self.ctx.dma(combb[0:64, :], self.d["combd"][b, k0:k0 + 1, :].broadcast_to([64, T]),
                             reads=["D:combd"], writes=[combbk])
                self.ctx.dma(combb[64:128, :], self.d["combd"][b, k0 + 1:k0 + 2, :].broadcast_to([64, T]),
                             reads=["D:combd"], writes=[combbk])
                self.ctx.dma(xgt[:, :], self.d["xgate"][b, jt * 128:(jt + 1) * 128, :],
                             writes=[xgtk], reads=["D:xgate"])
                self.ctx.dma(yrt[:, :], self.d["yraw"][b, jt * 128:(jt + 1) * 128, :],
                             writes=[yrtk], reads=["D:yraw"])
                self.o("V", lambda e: e.tensor_mul(g2[:, :], xgt[:, :], combb[:, :]),
                       reads=[xgtk, combbk], writes=[g2k])
                self.o("V", lambda e: e.tensor_mul(ygt[:, :], yrt[:, :], g2[:, :]),
                       reads=[yrtk, g2k], writes=[ygtk])
                self.ctx.dma(self.d["ygb"][b, jt * 128:(jt + 1) * 128, :], ygt[:, :],
                             reads=[ygtk], writes=["D:ygb"])
        A.reset()
        w = self.load_w_lhsT_bf16("L1_attn_out_proj_w", DI, D, "oproj")
        ygc = [A.alloc(CH, BF16, f"ygc{k}") for k in range(NDI)]
        at, atk = A.alloc(CH, BF16, "attn1t")
        for b in range(BL):
            for c in range(NCH):
                sl = slice(c * CH, (c + 1) * CH)
                for k in range(NDI):
                    ap, key = ygc[k]
                    self.ctx.dma(ap[:, :], self.d["ygb"][b, k * 128:(k + 1) * 128, sl],
                                 writes=[key], reads=["D:ygb"])
                for m in range(NT):
                    msl = slice(m * 128, (m + 1) * 128)
                    ps, pk = self.psum()
                    self.mm_group(ps[:, :], pk,
                                  [(w[k][0][:, msl], ygc[k][0][:, :]) for k in range(NDI)],
                                  reads=[w[k][1] for k in range(NDI)] + [y[1] for y in ygc])
                    self.o("V", lambda e, ps=ps: e.tensor_copy(at[:, :], ps[:, :]),
                           reads=[pk], writes=[atk])
                    self.ctx.dma(self.d["attn1"][b, msl, sl], at[:, :],
                                 reads=[atk], writes=["D:attn1"])


@_mixin
class _Block1Decl:
    def st_prep1(self):
        self.load_strip("conv_b", "conv_b_s", NPROJ)
        for i in range(4):
            self.load_strip(f"conv_k{i}", f"conv_k{i}_s", NPROJ)
        self.load_strip("theta_s", "theta_s", NDI)
        self.load_strip("score_ss", "score_ss", 1)
        self.load_strip("decay_s", "decay_s", 1)
        self.load_strip("anchor_s", "anchor_s", 1)
        self.load_strip("ns_re", "ns_re", 1)
        self.load_strip("ns_im", "ns_im", 1)
        self.load_strip("g64", "c_g64", 2)


B1A_STAGES = ["prep1", "ln1", "proj", "mixA", "mixB", "mixC", "out1"]


def declare_block1a(mk, feed_xn=False):
    for nm, shape in [("L1_attn_in_proj_w", [D, CPROJ]), ("conv_b_s", [128, NPROJ]),
                      ("conv_k0_s", [128, NPROJ]), ("conv_k1_s", [128, NPROJ]),
                      ("conv_k2_s", [128, NPROJ]), ("conv_k3_s", [128, NPROJ]),
                      ("theta_s", [128, NDI]), ("score_ss", [32, 1]),
                      ("decay_s", [28, 1]), ("anchor_s", [4, 1]),
                      ("deriv_l", [1, 3]), ("L1_attn_W_re", [64, 64]),
                      ("L1_attn_W_im", [64, 64]), ("ns_re", [128, 1]), ("ns_im", [128, 1]),
                      ("L1_attn_out_proj_w", [DI, D]), ("c_pos", [1, T]),
                      ("c_dist", [1, T]), ("c_g64", [128, 2])]:
        mk.dram_in(nm, shape)
    mk.dram_scratch("xn1f", [BL, D, T])
    mk.dram_scratch("xn1b", [BL, D, T], BF16)
    mk.dram_scratch("xval", [BL, DI, T], BF16)
    mk.dram_scratch("xgate", [BL, DI, T], BF16)
    mk.dram_scratch("sraw", [BL, 32, T])
    mk.dram_scratch("pwdram", [BL, 32, T])
    mk.dram_scratch("invdram", [BL, 32, T])
    mk.dram_scratch("msd", [BL, 32, T])
    mk.dram_scratch("combd", [BL, 32, T])
    mk.dram_scratch("yraw", [BL, DI, T], BF16)
    mk.dram_scratch("ygb", [BL, DI, T], BF16)
    mk.dram_scratch("attn1", [BL, D, T], BF16)


def emit_block1a(mk, upto=None, skip=()):
    stages = B1A_STAGES if upto is None else B1A_STAGES[:B1A_STAGES.index(upto) + 1]
    for s in stages:
        if s in skip:
            continue
        if s == "prep1":
            mk.st_prep1()
        elif s == "ln1":
            mk.st_ln("x1", "xn1f", "xn1b", "ln1")
        elif s == "proj":
            mk.st_l1proj()
        elif s == "mixA":
            mk.st_l1mixA()
        elif s == "mixB":
            mk.st_l1mixB()
        elif s == "mixC":
            mk.st_l1mixC()
        elif s == "out1":
            mk.st_l1out()


def host_prep_block1a(inp, Tcur):
    ck = np.asarray(inp["L1_attn_conv_k"], np.float32).reshape(4, CPROJ)
    theta = np.asarray(inp["L1_attn_theta"], np.float32).reshape(NHEAD * 64)
    ns = np.asarray(inp["L1_attn_norm_scale"], np.float32)
    pos = np.arange(Tcur, dtype=np.float32)
    g64 = np.zeros((128, 2), np.float32)
    g64[0:64, 0] = 1.0
    g64[64:128, 1] = 1.0
    m = {
        "L1_attn_in_proj_w": inp["L1_attn_in_proj_w"],
        "conv_b_s": vec_strip(inp["L1_attn_conv_b"], NPROJ),
        "theta_s": vec_strip(theta, NDI),
        "score_ss": np.asarray(inp["L1_attn_score_scale"], np.float32).reshape(32, 1),
        "decay_s": np.asarray(inp["L1_attn_decay_slopes"], np.float32).reshape(28, 1),
        "anchor_s": np.asarray(inp["L1_attn_anchor_slopes"], np.float32).reshape(4, 1),
        "deriv_l": np.asarray(inp["L1_attn_deriv_logits"], np.float32).reshape(1, 3),
        "L1_attn_W_re": inp["L1_attn_W_re"],
        "L1_attn_W_im": inp["L1_attn_W_im"],
        "ns_re": np.concatenate([ns[0:64], ns[0:64]]).reshape(128, 1),
        "ns_im": np.concatenate([ns[64:128], ns[64:128]]).reshape(128, 1),
        "L1_attn_out_proj_w": inp["L1_attn_out_proj_w"],
        "c_pos": pos.reshape(1, Tcur),
        "c_dist": np.maximum(np.float32(Tcur - 1) - pos, 0.0).reshape(1, Tcur),
        "c_g64": g64,
    }
    for i in range(4):
        m[f"conv_k{i}_s"] = vec_strip(ck[i], NPROJ)
    return {k: np.ascontiguousarray(np.asarray(v, np.float32)) for k, v in m.items()}


INNER = 512
NIN = 4
HID = 2730
NW1 = 22
QKS = 8.0


@_mixin
class _Block1Ffn:
    def load_w_ap_lhsT_bf16(self, dram_ap, dkey, K, M, tag, stg_n=2):
        ntk = ceil_div(K, 128)
        stg = [self.A.alloc(M, F32, f"{tag}wstg{i}") for i in range(stg_n)]
        out = []
        for k in range(ntk):
            kk = min(128, K - k * 128)
            f, fk = stg[k % stg_n]
            bb, bk = self.A.alloc(M, BF16, f"{tag}wb{k}")
            self.ctx.dma(f[:kk, :], dram_ap[k * 128:k * 128 + kk, :], writes=[fk], reads=[dkey])
            self.o("V", lambda e, bb=bb, f=f, kk=kk: e.tensor_copy(bb[:kk, :], f[:kk, :]),
                   reads=[fk], writes=[bk])
            out.append((bb, bk, kk))
        return out

    def st_ffn_qkv(self, li, fin_b):
        A = self.A
        A.reset()
        w = self.load_w_ap_lhsT_bf16(self.d["L1_ffn_wqkv"][li], "D:L1_ffn_wqkv",
                                     D, 3 * INNER, f"wqkv{li}", stg_n=1)
        fb = []
        for k in range(NT):
            ap, key = A.alloc(T + 1, BF16, f"fqkv{k}")
            fb.append((ap, key))
        qf2 = [A.alloc(T, F32, f"qkvqf{i}") for i in range(2)]
        vb2 = [A.alloc(T, BF16, f"qkvvb{i}") for i in range(2)]
        vtb2 = [A.alloc(128, BF16, f"qkvvtb{i}") for i in range(2)]
        idb, idbk = self.ident_bf16()
        for b in range(BL):
            for k in range(NT):
                ap, key = fb[k]
                self.o("V", lambda e, ap=ap: e.memset(ap[:, 0:1], 0.0), writes=[key])
                self.ctx.dma(ap[:, 1:T + 1], self.d[fin_b][b, k * 128:(k + 1) * 128, :],
                             writes=[key], reads=[f"D:{fin_b}"])
            for m in range(12):
                msl = slice(m * 128, (m + 1) * 128)
                is_v = m >= 8
                qf, qfk = qf2[m % 2]
                vb, vbk = vb2[m % 2]
                for c in range(NCH):
                    ps, pk = self.psum()
                    pairs = []
                    for k in range(NT):
                        off = 1 if k < 4 else 0
                        pairs.append((w[k][0][:, msl],
                                      fb[k][0][:, off + c * CH: off + c * CH + CH]))
                    self.mm_group(ps[:, :], pk, pairs,
                                  reads=[w[k][1] for k in range(NT)] + [f[1] for f in fb])
                    if is_v:
                        self.o("V", lambda e, ps=ps, c=c, vb=vb: e.tensor_copy(
                            vb[:, c * CH:(c + 1) * CH], ps[:, :]), reads=[pk], writes=[vbk])
                    else:
                        self.o("V", lambda e, ps=ps, c=c, qf=qf: e.tensor_copy(
                            qf[:, c * CH:(c + 1) * CH], ps[:, :]), reads=[pk], writes=[qfk])
                if is_v:
                    for tc in range(T // 128):
                        ps, pk = self.psum()
                        psb = ps[:, :].bitcast(BF16)
                        vtb, vtbk = vtb2[tc % 2]
                        self.o("T", lambda e, psb=psb, tc=tc, vb=vb: e.transpose(
                            psb[:, 0:128], vb[:, tc * 128:(tc + 1) * 128], idb[:, :]),
                            reads=[vbk, idbk], writes=[pk])
                        self.o("V", lambda e, psb=psb, vtb=vtb: e.tensor_copy(
                            vtb[:, :], psb[:, 0:128]),
                            reads=[pk], writes=[vtbk])
                        self.ctx.dma(self.d[f"vT{li}"][b, m - 8, tc * 128:(tc + 1) * 128, :],
                                     vtb[:, :], reads=[vtbk], writes=[f"D:vT{li}"])
                else:
                    dst = f"qraw{li}" if m < 4 else f"kraw{li}"
                    mm = m if m < 4 else m - 4
                    self.ctx.dma(self.d[dst][b, mm * 128:(mm + 1) * 128, :], qf[:, :],
                                 reads=[qfk], writes=[f"D:{dst}"])
        A.reset()
        csb, csbk = A.alloc(T, BF16, "csbufB")
        ssb, ssbk = A.alloc(T, BF16, "ssbufB")
        cstage, cstagek = A.alloc(T, F32, "cstage")
        qfB = [A.alloc(T, F32, f"qfB{i}") for i in range(2)]
        sqc2 = [A.alloc(CH, F32, f"sqcB{i}") for i in range(2)]
        rsbB = [A.alloc(T, F32, f"rsbB{i}") for i in range(2)]
        ss2B = rsbB
        qrotB = [A.alloc(T, F32, f"qrotB{i}") for i in range(2)]
        m1B = [A.alloc(T, F32, f"m1B{i}") for i in range(2)]
        qobB = [A.alloc(T, BF16, f"qobB{i}") for i in range(2)]
        g64, g64k = self.strip["g64"]
        rs_scr = self.dram_scratch(f"rs_scr{li}", [4, 2, T], F32)

        def load_consts(c1n, c2n):
            self.ctx.dma(cstage[:, :], self.d[c1n], writes=[cstagek], reads=[f"D:{c1n}"])
            self.o("V", lambda e: e.tensor_copy(csb[:, :], cstage[:, :]),
                   reads=[cstagek], writes=[csbk])
            self.ctx.dma(cstage[:, :], self.d[c2n], writes=[cstagek], reads=[f"D:{c2n}"])
            self.o("V", lambda e: e.tensor_copy(ssb[:, :], cstage[:, :]),
                   reads=[cstagek], writes=[ssbk])
        it = 0
        for qk in range(2):
            srcn = f"qraw{li}" if qk == 0 else f"kraw{li}"
            dstn = f"qb{li}" if qk == 0 else f"kb{li}"
            load_consts("c_csq2" if qk == 0 else "c_csk2",
                        "c_ssq2" if qk == 0 else "c_ssk2")
            dupn = f"qs_dup_{li}" if qk == 0 else f"ks_dup_{li}"
            rotn = f"qsrot_dup_{li}" if qk == 0 else f"ksrot_dup_{li}"
            dup, dupk = self.strip[dupn]
            rot, rotk = self.strip[rotn]
            for b in range(BL):
                for m in range(4):
                    qf, qfk = qfB[it % 2]
                    sqc, sqck = sqc2[it % 2]
                    ss2, ss2k = ss2B[it % 2]
                    rsb, rsbk = rsbB[it % 2]
                    qrot, qrotk = qrotB[it % 2]
                    m1, m1k = m1B[it % 2]
                    m2, m2k = qrot, qrotk
                    qob, qobk = qobB[it % 2]
                    scr_i = it % 4
                    it += 1
                    self.ctx.dma(qf[:, :], self.d[srcn][b, m * 128:(m + 1) * 128, :],
                                 writes=[qfk], reads=[f"D:{srcn}"])
                    for c in range(NCH):
                        ps2, pk2 = self.psum()
                        self.o("S", lambda e, c=c, sqc=sqc, qf=qf: e.activation(
                            out=sqc[:, :], in_=qf[:, c * CH:(c + 1) * CH], func=AF.Square),
                            reads=[qfk], writes=[sqck])
                        self.o("T", lambda e, ps2=ps2, sqc=sqc: e.matmul(
                            ps2[0:2, :], g64[:, 0:2], sqc[:, :], start=True, stop=True),
                            reads=[g64k, sqck], writes=[pk2])
                        self.o("V", lambda e, ps2=ps2, c=c, ss2=ss2: e.tensor_copy(
                            ss2[0:2, c * CH:(c + 1) * CH], ps2[0:2, :]),
                            reads=[pk2], writes=[ss2k])
                    self.o("S", lambda e, ss2=ss2: e.activation(out=ss2[0:2, :], in_=ss2[0:2, :],
                                                       func=AF.Ln, bias=1e-24),
                           reads=[ss2k, self.ck], writes=[ss2k])
                    self.o("S", lambda e, ss2=ss2: e.activation(out=ss2[0:2, :], in_=ss2[0:2, :],
                                                       func=AF.Exp, scale=-0.5),
                           reads=[ss2k], writes=[ss2k])
                    self.ctx.dma(rs_scr[scr_i, 0:2, :], ss2[0:2, :], reads=[ss2k],
                                 writes=[f"D:rs{li}{scr_i}"])
                    self.ctx.dma(rsb[0:64, :], rs_scr[scr_i, 0:1, :].broadcast_to([64, T]),
                                 reads=[f"D:rs{li}{scr_i}"], writes=[rsbk], eng="P")
                    self.ctx.dma(rsb[64:128, :], rs_scr[scr_i, 1:2, :].broadcast_to([64, T]),
                                 reads=[f"D:rs{li}{scr_i}"], writes=[rsbk], eng="P")
                    for (d0, s0) in [(0, 32), (32, 0), (64, 96), (96, 64)]:
                        self.ctx.dma(qrot[d0:d0 + 32, :], qf[s0:s0 + 32, :],
                                     reads=[qfk], writes=[qrotk])
                    self.o("V", lambda e, m1=m1, qf=qf: e.tensor_mul(m1[:, :], qf[:, :], csb[:, :]),
                           reads=[qfk, csbk], writes=[m1k])
                    self.o("V", lambda e, m1=m1, dup=dup: e.tensor_scalar_mul(
                        m1[:, :], m1[:, :], dup[:, 0:1]),
                           reads=[m1k, dupk], writes=[m1k])
                    self.o("P", lambda e, m2=m2, qrot=qrot: e.tensor_mul(
                        m2[:, :], qrot[:, :], ssb[:, :]),
                           reads=[qrotk, ssbk], writes=[m2k])
                    self.o("P", lambda e, m2=m2, rot=rot: e.tensor_scalar_mul(
                        m2[:, :], m2[:, :], rot[:, 0:1]),
                           reads=[m2k, rotk], writes=[m2k])
                    self.o("V", lambda e, m1=m1, m2=m2: e.tensor_add(m1[:, :], m1[:, :], m2[:, :]),
                           reads=[m1k, m2k], writes=[m1k])
                    self.o("V", lambda e, qob=qob, m1=m1, rsb=rsb: e.tensor_mul(
                        qob[:, :], m1[:, :], rsb[:, :]),
                           reads=[m1k, rsbk], writes=[qobk])
                    self.ctx.dma(self.d[dstn][b, m * 128:(m + 1) * 128, :], qob[:, :],
                                 reads=[qobk], writes=[f"D:{dstn}"])

    def strip_big(self, name):
        if name not in self.strip:
            ap, key = self.P.alloc(T, F32, name)
            self.ctx.dma(ap[:, :], self.d[name], writes=[key])
            self.strip[name] = (ap, key)
        return self.strip[name]

    def st_ffn_attn(self, li):
        A = self.A
        A.reset()
        NKC = T // 128
        NQC = T // 128
        NQG = T // CH
        sets = []
        for i in range(2):
            qaug = A.alloc(T, BF16, f"qaug{i}")
            kaug = A.alloc(T, BF16, f"kaug{i}")
            vtl, _vk0 = A.alloc(65 * NKC, BF16, f"vtl{i}")
            vtlk = [f"vtl{i}_{kc}" for kc in range(NKC)]
            for _k in vtlk:
                self.ctx.merge_keys(_k, [_vk0])
            scrE = A.alloc(256, F32, f"scrE{i}")
            rmxb = A.alloc(NQC + 4, F32, f"rmxb{i}")
            rmx2 = A.alloc(4, F32, f"rmx2{i}")
            rmte = A.alloc(128, F32, f"rmte{i}")
            rmf = A.alloc(T, F32, f"rmf{i}")
            nrm = A.alloc(T, BF16, f"nrm{i}")
            expb2 = [A.alloc(CH, BF16, f"expb{i}_{j}") for j in range(2)]
            oacc = A.alloc(T, F32, f"oacc{i}")
            rss = A.alloc(T, F32, f"rss{i}")
            sets.append(dict(qaug=qaug, kaug=kaug, vtl=(vtl, _vk0, vtlk), scrE=scrE,
                             rmxb=rmxb, rmx2=rmx2, rmte=rmte, rmf=rmf, nrm=nrm,
                             expb2=expb2, oacc=oacc, rss=rss, av=i, p1b=(2 + 2 * i, 3 + 2 * i)))
            self.o("V", lambda e, vtl=vtl: e.memset(vtl[:, :], 1.0), writes=vtlk + [_vk0])
        onesb, onesbk = A.alloc(2, BF16, "onesb")
        self.o("V", lambda e: e.memset(onesb[:, 0:1], 1.0), writes=[onesbk])
        mnd, mndk = self.strip_c128("c_mnd")
        mnf, mnfk = self.strip_c128("c_mnf")
        md2, md2k = self.strip_c128("c_md2")
        mf2, mf2k = self.strip_c128("c_mf2")
        idf, idfk = self.ident_f32()
        it = 0
        for b in range(BL):
            for h in range(8):
                S = sets[it % 2]
                it += 1
                (qaug, qaugk), (kaug, kaugk) = S["qaug"], S["kaug"]
                vtl, _vtlk0, vtlk = S["vtl"]
                scrE, scrEk = S["scrE"]
                rmxb, rmxbk = S["rmxb"]
                rmx2, rmx2k = S["rmx2"]
                rmte, rmtek = S["rmte"]
                rmf, rmfk = S["rmf"]
                nrm, nrmk = S["nrm"]
                expb2 = S["expb2"]
                oacc, oacck = S["oacc"]
                rss, rssk = S["rss"]
                psAV, pkAV = self.ps[S["av"]], self.ps_keys[S["av"]]
                pb0, pb1 = S["p1b"]
                mv, half = h // 2, h % 2
                self.ctx.dma(qaug[0:64, :], self.d[f"qb{li}"][b, half * 64 + mv * 128:
                                                             half * 64 + mv * 128 + 64, :],
                             writes=[qaugk], reads=[f"D:qb{li}"])
                self.ctx.dma(kaug[0:64, :], self.d[f"kb{li}"][b, half * 64 + mv * 128:
                                                              half * 64 + mv * 128 + 64, :],
                             writes=[kaugk], reads=[f"D:kb{li}"])
                self.o("V", lambda e, kaug=kaug: e.memset(kaug[64:65, :], 1.0), writes=[kaugk])
                for kc in range(NKC):
                    self.ctx.dma(vtl[:, kc * 65:kc * 65 + 64],
                                 self.d[f"vT{li}"][b, mv, kc * 128:(kc + 1) * 128,
                                                   half * 64:half * 64 + 64],
                                 writes=[vtlk[kc]], reads=[f"D:vT{li}", _vtlk0])
                for qc in range(NQC):
                    j0 = max(0, qc - 4) * 128
                    nj = (qc + 1) * 128 - j0
                    njA = min(nj, 512)
                    njB = nj - njA
                    pb = pb0 if qc % 2 == 0 else pb1
                    psA, pkA = self.ps[pb], self.ps_keys[pb]
                    psB, pkB = self.ps[pb1 if qc % 2 == 0 else pb0], \
                               self.ps_keys[pb1 if qc % 2 == 0 else pb0]
                    qsl = slice(qc * 128, (qc + 1) * 128)
                    def p1(e, qsl=qsl, j0=j0, njA=njA, njB=njB, psA=psA, psB=psB,
                           qaug=qaug, kaug=kaug):
                        last = e.matmul(psA[:, 0:njA], qaug[0:64, qsl],
                                        kaug[0:64, j0:j0 + njA], start=True, stop=True)
                        if njB:
                            last = e.matmul(psB[:, 0:njB], qaug[0:64, qsl],
                                            kaug[0:64, j0 + njA:j0 + njA + njB],
                                            start=True, stop=True)
                        return last
                    self.o("T", p1, reads=[qaugk, kaugk],
                           writes=[pkA] + ([pkB] if njB else []))
                    if njB:
                        self.o("V", lambda e, psB=psB, scrE=scrE: e.tensor_add(
                            scrE[:, 0:128], psB[:, 0:128], mnd[:, :]),
                            reads=[pkB, mndk], writes=[scrEk])
                        self.o("V", lambda e, psA=psA, scrE=scrE: e.tensor_add(
                            scrE[:, 128:256], psA[:, 0:128], mnf[:, :]),
                            reads=[pkA, mnfk], writes=[scrEk])
                        self.o("V", lambda e, qc=qc, scrE=scrE, rmxb=rmxb: e.tensor_reduce(
                            out=rmxb[:, qc:qc + 1], in_=scrE[:, 0:256], axis=AX.X, op=ALU.max),
                            reads=[scrEk], writes=[rmxbk])
                        self.o("V", lambda e, psA=psA, rmx2=rmx2: e.tensor_reduce(
                            out=rmx2[:, 0:1], in_=psA[:, 128:512], axis=AX.X, op=ALU.max),
                            reads=[pkA], writes=[rmx2k])
                        self.o("V", lambda e, qc=qc, rmxb=rmxb, rmx2=rmx2: e.tensor_max(
                            rmxb[:, qc:qc + 1], rmxb[:, qc:qc + 1], rmx2[:, 0:1]),
                               reads=[rmxbk, rmx2k], writes=[rmxbk])
                    else:
                        dcol = qc * 128
                        self.o("V", lambda e, psA=psA, dcol=dcol, scrE=scrE: e.tensor_add(
                            scrE[:, 0:128], psA[:, dcol:dcol + 128], mnd[:, :]),
                            reads=[pkA, mndk], writes=[scrEk])
                        self.o("V", lambda e, qc=qc, scrE=scrE, rmxb=rmxb: e.tensor_reduce(
                            out=rmxb[:, qc:qc + 1], in_=scrE[:, 0:128], axis=AX.X, op=ALU.max),
                            reads=[scrEk], writes=[rmxbk])
                        if dcol > 0:
                            self.o("V", lambda e, psA=psA, dcol=dcol, rmx2=rmx2: e.tensor_reduce(
                                out=rmx2[:, 0:1], in_=psA[:, 0:dcol], axis=AX.X, op=ALU.max),
                                reads=[pkA], writes=[rmx2k])
                            self.o("V", lambda e, qc=qc, rmxb=rmxb, rmx2=rmx2: e.tensor_max(
                                rmxb[:, qc:qc + 1], rmxb[:, qc:qc + 1], rmx2[:, 0:1]),
                                   reads=[rmxbk, rmx2k], writes=[rmxbk])
                pst, pkt = self.ps[6 + (it % 2)], self.ps_keys[6 + (it % 2)]
                self.o("T", lambda e, pst=pst, rmxb=rmxb: e.transpose(
                    pst[0:NQC, 0:128], rmxb[:, 0:NQC], idf[:, :]),
                    reads=[rmxbk, idfk], writes=[pkt])
                self.o("V", lambda e, pst=pst, rmte=rmte: e.tensor_scalar_mul(
                    rmte[0:NQC, :], pst[0:NQC, 0:128], -1.0),
                    reads=[pkt], writes=[rmtek])
                self.ctx.dma(self.d["rmx"][b, h, :].rearrange("(a c) -> a c", c=128),
                             rmte[0:NQC, :], reads=[rmtek], writes=["D:rmx"])
                self.ctx.dma(rmf[0:1, :], self.d["rmx"][b, h, :].unsqueeze(0),
                             reads=["D:rmx"], writes=[rmfk], eng="P")
                self.o("V", lambda e, nrm=nrm, rmf=rmf: e.tensor_copy(nrm[0:1, :], rmf[0:1, :]),
                       reads=[rmfk], writes=[nrmk])
                self.ctx.dma(qaug[64:65, :], nrm[0:1, :], reads=[nrmk], writes=[qaugk])
                for qg in range(NQG):
                    kcs = [kc for kc in range(max(0, qg * 4 - 4), qg * 4 + 4)]
                    kcs.sort(key=lambda kc: -(min(3, kc - qg * 4 + 4)
                                              - max(0, kc - qg * 4)))
                    qsl = slice(qg * CH, (qg + 1) * CH)
                    pend = None
                    def emit_av(p, last):
                        ilo_, wdt_, expb_, expbk_, kc_, first_ = p
                        self.o("T", lambda e, kc=kc_, first=first_, last=last, psAV=psAV,
                               ilo=ilo_, wdt=wdt_, expb=expb_, vtl=vtl: e.matmul(
                            psAV[0:65, ilo:ilo + wdt], vtl[:, kc * 65:(kc + 1) * 65],
                            expb[:, ilo:ilo + wdt], start=first, stop=last),
                            reads=[vtlk[kc_], _vtlk0, expbk_], writes=[pkAV])
                    for ki, kc in enumerate(kcs):
                        si_lo = max(0, kc - qg * 4)
                        si_hi = min(3, kc - qg * 4 + 4)
                        ilo = si_lo * 128
                        wdt = (si_hi - si_lo + 1) * 128
                        psS, pkS = (self.ps[pb0], self.ps_keys[pb0]) if ki % 2 == 0 else \
                                   (self.ps[pb1], self.ps_keys[pb1])
                        expb, expbk = expb2[ki % 2]
                        self.o("T", lambda e, psS=psS, kc=kc, qg=qg, ilo=ilo, wdt=wdt,
                               kaug=kaug, qaug=qaug: e.matmul(
                            psS[:, 0:wdt], kaug[0:65, kc * 128:(kc + 1) * 128],
                            qaug[0:65, qg * CH + ilo:qg * CH + ilo + wdt],
                            start=True, stop=True),
                            reads=[kaugk, qaugk], writes=[pkS])
                        self.o("S", lambda e, psS=psS, ilo=ilo, wdt=wdt, expb=expb: e.activation(
                            out=expb[:, ilo:ilo + wdt], in_=psS[:, 0:wdt],
                            func=AF.Exp, scale=QKS),
                            reads=[pkS], writes=[expbk])
                        for si in range(si_lo, si_hi + 1):
                            rel = (qg * 4 + si) - kc
                            if rel == 0:
                                self.o("P", lambda e, si=si, expb=expb: e.tensor_mul(
                                    expb[:, si * 128:(si + 1) * 128],
                                    expb[:, si * 128:(si + 1) * 128], md2[:, :]),
                                    reads=[expbk, md2k], writes=[expbk])
                            elif rel == 4:
                                self.o("P", lambda e, si=si, expb=expb: e.tensor_mul(
                                    expb[:, si * 128:(si + 1) * 128],
                                    expb[:, si * 128:(si + 1) * 128], mf2[:, :]),
                                    reads=[expbk, mf2k], writes=[expbk])
                        if pend is not None:
                            emit_av(pend, False)
                        pend = (ilo, wdt, expb, expbk, kc, ki == 0)
                    emit_av(pend, True)
                    self.o("V", lambda e, psAV=psAV, qsl=qsl, oacc=oacc: e.tensor_copy(
                        oacc[0:64, qsl], psAV[0:64, :]), reads=[pkAV], writes=[oacck])
                    self.o("V", lambda e, psAV=psAV, qsl=qsl, rss=rss: e.tensor_copy(
                        rss[0:1, qsl], psAV[64:65, :]), reads=[pkAV], writes=[rssk])
                self.ctx.dma(self.d[f"oraw{li}"][b, h * 64:(h + 1) * 64, :], oacc[0:64, :],
                             reads=[oacck], writes=[f"D:oraw{li}"])
                self.ctx.dma(self.d[f"rsum{li}"][b, h, :].unsqueeze(0), rss[0:1, :],
                             reads=[rssk], writes=[f"D:rsum{li}"])

    def strip_c128(self, name):
        if name not in self.strip:
            ap, key = self.P.alloc(128, F32, name)
            self.ctx.dma(ap[:, :], self.d[name], writes=[key])
            self.strip[name] = (ap, key)
        return self.strip[name]

    def st_ffn_wo(self, li, fin_f, gout):
        A = self.A
        A.reset()
        w = self.load_w_ap_lhsT_bf16(self.d["L1_ffn_wo"][li], "D:L1_ffn_wo",
                                     INNER, D, f"wo{li}")
        rcp, rcpk = A.alloc(T, F32, "rcp")
        rcb, rcbk = A.alloc(T, F32, "rcb")
        orm, ormk = A.alloc(T, F32, "orm")
        onb = [A.alloc(T, BF16, f"onb{k}") for k in range(NIN)]
        ft, ftk = A.alloc(T, F32, "ftwo")
        go, gok = A.alloc(T, F32, "gowo")
        gob, gobk = A.alloc(T, BF16, "gobwo")
        rc_scr = self.dram_scratch(f"rc_scr{li}", [8, T], F32)
        for b in range(BL):
            self.ctx.dma(rcp[0:8, :], self.d[f"rsum{li}"][b, :, :], writes=[rcpk],
                         reads=[f"D:rsum{li}"])
            self.o("V", lambda e: e.reciprocal(rcp[0:8, :], rcp[0:8, :]),
                   reads=[rcpk], writes=[rcpk])
            self.ctx.dma(rc_scr[0:8, :], rcp[0:8, :], reads=[rcpk], writes=[f"D:rc{li}"])
            for mv in range(NIN):
                h0 = mv * 2
                self.ctx.dma(rcb[0:64, :], rc_scr[h0:h0 + 1, :].broadcast_to([64, T]),
                             reads=[f"D:rc{li}"], writes=[rcbk], eng="P")
                self.ctx.dma(rcb[64:128, :], rc_scr[h0 + 1:h0 + 2, :].broadcast_to([64, T]),
                             reads=[f"D:rc{li}"], writes=[rcbk], eng="P")
                self.ctx.dma(orm[:, :], self.d[f"oraw{li}"][b, mv * 128:(mv + 1) * 128, :],
                             writes=[ormk], reads=[f"D:oraw{li}"])
                ap, key = onb[mv]
                self.o("V", lambda e, ap=ap: e.tensor_mul(ap[:, :], orm[:, :], rcb[:, :]),
                       reads=[ormk, rcbk], writes=[key])
            for m in range(NT):
                msl = slice(m * 128, (m + 1) * 128)
                self.ctx.dma(ft[:, :], self.d[fin_f][b, msl, :], writes=[ftk],
                             reads=[f"D:{fin_f}"])
                for c in range(NCH):
                    ps, pk = self.psum()
                    sl = slice(c * CH, (c + 1) * CH)
                    self.mm_group(ps[:, :], pk,
                                  [(w[k][0][:, msl], onb[k][0][:, sl]) for k in range(NIN)],
                                  reads=[w[k][1] for k in range(NIN)] + [x[1] for x in onb])
                    self.o("V", lambda e, ps=ps, sl=sl: e.tensor_add(go[:, sl], ps[:, :],
                                                                     ft[:, sl]),
                           reads=[pk, ftk], writes=[gok])
                self.o("V", lambda e: e.tensor_copy(gob[:, :], go[:, :]),
                       reads=[gok], writes=[gobk])
                self.ctx.dma(self.d[gout][b, msl, :], go[:, :], reads=[gok],
                             writes=[f"D:{gout}"])
                self.ctx.dma(self.d[gout + "b"][b, msl, :], gob[:, :], reads=[gobk],
                             writes=[f"D:{gout}b"])

    def st_ffn_lnmid(self, li, gsrc, xmdst):
        A = self.A
        A.reset()
        lng, lngk = self.strip[f"lng{li}"]
        lnb, lnbk = self.strip[f"lnb{li}"]
        onesw, onesk = A.alloc(2, F32, "onesm")
        self.o("V", lambda e: e.memset(onesw[:, 0:1], 1.0), writes=[onesk])
        xch = [A.alloc(CH + 1, F32, f"mch{i}") for i in range(2)]
        sqch, sqchk = A.alloc(CH, F32, "msq")
        R1, R1k = A.alloc(T, F32, "mR1")
        R2, R2k = A.alloc(T, F32, "mR2")
        R3, R3k = A.alloc(T, F32, "mR3")
        R4, R4k = A.alloc(T, F32, "mR4")
        R5, R5k = A.alloc(T, F32, "mR5")
        sums, sumsq = R1[0:1, :], R2[0:1, :]
        c1, c2, msq2 = R3[0:1, :], R4[0:1, :], R5[0:1, :]
        R1k_c1, R2k_c2, R2k_m = R3k, R4k, R5k
        c1b, c1bk = A.alloc(T, F32, "mc1b")
        c2b, c2bk = A.alloc(T, F32, "mc2b")
        xfull = [A.alloc(T + 1, F32, f"mxf{i}") for i in range(2)]
        t1, t1k = A.alloc(T, F32, "mt1")
        xmb, xmbk = A.alloc(T, BF16, "mxmb")
        scr_b = self.dram_scratch(f"mscr_{li}", [2, T], F32)
        for b in range(BL):
            for c in range(NCH):
                psA, pkA = self.psum()
                psB, pkB = self.psum()
                for j in range(NT):
                    xa, xk = xch[j % 2]
                    if c == 0:
                        self.o("V", lambda e, xa=xa: e.memset(xa[:, 0:1], 0.0), writes=[xk])
                        self.ctx.dma(xa[:, 1:CH + 1], self.d[gsrc][b, j * 128:(j + 1) * 128,
                                                                   0:CH],
                                     writes=[xk], reads=[f"D:{gsrc}"])
                    else:
                        self.ctx.dma(xa[:, 0:CH + 1], self.d[gsrc][b, j * 128:(j + 1) * 128,
                                                                   c * CH - 1:c * CH + CH],
                                     writes=[xk], reads=[f"D:{gsrc}"])
                    off = 1 if j < 4 else 0
                    self.o("S", lambda e, xa=xa, off=off: e.activation(
                        out=sqch[:, :], in_=xa[:, off:off + CH], func=AF.Square),
                        reads=[xk], writes=[sqchk])
                    self.o("T", lambda e, psA=psA, xa=xa, off=off, j=j: e.matmul(
                        psA[0:1, :], onesw[:, 0:1], xa[:, off:off + CH],
                        start=(j == 0), stop=(j == NT - 1)),
                        reads=[onesk, xk], writes=[pkA])
                    self.o("T", lambda e, psB=psB, j=j: e.matmul(
                        psB[0:1, :], onesw[:, 0:1], sqch[:, :],
                        start=(j == 0), stop=(j == NT - 1)),
                        reads=[onesk, sqchk], writes=[pkB])
                self.o("V", lambda e, psA=psA, c=c: e.tensor_copy(
                    sums[:, c * CH:(c + 1) * CH], psA[0:1, :]), reads=[pkA], writes=[R1k])
                self.o("V", lambda e, psB=psB, c=c: e.tensor_copy(
                    sumsq[:, c * CH:(c + 1) * CH], psB[0:1, :]), reads=[pkB], writes=[R2k])
            self.o("V", lambda e: e.tensor_scalar_mul(sums, sums, 1.0 / D),
                   reads=[R1k], writes=[R1k])
            self.o("S", lambda e: e.activation(out=msq2, in_=sums, func=AF.Square),
                   reads=[R1k], writes=[R2k_m])
            self.o("V", lambda e: e.tensor_scalar_mul(sumsq, sumsq, 1.0 / D),
                   reads=[R2k], writes=[R2k])
            self.o("V", lambda e: e.tensor_sub(sumsq, sumsq, msq2),
                   reads=[R2k, R2k_m], writes=[R2k])
            self.o("S", lambda e: e.activation(out=sumsq, in_=sumsq, func=AF.Ln, bias=EPS_LN),
                   reads=[R2k, self.ck], writes=[R2k])
            self.o("S", lambda e: e.activation(out=c1, in_=sumsq, func=AF.Exp, scale=-0.5),
                   reads=[R2k], writes=[R1k_c1])
            self.o("V", lambda e: e.scalar_tensor_tensor(out=c2, in0=sums, scalar=-1.0,
                                                         in1=c1, op0=ALU.mult, op1=ALU.mult),
                   reads=[R1k, R1k_c1], writes=[R2k_c2])
            self.ctx.dma(scr_b[0:1, :], c1, reads=[R1k_c1], writes=[f"D:mscr{li}"])
            self.ctx.dma(scr_b[1:2, :], c2, reads=[R2k_c2], writes=[f"D:mscr{li}"])
            self.ctx.dma(c1b[:, :], scr_b[0:1, :].broadcast_to([128, T]),
                         reads=[f"D:mscr{li}"], writes=[c1bk], eng="P")
            self.ctx.dma(c2b[:, :], scr_b[1:2, :].broadcast_to([128, T]),
                         reads=[f"D:mscr{li}"], writes=[c2bk], eng="P")
            for j in range(NT):
                xa, xk = xfull[j % 2]
                self.o("V", lambda e, xa=xa: e.memset(xa[:, 0:1], 0.0), writes=[xk])
                self.ctx.dma(xa[:, 1:T + 1], self.d[gsrc][b, j * 128:(j + 1) * 128, :],
                             writes=[xk], reads=[f"D:{gsrc}"])
                off = 1 if j < 4 else 0
                self.o("V", lambda e, xa=xa, off=off: e.tensor_mul(
                    t1[:, :], xa[:, off:off + T], c1b[:, :]),
                    reads=[xk, c1bk], writes=[t1k])
                self.o("V", lambda e: e.tensor_add(t1[:, :], t1[:, :], c2b[:, :]),
                       reads=[t1k, c2bk], writes=[t1k])
                self.o("V", lambda e, j=j: e.tensor_scalar(
                    out=xmb[:, :], in0=t1[:, :], scalar1=lng[:, j:j + 1],
                    scalar2=lnb[:, j:j + 1], op0=ALU.mult, op1=ALU.add),
                    reads=[t1k, lngk, lnbk], writes=[xmbk])
                self.ctx.dma(self.d[xmdst][b, j * 128:(j + 1) * 128, :], xmb[:, :],
                             reads=[xmbk], writes=[f"D:{xmdst}"])

    def st_ffn_w1(self, li, xmsrc):
        A = self.A
        A.reset()
        w = self.load_w_ap_lhsT_bf16(self.d["L1_ffn_w1"][li], "D:L1_ffn_w1",
                                     D, 2 * HID, f"w1{li}", stg_n=1)
        b1a, b1ak = self.strip[f"b1a{li}"]
        b1g, b1gk = self.strip[f"b1g{li}"]
        xm = []
        for k in range(NT):
            ap, key = A.alloc(T, BF16, f"xm{k}")
            xm.append((ap, key))
        gl, glk = A.alloc(CH, F32, "gelu")
        ta, tak = A.alloc(CH, F32, "ta")
        hgm, hgmk = A.alloc(T, BF16, "hgm")
        for b in range(BL):
            for k in range(NT):
                ap, key = xm[k]
                self.ctx.dma(ap[:, :], self.d[xmsrc][b, k * 128:(k + 1) * 128, :],
                             writes=[key], reads=[f"D:{xmsrc}"])
            for m in range(NW1):
                mm = min(128, HID - m * 128)
                asl = slice(m * 128, m * 128 + mm)
                gsl = slice(HID + m * 128, HID + m * 128 + mm)
                for c in range(NCH):
                    sl = slice(c * CH, (c + 1) * CH)
                    psA, pkA = self.psum()
                    psG, pkG = self.psum()
                    self.mm_group(psA[:mm, :], pkA,
                                  [(w[k][0][:, asl], xm[k][0][:, sl]) for k in range(NT)],
                                  reads=[w[k][1] for k in range(NT)] + [x[1] for x in xm])
                    self.mm_group(psG[:mm, :], pkG,
                                  [(w[k][0][:, gsl], xm[k][0][:, sl]) for k in range(NT)],
                                  reads=[w[k][1] for k in range(NT)] + [x[1] for x in xm])
                    self.o("S", lambda e, psG=psG, m=m, mm=mm: e.activation(
                        out=gl[:mm, :], in_=psG[:mm, :], func=AF.Gelu,
                        bias=b1g[:mm, m:m + 1]),
                        reads=[pkG, b1gk], writes=[glk])
                    self.o("V", lambda e, psA=psA, m=m, mm=mm: e.tensor_scalar(
                        out=ta[:mm, :], in0=psA[:mm, :], scalar1=b1a[:mm, m:m + 1],
                        scalar2=None, op0=ALU.add),
                        reads=[pkA, b1ak], writes=[tak])
                    self.o("V", lambda e, sl=sl, mm=mm: e.tensor_mul(
                        hgm[:mm, sl], ta[:mm, :], gl[:mm, :]),
                        reads=[tak, glk], writes=[hgmk])
                self.ctx.dma(self.d[f"hgb{li}"][b, m * 128:m * 128 + mm, :], hgm[:mm, :],
                             reads=[hgmk], writes=[f"D:hgb{li}"])

    def st_ffn_w2(self, li, gsrc, fdst, also_bf16):
        A = self.A
        A.reset()
        w = self.load_w_ap_lhsT_bf16(self.d["L1_ffn_w2"][li], "D:L1_ffn_w2",
                                     HID, D, f"w2{li}")
        b2, b2k = self.strip[f"b2{li}"]
        hq = [A.alloc(CH, BF16, f"hgc{k}") for k in range(NW1)]
        gt, gtk = A.alloc(CH, F32, "gt2")
        f2, f2k = A.alloc(CH, F32, "f2t")
        f2b, f2bk = A.alloc(CH, BF16, "f2bt")
        for b in range(BL):
            for c in range(NCH):
                sl = slice(c * CH, (c + 1) * CH)
                for k in range(NW1):
                    kk = min(128, HID - k * 128)
                    ap, key = hq[k]
                    self.ctx.dma(ap[:kk, :], self.d[f"hgb{li}"][b, k * 128:k * 128 + kk, sl],
                                 writes=[key], reads=[f"D:hgb{li}"])
                for m in range(NT):
                    msl = slice(m * 128, (m + 1) * 128)
                    ps, pk = self.psum()
                    pairs = []
                    for k in range(NW1):
                        kk = min(128, HID - k * 128)
                        pairs.append((w[k][0][:kk, msl], hq[k][0][:kk, :]))
                    self.mm_group(ps[:, :], pk, pairs,
                                  reads=[w[k][1] for k in range(NW1)] + [x[1] for x in hq])
                    self.ctx.dma(gt[:, :], self.d[gsrc][b, msl, sl], writes=[gtk],
                                 reads=[f"D:{gsrc}"])
                    self.o("V", lambda e, ps=ps, m=m: e.tensor_scalar(
                        out=f2[:, :], in0=ps[:, :], scalar1=b2[:, m:m + 1],
                        scalar2=None, op0=ALU.add),
                        reads=[pk, b2k], writes=[f2k])
                    self.o("V", lambda e: e.tensor_add(f2[:, :], f2[:, :], gt[:, :]),
                           reads=[f2k, gtk], writes=[f2k])
                    self.ctx.dma(self.d[fdst][b, msl, sl], f2[:, :], reads=[f2k],
                                 writes=[f"D:{fdst}"])
                    if also_bf16:
                        self.o("V", lambda e: e.tensor_copy(f2b[:, :], f2[:, :]),
                               reads=[f2k], writes=[f2bk])
                        self.ctx.dma(self.d[fdst + "b"][b, msl, sl], f2b[:, :],
                                     reads=[f2bk], writes=[f"D:{fdst}b"])

    def st_final(self):
        A = self.A
        A.reset()
        x1t, x1k = A.alloc(T, F32, "fx1")
        att, attk = A.alloc(T, BF16, "fatt")
        f1t, f1k = A.alloc(T, F32, "ff1")
        ot, otk = A.alloc(T, F32, "fout")
        for b in range(BL):
            for j in range(NT):
                msl = slice(j * 128, (j + 1) * 128)
                self.ctx.dma(x1t[:, :], self.d["x1"][b, msl, :], writes=[x1k], reads=["D:x1"])
                self.ctx.dma(att[:, :], self.d["attn1"][b, msl, :], writes=[attk],
                             reads=["D:attn1"])
                self.ctx.dma(f1t[:, :], self.d["f1"][b, msl, :], writes=[f1k], reads=["D:f1"])
                self.o("V", lambda e: e.tensor_add(ot[:, :], x1t[:, :], f1t[:, :]),
                       reads=[x1k, f1k], writes=[otk])
                self.o("V", lambda e: e.tensor_add(ot[:, :], ot[:, :], att[:, :]),
                       reads=[otk, attk], writes=[otk])
                self.ctx.dma(self.d["out"][b, msl, :], ot[:, :], reads=[otk], writes=["D:out"])


B1B_STAGES = ["prep2",
              "qkv0", "attn0", "wo0", "lnm0", "w10", "w20",
              "qkv1", "attn1s", "wo1", "lnm1", "w11", "w21", "final"]


def declare_block1b(mk):
    for nm, shape in [("L1_ffn_wqkv", [2, D, 3 * INNER]), ("L1_ffn_wo", [2, INNER, D]),
                      ("L1_ffn_w1", [2, D, 2 * HID]), ("L1_ffn_w2", [2, HID, D]),
                      ("c_csq2", [128, T]), ("c_ssq2", [128, T]),
                      ("c_csk2", [128, T]), ("c_ssk2", [128, T]),
                      ("c_mnd", [128, 128]), ("c_mnf", [128, 128]),
                      ("c_md2", [128, 128]), ("c_mf2", [128, 128]),
                      ("qs_dup_0", [128, 1]), ("qsrot_dup_0", [128, 1]),
                      ("ks_dup_0", [128, 1]), ("ksrot_dup_0", [128, 1]),
                      ("qs_dup_1", [128, 1]), ("qsrot_dup_1", [128, 1]),
                      ("ks_dup_1", [128, 1]), ("ksrot_dup_1", [128, 1]),
                      ("lng0_s", [128, NT]), ("lnb0_s", [128, NT]),
                      ("lng1_s", [128, NT]), ("lnb1_s", [128, NT]),
                      ("b1a0_s", [128, NW1]), ("b1g0_s", [128, NW1]),
                      ("b1a1_s", [128, NW1]), ("b1g1_s", [128, NW1]),
                      ("b20_s", [128, NT]), ("b21_s", [128, NT])]:
        mk.dram_in(nm, shape)
    for li in range(2):
        mk.dram_scratch(f"qraw{li}", [BL, INNER, T])
        mk.dram_scratch(f"kraw{li}", [BL, INNER, T])
        mk.dram_scratch(f"qb{li}", [BL, INNER, T], BF16)
        mk.dram_scratch(f"kb{li}", [BL, INNER, T], BF16)
        mk.dram_scratch(f"vT{li}", [BL, NIN, T, 128], BF16)
        mk.dram_scratch(f"oraw{li}", [BL, INNER, T])
        mk.dram_scratch(f"rsum{li}", [BL, 8, T])
        mk.dram_scratch(f"g{li}", [BL, D, T])
        mk.dram_scratch(f"g{li}b", [BL, D, T], BF16)
        mk.dram_scratch(f"xm{li}", [BL, D, T], BF16)
        mk.dram_scratch(f"hgb{li}", [BL, HID, T], BF16)
    mk.dram_scratch("rmx", [BL, 8, T])
    mk.dram_scratch("f0", [BL, D, T])
    mk.dram_scratch("f0b", [BL, D, T], BF16)
    mk.dram_scratch("f1", [BL, D, T])
    mk.dram_out("out", [BL, D, T])


def emit_block1b(mk, upto=None, skip=()):
    stages = B1B_STAGES if upto is None else B1B_STAGES[:B1B_STAGES.index(upto) + 1]
    for s in stages:
        if s in skip:
            continue
        if s == "prep2":
            if "g64" not in mk.strip:
                mk.load_strip("g64", "c_g64", 2)
            for li in range(2):
                mk.load_strip(f"qs_dup_{li}", f"qs_dup_{li}", 1)
                mk.load_strip(f"qsrot_dup_{li}", f"qsrot_dup_{li}", 1)
                mk.load_strip(f"ks_dup_{li}", f"ks_dup_{li}", 1)
                mk.load_strip(f"ksrot_dup_{li}", f"ksrot_dup_{li}", 1)
                mk.load_strip(f"lng{li}", f"lng{li}_s", NT)
                mk.load_strip(f"lnb{li}", f"lnb{li}_s", NT)
                mk.load_strip(f"b1a{li}", f"b1a{li}_s", NW1)
                mk.load_strip(f"b1g{li}", f"b1g{li}_s", NW1)
                mk.load_strip(f"b2{li}", f"b2{li}_s", NT)
        elif s == "qkv0":
            mk.st_ffn_qkv(0, "xn1b")
        elif s == "attn0":
            mk.st_ffn_attn(0)
        elif s == "wo0":
            mk.st_ffn_wo(0, "xn1f", "g0")
        elif s == "lnm0":
            mk.st_ffn_lnmid(0, "g0", "xm0")
        elif s == "w10":
            mk.st_ffn_w1(0, "xm0")
        elif s == "w20":
            mk.st_ffn_w2(0, "g0", "f0", True)
        elif s == "qkv1":
            mk.st_ffn_qkv(1, "f0b")
        elif s == "attn1s":
            mk.st_ffn_attn(1)
        elif s == "wo1":
            mk.st_ffn_wo(1, "f0", "g1")
        elif s == "lnm1":
            mk.st_ffn_lnmid(1, "g1", "xm1")
        elif s == "w11":
            mk.st_ffn_w1(1, "xm1")
        elif s == "w21":
            mk.st_ffn_w2(1, "g1", "f1", False)
        elif s == "final":
            mk.st_final()


def host_prep_block1b(inp, Tcur):
    DIM_HEAD = 64
    WINDOW = 512
    half_idx = np.arange(0, DIM_HEAD, 2, dtype=np.float64)
    inv_freq = 1.0 / 10000.0 ** (half_idx / DIM_HEAD)
    t = np.arange(Tcur, dtype=np.float64)
    freqs = t[:, None] * inv_freq[None, :]
    freqs = np.concatenate([freqs, freqs], axis=-1)
    base = (half_idx + 0.4 * DIM_HEAD) / (1.4 * DIM_HEAD)
    power = (t - Tcur // 2) / float(WINDOW // 2)
    scale = base[None, :] ** power[:, None]
    scale = np.concatenate([scale, scale], axis=-1)
    cos, sin = np.cos(freqs), np.sin(freqs)
    sgn = np.ones((1, 64)); sgn[0, :32] = -1.0
    csq = (cos * scale).T
    ssq = (sin * scale).T * sgn.T
    csk = (cos / scale).T
    ssk = (sin / scale).T * sgn.T
    ii = np.arange(128)
    mnd = np.where(ii[:, None] >= ii[None, :], 0.0, -1e30)
    mnf = np.where(ii[:, None] <= ii[None, :], 0.0, -1e30)
    md2 = (ii[None, :] >= ii[:, None]).astype(np.float32)
    mf2 = (ii[None, :] <= ii[:, None]).astype(np.float32)
    rot = (np.arange(64) + 32) % 64
    m = {
        "L1_ffn_wqkv": inp["L1_ffn_wqkv"], "L1_ffn_wo": inp["L1_ffn_wo"],
        "L1_ffn_w1": inp["L1_ffn_w1"], "L1_ffn_w2": inp["L1_ffn_w2"],
        "c_csq2": np.tile(csq, (2, 1)), "c_ssq2": np.tile(ssq, (2, 1)),
        "c_csk2": np.tile(csk, (2, 1)), "c_ssk2": np.tile(ssk, (2, 1)),
        "c_mnd": mnd, "c_mnf": mnf, "c_md2": md2, "c_mf2": mf2,
    }
    qsc = np.asarray(inp["L1_ffn_q_scale"], np.float32)
    ksc = np.asarray(inp["L1_ffn_k_scale"], np.float32)
    for li in range(2):
        m[f"qs_dup_{li}"] = np.tile(qsc[li], 2).reshape(128, 1)
        m[f"qsrot_dup_{li}"] = np.tile(qsc[li][rot], 2).reshape(128, 1)
        m[f"ks_dup_{li}"] = np.tile(ksc[li], 2).reshape(128, 1)
        m[f"ksrot_dup_{li}"] = np.tile(ksc[li][rot], 2).reshape(128, 1)
        m[f"lng{li}_s"] = vec_strip(np.asarray(inp["L1_ffn_ln_g"])[li], NT)
        m[f"lnb{li}_s"] = vec_strip(np.asarray(inp["L1_ffn_ln_b"])[li], NT)
        b1 = np.asarray(inp["L1_ffn_b1"], np.float32)[li]
        m[f"b1a{li}_s"] = vec_strip(b1[0:HID], NW1)
        m[f"b1g{li}_s"] = vec_strip(b1[HID:2 * HID], NW1)
        m[f"b2{li}_s"] = vec_strip(np.asarray(inp["L1_ffn_b2"])[li], NT)
    return {k: np.ascontiguousarray(np.asarray(v, np.float32)) for k, v in m.items()}


def declare_full(mk):
    declare_block0(mk)
    declare_block1a(mk)
    declare_block1b(mk)


def emit_full(mk):
    mk.setup_consts([EPS_LN, PI / 2, 1.0, 1e-24])
    emit_block0(mk, consts=False)
    emit_block1a(mk)
    emit_block1b(mk)


def host_prep_full(inp, b_sl, Tcur):
    m = host_prep_block0(inp, b_sl, Tcur)
    m.update(host_prep_block1a(inp, Tcur))
    m.update(host_prep_block1b(inp, Tcur))
    g64 = np.zeros((128, 2), np.float32)
    g64[0:64, 0] = 1.0
    g64[64:128, 1] = 1.0
    m["c_g64"] = g64
    return m


from concourse.bass_utils import run_bass_kernel_spmd

N_CORES = 8


def _host_base(inputs):
    inp = {k: np.asarray(v) for k, v in inputs.items()}
    m = host_prep_block0(inp, None, T)
    m.update(host_prep_block1a(inp, T))
    m.update(host_prep_block1b(inp, T))
    g64 = np.zeros((128, 2), np.float32)
    g64[0:64, 0] = 1.0
    g64[64:128, 1] = 1.0
    m["c_g64"] = g64
    return m


_BUILD_CACHE = {}


def _build():
    if "nc" not in _BUILD_CACHE:
        nc = bass.Bass("TRN2", target_bir_lowering=False, debug=False)
        m = MK(nc)
        declare_full(m)
        emit_full(m)
        m.ctx.barrier_end()
        m.ctx.finish()
        _BUILD_CACHE["nc"] = nc
    return _BUILD_CACHE["nc"]


def kernel(**inputs):
    nc = _build()
    base = _host_base(inputs)
    x = np.asarray(inputs["L0_attn_x"], np.float32)
    in_maps = []
    for c in range(N_CORES):
        mm = dict(base)
        mm["x"] = np.ascontiguousarray(x[2 * c:2 * c + 2].transpose(0, 2, 1))
        in_maps.append(mm)
    res = run_bass_kernel_spmd(nc, in_maps, core_ids=list(range(N_CORES)))
    outs = [np.asarray(r["out"], np.float32).transpose(0, 2, 1)
            for r in res.results]
    return np.ascontiguousarray(np.concatenate(outs, axis=0))
